# version 22 (final)
# baseline (speedup 1.0000x reference)
import sys
import contextlib
for _p in ("/opt/trn_rl_repo",):
    if _p not in sys.path:
        sys.path.append(_p)
import numpy as np
import ml_dtypes
import concourse.bass as bass
import concourse.mybir as mybir
from concourse.mybir import AluOpType as OP
from concourse.bass_utils import run_bass_kernel_spmd

F32, BF16 = mybir.dt.float32, mybir.dt.bfloat16
AF = mybir.ActivationFunctionType
BF = ml_dtypes.bfloat16
AX = mybir.AxisListType.X

B, S, D = 2, 8192, 2048
T = B * S
NCORES = 8
TLOC = T // NCORES
NB = D // 2
NT = D // 128
NBT = NB // 128
TC = 512
NCH = TLOC // TC
TCF = 256
NCHF = TLOC // TCF
FF = 7168
NFT = FF // 128
E, R = 8, 32
ER = E * R
HID = 8192
NHT = HID // 128
EPS = 1e-5


def make_tables():
    f = np.arange(D, dtype=np.float64)[:, None]
    bh = np.arange(NB, dtype=np.float64)[None, :]
    bf_ = np.arange(D, dtype=np.float64)[None, :]
    ah = 2 * np.pi * f * bh / D
    af = 2 * np.pi * f * bf_ / D
    Ch = np.cos(ah)
    Sm = -np.sin(ah)
    Sm[:, 0] = np.cos(np.pi * np.arange(D))
    Cf = np.cos(af)
    Sf = -np.sin(af)
    jh = np.arange(NB, dtype=np.float64)[:, None]
    n = np.arange(D, dtype=np.float64)[None, :]
    IC2 = 2 * np.cos(2 * np.pi * jh * n / D) / D
    IC2[0, :] = 1.0 / D
    IS2 = -2 * np.sin(2 * np.pi * jh * n / D) / D
    IS2[0, :] = np.cos(np.pi * np.arange(D)) / D
    jf = np.arange(D, dtype=np.float64)[:, None]
    ICf = np.cos(2 * np.pi * jf * n / D) / D
    ISf = -np.sin(2 * np.pi * jf * n / D) / D
    t = {"Ch": Ch, "Sm": Sm, "Cf": Cf, "Sf": Sf,
         "IC2": IC2, "IS2": IS2, "ICf": ICf, "ISf": ISf}
    return {k: np.ascontiguousarray(v.astype(BF)) for k, v in t.items()}


def make_flipm():
    idh = 0.5 * np.eye(128)
    xf = np.zeros((128, 128))
    for m in range(1, 128):
        xf[128 - m, m] = 0.5
    return np.ascontiguousarray(
        np.concatenate([idh, xf, -xf], axis=1).astype(BF))


def image_tab(tab, slabs):
    cols = sum(nkt * mc for (_, nkt, _, mc) in slabs)
    img = np.empty((128, cols), tab.dtype)
    off = 0
    for (kt0, nkt, m0, mc) in slabs:
        blk = tab[kt0 * 128:(kt0 + nkt) * 128, m0:m0 + mc]
        img[:, off:off + nkt * mc] = (
            blk.reshape(nkt, 128, mc).transpose(1, 0, 2).reshape(128, -1))
        off += nkt * mc
    return np.ascontiguousarray(img)


def geom_mslabs(K, M, mcols):
    return [(0, K // 128, m0, mcols) for m0 in range(0, M, mcols)]


def geom_kslabs(K, M, nkt):
    return [(k0, nkt, 0, M) for k0 in range(0, K // 128, nkt)]


def expansion_mat():
    m = np.zeros((8, 256), np.float32)
    for e in range(8):
        for r_ in range(32):
            m[e, (e % 4) * 32 + r_ + (e // 4) * 128] = 1.0
    return np.ascontiguousarray(m.astype(BF))


ENGINES = ("tensor", "vector", "scalar", "gpsimd", "sync")
QUEUES = {"SQ": "sync", "GQ": "gpsimd", "AQ": "scalar"}


class Buf:
    __slots__ = ("name", "wtok", "rtoks")

    def __init__(self, name):
        self.name = name
        self.wtok = None
        self.rtoks = []


class PlanOp:
    __slots__ = ("engine", "fn", "deps", "queue", "idx", "needs_inc", "nm",
                 "selfsync")

    def __init__(self, engine, fn, deps, queue, nm="", selfsync=False):
        self.engine = engine
        self.fn = fn
        self.deps = deps
        self.queue = queue
        self.idx = None
        self.needs_inc = False
        self.nm = nm
        self.selfsync = selfsync


class Prog:
    def __init__(self, nc):
        self.nc = nc
        self.ops = {e: [] for e in ENGINES}
        self.qcount = {q: 0 for q in QUEUES}

    def plan(self, engine, fn, reads=(), writes=(), queue=None, nm="",
             selfsync=False):
        deps = []
        for b in reads:
            if b.wtok is not None:
                deps.append(b.wtok)
        for b in writes:
            if b.wtok is not None:
                deps.append(b.wtok)
            deps.extend(b.rtoks)
        op_ = PlanOp(engine, fn, deps, queue, nm, selfsync)
        if queue is not None:
            self.qcount[queue] += 1
            op_.idx = self.qcount[queue]
        self.ops[engine].append(op_)
        for b in reads:
            b.rtoks.append(op_)
        for b in writes:
            b.wtok = op_
            b.rtoks = []
        return op_

    def dma(self, queue, dst_ap, src_ap, reads=(), writes=(), nm="", **kw):
        def fn(e, _dst=dst_ap, _src=src_ap, _kw=kw):
            return e.dma_start(_dst, _src, **_kw)
        return self.plan(QUEUES[queue], fn, reads=reads, writes=writes,
                         queue=queue, nm=nm or f"dma_{queue}")

    def finalize_and_emit(self, sems):
        for e in ENGINES:
            for op_ in self.ops[e]:
                for d in op_.deps:
                    if d.queue is None and d.engine != op_.engine:
                        d.needs_inc = True
        ecount = {e: 0 for e in ENGINES}
        for e in ENGINES:
            for op_ in self.ops[e]:
                if op_.queue is None and op_.needs_inc:
                    ecount[e] += 1
                    op_.idx = ecount[e]
        self.final_counts = dict(ecount)
        nc = self.nc

        def emit_stream(eng_obj, ename):
            high = {}
            vs_n = 0
            vs_sem = sems.get("vs_" + ename)
            for op_ in self.ops[ename]:
                for d in op_.deps:
                    if d.queue is not None:
                        sem, val = sems[d.queue], 16 * d.idx
                    elif d.engine != ename:
                        sem, val = sems[d.engine], d.idx
                    else:
                        continue
                    if high.get(sem.name, 0) < val:
                        eng_obj.wait_ge(sem, val)
                        high[sem.name] = val
                inst = op_.fn(eng_obj)
                if op_.queue is not None:
                    inst.then_inc(sems[op_.queue], 16)
                elif op_.selfsync:
                    vs_n += 1
                    inst.then_inc(vs_sem, 1)
                    eng_obj.wait_ge(vs_sem, vs_n)
                    if op_.needs_inc:
                        eng_obj.sem_inc(sems[ename], 1)
                elif op_.needs_inc:
                    inst.then_inc(sems[ename], 1)

        with nc.Block() as block:
            block.tensor(lambda t: emit_stream(t, "tensor"))
            block.vector(lambda v: emit_stream(v, "vector"))
            block.scalar(lambda s: emit_stream(s, "scalar"))
            block.gpsimd(lambda g: emit_stream(g, "gpsimd"))

            def sync_fn(s):
                emit_stream(s, "sync")
                for q in QUEUES:
                    if self.qcount[q]:
                        s.wait_ge(sems[q], 16 * self.qcount[q])
            block.sync(sync_fn)


class ArAlloc:
    __slots__ = ("t", "buf", "off", "size")

    def __init__(self, t, buf, off, size):
        self.t = t
        self.buf = buf
        self.off = off
        self.size = size

    def ap(self):
        return self.t.ap()


class Arena:
    def __init__(self, nc, base, top):
        self.nc = nc
        self.base = base
        self.top = top
        self.free_list = [[base, top, []]]
        self.n = 0
        self.peak = base
        self.active = {}

    def alloc(self, cols, dtype, name):
        esz = 4 if dtype == F32 else 2
        nbytes = (cols * esz + 511) & ~511
        for seg in self.free_list:
            if seg[1] - seg[0] >= nbytes:
                off = seg[0]
                preds = list(seg[2])
                seg[0] += nbytes
                if seg[0] >= seg[1]:
                    self.free_list.remove(seg)
                self.n += 1
                self.peak = max(self.peak, off + nbytes)
                for (ao, asz), anm in self.active.items():
                    assert off + nbytes <= ao or off >= ao + asz, (
                        f"arena overlap: {name}@{off}+{nbytes} vs "
                        f"{anm}@{ao}+{asz}")
                t = self.nc.alloc_sbuf_tensor_at(
                    f"ar_{name}_{self.n}", [128, cols], dtype, offset=off)
                buf = Buf(f"{name}@{off}")
                buf.rtoks = preds
                a = ArAlloc(t, buf, off, nbytes)
                self.active[(off, nbytes)] = name
                return a
        raise MemoryError(
            f"arena OOM for {name} ({nbytes}B); free="
            f"{[(s[1]-s[0]) for s in self.free_list]}")

    def free(self, a):
        assert (a.off, a.size) in self.active, \
            f"double free {a.buf.name}@{a.off}+{a.size}"
        for seg in self.free_list:
            assert a.off + a.size <= seg[0] or a.off >= seg[1], \
                f"free overlap {a.buf.name}@{a.off}+{a.size} vs seg {seg[:2]}"
        self.active.pop((a.off, a.size), None)
        toks = list(a.buf.rtoks)
        if a.buf.wtok is not None:
            toks.append(a.buf.wtok)
        self.free_list.append([a.off, a.off + a.size, toks])
        self.free_list.sort(key=lambda s: s[0])
        merged = []
        for seg in self.free_list:
            if merged and merged[-1][1] == seg[0]:
                merged[-1][1] = seg[1]
                merged[-1][2] = merged[-1][2] + seg[2]
            else:
                merged.append(seg)
        self.free_list = merged


def build_program(debug=False):
    nc = bass.Bass("TRN2", target_bir_lowering=False)
    P = Prog(nc)
    din = {}

    def ein(name, shape, dtype):
        din[name] = (nc.dram_tensor(name, shape, dtype, kind="ExternalInput"),
                     Buf("d_" + name))

    def internal(name, shape, dtype):
        din[name] = (nc.dram_tensor(name, shape, dtype, kind="Internal"),
                     Buf("i_" + name))

    ein("xT", (128, NCH * NT * TC), F32)
    for w in ("Wq_T", "Wk_T", "Wv_T", "Ws_T", "Wt_T"):
        ein(w, (D, D), F32)
    for w in ("W1e_a", "W1e_b", "W1e_c", "W1e_d"):
        ein(w, (NB, HID), F32)
    ein("W2a", (128, 64 * 4096), F32)
    ein("b1t", (128, NHT), F32)
    ein("b2t", (128, 2 * NT), F32)
    ein("W1f_T", (128, 14 * 8192), F32)
    ein("W3f_T", (128, 14 * 8192), F32)
    ein("W2f_T", (128, 16 * 7168), F32)
    ein("Wg_tiled", (128, NT * E), F32)
    ein("A1p", (128, NT * 256), F32)
    ein("A3p", (128, NT * 256), F32)
    ein("A2p", (128, 7 * 2048), F32)
    ein("B1p", (128, 14 * 1024), F32)
    ein("B3p", (128, 14 * 1024), F32)
    ein("B2p", (128, 16 * 256), F32)
    for tname, shp in (("Ch", (128, 2 * 8192)), ("Sm", (128, 2 * 8192)),
                       ("IC2", (128, 8 * 2048)), ("IS2", (128, 8 * 2048))):
        ein(tname, shp, BF16)
    ein("EXPM", (8, 256), BF16)
    ein("idn", (128, 128), F32)
    ein("flipm", (128, 384), BF16)
    ein("xR_T", (128, NCH * NT * TC), F32)
    ein("cid2", (1, 2), mybir.dt.int32)
    ein("exw", (128, 4), F32)

    out_h = nc.dram_tensor("outT", (128, NCHF * NT * TCF), F32,
                           kind="ExternalOutput")
    out_b = Buf("outT")
    if debug:
        dbg_x1 = nc.dram_tensor("dbg_x1", (128, NCHF * NT * TCF), F32,
                                kind="ExternalOutput")
        dbg_x1_b = Buf("dbg_x1")
        dbg_cum = nc.dram_tensor("dbg_cum", (128, NCH * NT * TC), F32,
                                 kind="ExternalOutput")
        dbg_cum_b = Buf("dbg_cum")
        dbg_sm = nc.dram_tensor("dbg_sm", (128, 64), F32,
                                kind="ExternalOutput")
        dbg_sm_b = Buf("dbg_sm")
        dbg_ln1 = nc.dram_tensor("dbg_ln1", (D, TC), F32,
                                 kind="ExternalOutput")
        dbg_ln1_b = Buf("dbg_ln1")
        dbg_q = nc.dram_tensor("dbg_q", (D, TC), F32, kind="ExternalOutput")
        dbg_q_b = Buf("dbg_q")
        dbg_p = nc.dram_tensor("dbg_p", (D, TC), F32, kind="ExternalOutput")
        dbg_p_b = Buf("dbg_p")
        dbg_ffn = nc.dram_tensor("dbg_ffn", (128, 2048), F32,
                                 kind="ExternalOutput")
        dbg_ffn_b = Buf("dbg_ffn")
        dbg_f2 = nc.dram_tensor("dbg_f2", (128, 2048), F32,
                                kind="ExternalOutput")
        dbg_f2_b = Buf("dbg_f2")
        dbg_stk = nc.dram_tensor("dbg_stk", (128, 192), F32,
                                 kind="ExternalOutput")
        dbg_stk_b = Buf("dbg_stk")
        dbg_u = nc.dram_tensor("dbg_u", (128, NT * TC), F32,
                               kind="ExternalOutput")
        dbg_u_b = Buf("dbg_u")
        dbg_hf = nc.dram_tensor("dbg_hf", (128, NT * TC), F32,
                                kind="ExternalOutput")
        dbg_hf_b = Buf("dbg_hf")
        dbg_trf = nc.dram_tensor("dbg_trf", (128, 2 * NT * TC), F32,
                                 kind="ExternalOutput")
        dbg_trf_b = Buf("dbg_trf")
        dbg_t = nc.dram_tensor("dbg_t", (128, 2 * NT * TC), F32,
                               kind="ExternalOutput")
        dbg_t_b = Buf("dbg_t")
        dbg_P = nc.dram_tensor("dbg_P", (128, 2 * NT * TC), F32,
                               kind="ExternalOutput")
        dbg_P_b = Buf("dbg_P")
        dbg_ao = nc.dram_tensor("dbg_ao", (128, NT * TC), F32,
                                kind="ExternalOutput")
        dbg_ao_b = Buf("dbg_ao")
        dbg_w1 = nc.dram_tensor("dbg_w1", (128, 32 * 4096), F32,
                                kind="ExternalOutput")
        dbg_w1_b = Buf("dbg_w1")
        dbg_h1 = nc.dram_tensor("dbg_h1", (128, NHT * TC), F32,
                                kind="ExternalOutput")
        dbg_h1_b = Buf("dbg_h1")

    for w in ("q", "k", "v", "s"):
        internal(f"fold_{w}C", (128, 4 * 4096), BF16)
        internal(f"fold_{w}S", (128, 4 * 4096), BF16)
    internal("fold_tC", (128, 4 * 4096), BF16)
    internal("fold_tS", (128, 4 * 4096), BF16)
    internal("W1eff", (128, 32 * 4096), BF16)
    internal("W2a_d", (128, 64 * 4096), BF16)
    internal("W1f_d", (128, 14 * 8192), BF16)
    internal("W3f_d", (128, 14 * 8192), BF16)
    internal("W2f_d", (128, 16 * 7168), BF16)
    internal("A1d", (128, NT * 256), BF16)
    internal("A3d", (128, NT * 256), BF16)
    internal("A2d", (128, 7 * 2048), BF16)
    internal("B1d", (128, 14 * 1024), BF16)
    internal("B3d", (128, 14 * 1024), BF16)
    internal("B2d", (128, 16 * 256), BF16)
    internal("cum_d", (128, NCH * NT * TC), F32)
    internal("x1_d", (128, NCHF * NT * TCF), F32)
    din["sh_x"] = (nc.dram_tensor("sh_x", (256, 32), F32, kind="Internal",
                                  addr_space="Shared"), Buf("i_sh_x"))
    din["sh_f"] = (nc.dram_tensor("sh_f", (2, 16), mybir.dt.int32,
                                  kind="Internal", addr_space="Shared"),
                   Buf("i_sh_f"))

    def dh(name):
        return din[name][0]

    def db(name):
        return din[name][1]

    def dap(name):
        return din[name][0].ap()

    ar = Arena(nc, (nc.sbuf_base + 511) & ~511, nc.sbuf_top & ~511)
    psum = [(nc.alloc_psum_tensor(f"ps{i}", [128, 512], F32), Buf(f"ps{i}"))
            for i in range(8)]
    ps_rr = [0]

    def next_bank():
        i = ps_rr[0] % 8
        ps_rr[0] += 1
        return psum[i]

    def op(engine, fn, reads=(), writes=(), nm=""):
        return P.plan(engine, fn, reads=reads, writes=writes, nm=nm)

    ones_bf = ar.alloc(16, BF16, "ones_bf")
    ones_f1 = ar.alloc(128, F32, "ones_f1")
    wg_cast = ar.alloc(NT * E, BF16, "wgc")
    b1_sb = ar.alloc(NHT, F32, "b1")
    b2_sb = ar.alloc(2 * NT, F32, "b2")
    cid_sb = ar.alloc(2, mybir.dt.int32, "cid_sb")
    mg_sb = ar.alloc(16, mybir.dt.int32, "mg_sb")
    fl_sb = ar.alloc(32, mybir.dt.int32, "fl_sb")
    ex_sb = ar.alloc(64, F32, "ex_sb")
    pollct = ar.alloc(2, mybir.dt.int32, "pollct")
    exw_sb = ar.alloc(4, F32, "exw_sb")
    idn_sb = ar.alloc(128, F32, "idn")
    flip_sb = ar.alloc(384, BF16, "flipm")
    expm_sb = ar.alloc(256, BF16, "expm")
    stats_m = ar.alloc(TLOC, F32, "stats_m")
    stats_r = ar.alloc(TLOC, F32, "stats_r")
    carry = ar.alloc(16, F32, "carry")
    rcarry = ar.alloc(16, F32, "rcarry")
    basep = ar.alloc(16, F32, "base")
    w1pool = [ar.alloc(512, F32, "w1pa"), ar.alloc(512, F32, "w1pb"),
              ar.alloc(512, BF16, "w1po")]
    sr_sm = ar.alloc(512, F32, "sr_sm")
    sr_s2 = ar.alloc(512, F32, "sr_s2")
    sr_tmp = ar.alloc(512, F32, "sr_tmp")

    P.dma("SQ", b1_sb.ap(), dap("b1t"), reads=[db("b1t")], writes=[b1_sb.buf])
    P.dma("SQ", b2_sb.ap(), dap("b2t"), reads=[db("b2t")], writes=[b2_sb.buf])
    P.dma("SQ", cid_sb.ap()[0:1, :], dap("cid2"), reads=[db("cid2")],
          writes=[cid_sb.buf])
    P.dma("SQ", exw_sb.ap(), dap("exw"), reads=[db("exw")],
          writes=[exw_sb.buf])
    P.dma("SQ", idn_sb.ap(), dap("idn"), reads=[db("idn")],
          writes=[idn_sb.buf])
    P.dma("SQ", flip_sb.ap(), dap("flipm"), reads=[db("flipm")],
          writes=[flip_sb.buf])
    P.dma("SQ", expm_sb.ap()[0:8, :], dap("EXPM"), reads=[db("EXPM")],
          writes=[expm_sb.buf])
    P.dma("GQ", wg_cast.ap(), dap("Wg_tiled"), reads=[db("Wg_tiled")],
          writes=[wg_cast.buf])
    eps_sb = ar.alloc(2, F32, "eps")
    op("vector", lambda v: v.memset(eps_sb.ap()[0:1, 0:1], EPS),
       writes=[eps_sb.buf])
    op("vector", lambda v: v.memset(ones_bf.ap(), 1.0), writes=[ones_bf.buf])
    op("vector", lambda v: v.memset(ones_f1.ap()[0:1, :], 1.0),
       writes=[ones_f1.buf])
    op("vector", lambda v: v.memset(carry.ap(), 0.0), writes=[carry.buf])
    op("vector", lambda v: v.memset(rcarry.ap(), 0.0), writes=[rcarry.buf])
    op("vector", lambda v, _d=None: v.tensor_copy(
        mg_sb.ap()[0:1, :],
        cid_sb.ap()[0:1, 1:2].broadcast_to([1, 16])),
        reads=[cid_sb.buf], writes=[mg_sb.buf], nm="mgrep")

    def mm_group(bank, bank_ap, pairs, reads, nm="mm", start=True, stop=True):
        def fn(t, _pairs=tuple(pairs), _ap=bank_ap, _sa=start, _so=stop):
            n = len(_pairs)
            inst = None
            for i, (l, r_) in enumerate(_pairs):
                inst = t.matmul(_ap, l, r_, start=(_sa and i == 0),
                                stop=(_so and i == n - 1))
            return inst
        return op("tensor", fn, reads=list(reads), writes=[bank[1]], nm=nm)

    def load_image_slab(dramname, dst, s, W, nm="ldw"):
        return P.dma("SQ", dst.ap()[:, 0:W],
                     dap(dramname)[:, s * W:(s + 1) * W],
                     reads=[db(dramname)], writes=[dst.buf], nm=nm)

    def ln_stats(src_tiles, src_bufs, scratch, w, m_ap, r_ap, mbuf, rbuf,
                 src_is_bf16):
        nt = len(src_tiles)
        if src_is_bf16:
            bank = next_bank()
            mm_group(bank, bank[0].ap()[0:1, 0:w],
                     [(ones_bf.ap()[:, 0:1], a) for a in src_tiles],
                     list(src_bufs) + [ones_bf.buf], nm="ln_sum")
        else:
            bank = next_bank()
            for hf in range(2):
                for i in range(nt // 2):
                    g = hf * (nt // 2) + i
                    op("scalar", lambda s,
                       _d=scratch.ap()[:, i * w:(i + 1) * w],
                       _s=src_tiles[g]: s.activation(_d, _s, AF.Copy),
                       reads=list(src_bufs), writes=[scratch.buf],
                       nm="lncast")
                mm_group(bank, bank[0].ap()[0:1, 0:w],
                         [(ones_bf.ap()[:, 0:1],
                           scratch.ap()[:, i * w:(i + 1) * w])
                          for i in range(nt // 2)],
                         [scratch.buf, ones_bf.buf], nm="ln_sum",
                         start=(hf == 0), stop=(hf == 1))
        op("scalar", lambda s, _d=sr_sm.ap()[0:1, 0:w],
           _s=bank[0].ap()[0:1, 0:w]:
           s.activation(_d, _s, AF.Copy, scale=1.0 / D),
           reads=[bank[1]], writes=[sr_sm.buf], nm="ln_m")
        bank2 = next_bank()
        for hf in range(2):
            for i in range(nt // 2):
                g = hf * (nt // 2) + i
                op("scalar", lambda s, _d=scratch.ap()[:, i * w:(i + 1) * w],
                   _s=src_tiles[g]: s.activation(_d, _s, AF.Square),
                   reads=list(src_bufs), writes=[scratch.buf], nm="lnsq")
            mm_group(bank2, bank2[0].ap()[0:1, 0:w],
                     [(ones_bf.ap()[:, 0:1],
                       scratch.ap()[:, i * w:(i + 1) * w])
                      for i in range(nt // 2)],
                     [scratch.buf, ones_bf.buf], nm="ln_sum2",
                     start=(hf == 0), stop=(hf == 1))
        op("scalar", lambda s, _d=sr_s2.ap()[0:1, 0:w],
           _s=bank2[0].ap()[0:1, 0:w]:
           s.activation(_d, _s, AF.Copy, scale=1.0 / D),
           reads=[bank2[1]], writes=[sr_s2.buf], nm="ln_e2")
        op("vector", lambda v, _t=sr_tmp.ap()[0:1, 0:w],
           _m=sr_sm.ap()[0:1, 0:w]: v.tensor_mul(_t, _m, _m),
           reads=[sr_sm.buf], writes=[sr_tmp.buf], nm="ln_msq")
        op("vector", lambda v, _t=sr_tmp.ap()[0:1, 0:w],
           _a=sr_s2.ap()[0:1, 0:w], _b=sr_tmp.ap()[0:1, 0:w]:
           v.tensor_sub(_t, _a, _b),
           reads=[sr_s2.buf, sr_tmp.buf], writes=[sr_tmp.buf], nm="ln_var")
        op("scalar", lambda s, _t=sr_tmp.ap()[0:1, 0:w],
           _e=eps_sb.ap()[0:1, 0:1]:
           s.activation(_t, _t, AF.Sqrt, bias=_e),
           reads=[sr_tmp.buf, eps_sb.buf], writes=[sr_tmp.buf], nm="ln_sqrt")
        P.plan("vector", lambda v, _d=r_ap, _s=sr_tmp.ap()[0:1, 0:w]:
               v.reciprocal(_d, _s), reads=[sr_tmp.buf], writes=[rbuf],
               nm="ln_rs", selfsync=True)
        op("vector", lambda v, _d=m_ap, _s=sr_sm.ap()[0:1, 0:w]:
           v.tensor_copy(_d, _s), reads=[sr_sm.buf], writes=[mbuf], nm="ln_mc")

    def bcast_row(row_ap, rowbuf, w, bf=False):
        bank = next_bank()
        lo = ones_f1
        mm_group(bank, bank[0].ap()[:, 0:w],
                 [(lo.ap()[0:1, 0:128], row_ap)],
                 [rowbuf, lo.buf], nm="bcast")
        return bank

    bcast_rr = [0]

    def bcast_row_sb(row_ap, rowbuf, w):
        bank = bcast_row(row_ap, rowbuf, w)
        dst = sr_tmp if bcast_rr[0] % 2 == 0 else sr_sm
        bcast_rr[0] += 1
        op("scalar", lambda s, _d=dst.ap()[:, 0:w],
           _s=bank[0].ap()[:, 0:w]: s.activation(_d, _s, AF.Copy),
           reads=[bank[1]], writes=[dst.buf], nm="bcastc")
        return (dst, dst.buf)

    wt_sb = ar.alloc(NT * D, BF16, "wt_sb")
    tab_sb = ar.alloc(NT * D, BF16, "tab_sb")
    stg = ar.alloc(2048, BF16, "stage")

    def fold_one(wname, tname, outname, ncols, reload_w):
        if reload_w:
            src = dap(wname).rearrange("(kt p) m -> p kt m", p=128)
            P.dma("GQ", wt_sb.ap().rearrange("p (kt m) -> p kt m", m=D), src,
                  reads=[db(wname)], writes=[wt_sb.buf], nm=f"cWT_{wname}")
        P.dma("SQ", tab_sb.ap()[:, 0:NT * ncols], dap(tname),
              reads=[db(tname)], writes=[tab_sb.buf], nm=f"ldtab_{tname}")
        for mt in range(NT):
            for nch0 in range(0, ncols, 512):
                sl = nch0 // 512
                bank = next_bank()
                pairs = []
                for kt in range(NT):
                    l = wt_sb.ap()[:, kt * D + mt * 128:kt * D + mt * 128 + 128]
                    r_ = tab_sb.ap()[:, sl * 8192 + kt * 512:
                                     sl * 8192 + kt * 512 + 512]
                    pairs.append((l, r_))
                mm_group(bank, bank[0].ap(), pairs,
                         [wt_sb.buf, tab_sb.buf], nm=f"fo_{outname}")
                op("scalar", lambda s, _d=stg.ap()[:, nch0:nch0 + 512],
                   _s=bank[0].ap(): s.activation(_d, _s, AF.Copy),
                   reads=[bank[1]], writes=[stg.buf], nm="fo_e")
            nsl = ncols // 256
            d = dap(outname)[:, 0:nsl * 4096].rearrange(
                "p (s km) -> p s km", km=4096)[:, :, mt * 256:(mt + 1) * 256]
            s_ = stg.ap()[:, 0:ncols].rearrange("p (s m) -> p s m", m=256)
            P.dma("AQ", d, s_, reads=[stg.buf], writes=[db(outname)],
                  nm=f"st_{outname}")

    for w, wn in (("q", "Wq_T"), ("k", "Wk_T"), ("v", "Wv_T"), ("s", "Ws_T")):
        fold_one(wn, "Ch", f"fold_{w}C", NB, True)
        fold_one(wn, "Sm", f"fold_{w}S", NB, False)
    fold_one("Wt_T", "Ch", "fold_tC", NB, True)
    fold_one("Wt_T", "Sm", "fold_tS", NB, False)
    ar.free(wt_sb)
    ar.free(tab_sb)
    ar.free(stg)

    DEFER = []
    DEFER2 = []

    def drain_defer(n, lst=None):
        lst = DEFER if lst is None else lst
        for _ in range(n):
            if lst:
                lst.pop(0)()

    def emit_w1eff_fold():
        QW = 512
        pa, pb, po = w1pool
        for grp, (na, nb_, sgn) in enumerate((("W1e_a", "W1e_b", OP.add),
                                              ("W1e_c", "W1e_d",
                                               OP.subtract))):
            for kt in range(NBT):
                for hh in range(HID // QW):

                    def piece(na=na, nb_=nb_, sgn=sgn, kt=kt, hh=hh,
                              grp=grp):
                        P.dma("SQ", pa.ap(),
                              dap(na)[kt * 128:(kt + 1) * 128,
                                      hh * QW:(hh + 1) * QW],
                              reads=[db(na)], writes=[pa.buf], nm="ld_w1a")
                        P.dma("SQ", pb.ap(),
                              dap(nb_)[kt * 128:(kt + 1) * 128,
                                       hh * QW:(hh + 1) * QW],
                              reads=[db(nb_)], writes=[pb.buf], nm="ld_w1b")
                        op("vector", lambda v, _o=po.ap(), _a=pa.ap(),
                           _b=pb.ap(), _s=sgn:
                           v.tensor_tensor(_o, _a, _b, _s),
                           reads=[pa.buf, pb.buf], writes=[po.buf],
                           nm="w1fold")
                        ktg = grp * NBT + kt
                        d = dap("W1eff").rearrange(
                            "p (s km) -> p s km", km=4096)[
                            :, hh * (QW // 256):(hh + 1) * (QW // 256),
                            ktg * 256:(ktg + 1) * 256]
                        P.dma("AQ", d,
                              po.ap().rearrange("p (s m) -> p s m", m=256),
                              reads=[po.buf], writes=[db("W1eff")],
                              nm="st_w1eff")
                    DEFER.append(piece)

    def emit_converts():
        for s_, d_ in (("W2a", "W2a_d"), ("W1f_T", "W1f_d"),
                       ("W3f_T", "W3f_d"), ("W2f_T", "W2f_d"),
                       ("A1p", "A1d"), ("A3p", "A3d"), ("A2p", "A2d"),
                       ("B1p", "B1d"), ("B3p", "B3d"), ("B2p", "B2d")):
            C = dh(s_).shape[1]
            step = 32768
            for c0 in range(0, C, step):
                w = min(step, C - c0)

                def piece(s_=s_, d_=d_, c0=c0, w=w):
                    P.dma("GQ", dap(d_)[:, c0:c0 + w], dap(s_)[:, c0:c0 + w],
                          reads=[db(s_)], writes=[db(d_)], nm=f"cv_{s_}")
                (DEFER if s_ == "W2a" else DEFER2).append(piece)

    ln1 = ar.alloc(NT * TC, BF16, "ln1")
    slab0 = ar.alloc(NT * 256, BF16, "slab0")
    slab1 = ar.alloc(NT * 256, BF16, "slab1")
    onesTC = ar.alloc(TC, F32, "onesTC")
    tmp1 = ar.alloc(TC, F32, "tmp1")
    tmp2 = ar.alloc(TC, F32, "tmp2")
    tmp34 = [None, None]
    op("vector", lambda v: v.memset(onesTC.ap(), 1.0), writes=[onesTC.buf])

    def tile(a, i, w=TC):
        return a.ap()[:, i * w:(i + 1) * w]

    xbf_bufs = [ar.alloc(NT * TC, BF16, "x_bf0")]
    xseq = ([("xR_T", c) for c in range(NCH)] +
            [("xT", c) for c in range(NCH)] +
            [("xT", c) for c in range(NCH)])
    xseq_i = [0]

    def issue_xload(i):
        if i >= len(xseq):
            return
        xsrc, c = xseq[i]
        xb = xbf_bufs[0]
        P.dma("GQ", xb.ap(), dap(xsrc)[:, c * NT * TC:(c + 1) * NT * TC],
              reads=[db(xsrc)], writes=[xb.buf], nm="ld_xbf")

    issue_xload(0)

    def compute_ln1(c, want_stats, sq_scratch, xsrc="xT"):
        i = xseq_i[0]
        assert xseq[i] == (xsrc, c), (xseq[i], xsrc, c)
        xseq_i[0] += 1
        x_bf = xbf_bufs[0]
        m_ap = stats_m.ap()[0:1, c * TC:(c + 1) * TC]
        r_ap = stats_r.ap()[0:1, c * TC:(c + 1) * TC]
        if want_stats:
            ln_stats([tile(x_bf, i2) for i2 in range(NT)], [x_bf.buf],
                     sq_scratch, TC, m_ap, r_ap, stats_m.buf, stats_r.buf,
                     True)
        bm = bcast_row_sb(m_ap, stats_m.buf, TC)
        br = bcast_row_sb(r_ap, stats_r.buf, TC)
        lnv = ln1.ap().rearrange("p (t m) -> p t m", m=TC)
        xv = x_bf.ap().rearrange("p (t m) -> p t m", m=TC)
        op("vector", lambda v, _d=lnv, _a=xv,
           _b=bm[0].ap()[:, 0:TC].unsqueeze(1).broadcast_to([128, NT, TC]):
           v.tensor_sub(_d, _a, _b),
           reads=[x_bf.buf, bm[1]], writes=[ln1.buf], nm="ln1a")
        op("vector", lambda v, _d=lnv, _a=lnv,
           _b=br[0].ap()[:, 0:TC].unsqueeze(1).broadcast_to([128, NT, TC]):
           v.tensor_mul(_d, _a, _b),
           reads=[ln1.buf, br[1]], writes=[ln1.buf], nm="ln1b")
        issue_xload(i + 1)

    def proj_plane(foldname, outplane, nm):
        for q4 in range(4):
            slab = slab0 if q4 % 2 == 0 else slab1
            load_image_slab(foldname, slab, q4, 4096, nm=f"l{nm}")
            for mt in range(2):
                bank = next_bank()
                pairs = []
                for kt in range(NT):
                    l = slab.ap()[:, kt * 256 + mt * 128:
                                  kt * 256 + mt * 128 + 128]
                    pairs.append((l, tile(ln1, kt)))
                mm_group(bank, bank[0].ap(), pairs, [slab.buf, ln1.buf], nm=nm)
                gi = q4 * 2 + mt
                op("scalar", lambda s, _d=tile(outplane, gi),
                   _s=bank[0].ap(): s.activation(_d, _s, AF.Copy),
                   reads=[bank[1]], writes=[outplane.buf], nm=nm + "e")

    def pk_mul_planes(aR, aI, bR, bI, oR, oI, conj, nm):
        s1 = OP.add if conj else OP.subtract
        s2 = OP.subtract if conj else OP.add
        for i in range(NBT):
            op("vector", lambda v, _d=tmp1.ap()[:, 0:TC], _a=tile(aR, i),
               _b=tile(bR, i): v.tensor_mul(_d, _a, _b),
               reads=[aR.buf, bR.buf], writes=[tmp1.buf], nm=nm + "1")
            op("vector", lambda v, _d=tmp2.ap()[:, 0:TC], _a=tile(aI, i),
               _b=tile(bI, i): v.tensor_mul(_d, _a, _b),
               reads=[aI.buf, bI.buf], writes=[tmp2.buf], nm=nm + "2")
            op("vector", lambda v, _d=tile(oR, i), _a=tmp1.ap()[:, 0:TC],
               _b=tmp2.ap()[:, 0:TC], _s=s1: v.tensor_tensor(_d, _a, _b, _s),
               reads=[tmp1.buf, tmp2.buf], writes=[oR.buf], nm=nm + "3")
            if i == 0:
                op("vector", lambda v, _d=tile(oR, 0)[0:1, :],
                   _s=tmp1.ap()[0:1, 0:TC]: v.tensor_copy(_d, _s),
                   reads=[tmp1.buf], writes=[oR.buf], nm=nm + "f1")
                op("vector", lambda v, _d=tmp1.ap()[0:1, 0:TC],
                   _s=tmp2.ap()[0:1, 0:TC]: v.tensor_copy(_d, _s),
                   reads=[tmp2.buf], writes=[tmp1.buf], nm=nm + "f2")
            op("vector", lambda v, _d=tmp2.ap()[:, 0:TC], _a=tile(aR, i),
               _b=tile(bI, i): v.tensor_mul(_d, _a, _b),
               reads=[aR.buf, bI.buf], writes=[tmp2.buf], nm=nm + "4")
            op("vector", lambda v, _d=tile(oI, i), _a=tile(aI, i),
               _b=tile(bR, i): v.tensor_mul(_d, _a, _b),
               reads=[aI.buf, bR.buf], writes=[oI.buf], nm=nm + "5")
            op("vector", lambda v, _d=tile(oI, i), _a=tmp2.ap()[:, 0:TC],
               _b=tile(oI, i), _s=s2: v.tensor_tensor(_d, _a, _b, _s),
               reads=[tmp2.buf, oI.buf], writes=[oI.buf], nm=nm + "6")
            if i == 0:
                op("vector", lambda v, _d=tile(oI, 0)[0:1, :],
                   _s=tmp1.ap()[0:1, 0:TC]: v.tensor_copy(_d, _s),
                   reads=[tmp1.buf], writes=[oI.buf], nm=nm + "f3")

    sq_g = ar.alloc(8 * TC, BF16, "sq_g")
    plA = ar.alloc(NBT * TC, F32, "plA")
    plB = ar.alloc(NBT * TC, F32, "plB")
    plC = ar.alloc(NBT * TC, F32, "plC")
    plD = ar.alloc(NBT * TC, F32, "plD")
    plE = ar.alloc(NBT * TC, F32, "plE")
    plF = ar.alloc(NBT * TC, F32, "plF")

    def p1_chunk(c, xsrc, carrybuf, full, tap=False):
        compute_ln1(c, True, sq_g, xsrc)
        if tap:
            P.dma("GQ", dbg_ln1.ap().rearrange("(kt p) m -> p kt m", p=128),
                  ln1.ap().rearrange("p (kt m) -> p kt m", m=TC),
                  reads=[ln1.buf], writes=[dbg_ln1_b], nm="tap_ln1")
        proj_plane("fold_qC", plA, "pqC")
        proj_plane("fold_qS", plB, "pqS")
        proj_plane("fold_kC", plC, "pkC")
        proj_plane("fold_kS", plD, "pkS")
        if tap:
            for half, pl in ((0, plA), (1, plB)):
                P.dma("AQ", dbg_q.ap()[half * NB:(half + 1) * NB, :]
                      .rearrange("(kt p) m -> p kt m", p=128),
                      pl.ap().rearrange("p (kt m) -> p kt m", m=TC),
                      reads=[pl.buf], writes=[dbg_q_b], nm="tap_q")
        pk_mul_planes(plA, plB, plC, plD, plE, plF, True, "m")
        proj_plane("fold_vC", plA, "pvC")
        proj_plane("fold_vS", plB, "pvS")
        pk_mul_planes(plE, plF, plA, plB, plC, plD, False, "p")
        if tap:
            for half, pl in ((0, plC), (1, plD)):
                P.dma("AQ", dbg_p.ap()[half * NB:(half + 1) * NB, :]
                      .rearrange("(kt p) m -> p kt m", p=128),
                      pl.ap().rearrange("p (kt m) -> p kt m", m=TC),
                      reads=[pl.buf], writes=[dbg_p_b], nm="tap_p")
        for half, (src, dstp) in enumerate(((plC, plE), (plD, plF))):
            for i in range(NBT):
                ci = half * 8 + i
                P.plan("vector", lambda v, _d=tile(dstp, i), _o=onesTC.ap(),
                       _s=tile(src, i), _in=carrybuf.ap()[:, ci:ci + 1]:
                       v.tensor_tensor_scan(_d, _o, _s, _in, OP.mult, OP.add),
                       reads=[src.buf, carrybuf.buf, onesTC.buf],
                       writes=[dstp.buf], nm="scan", selfsync=True)
                P.plan("vector", lambda v, _d=carrybuf.ap()[:, ci:ci + 1],
                       _s=tile(dstp, i)[:, TC - 1:TC]: v.tensor_copy(_d, _s),
                       reads=[dstp.buf], writes=[carrybuf.buf], nm="carry",
                       selfsync=True)
        if not full:
            return
        for tabname, dstp in (("Ch", plE), ("Sm", plF)):
            for s8 in range(2):
                P.dma("SQ", slab0.ap()[:, 0:4096],
                      dap(tabname)[:, s8 * 8192:s8 * 8192 + 4096],
                      reads=[db(tabname)], writes=[slab0.buf], nm="lfft0")
                P.dma("SQ", slab1.ap()[:, 0:4096],
                      dap(tabname)[:, s8 * 8192 + 4096:(s8 + 1) * 8192],
                      reads=[db(tabname)], writes=[slab1.buf], nm="lfft1")
                for mt in range(4):
                    bank = next_bank()
                    pairs = []
                    for kt in range(NT):
                        sb = slab0 if kt < 8 else slab1
                        l = sb.ap()[:, (kt % 8) * 512 + mt * 128:
                                    (kt % 8) * 512 + mt * 128 + 128]
                        pairs.append((l, tile(ln1, kt)))
                    mm_group(bank, bank[0].ap(), pairs,
                             [slab0.buf, slab1.buf, ln1.buf], nm="fft")
                    gi = s8 * 4 + mt
                    op("vector", lambda v, _d=tile(dstp, gi),
                       _a=bank[0].ap(), _b=tile(dstp, gi):
                       v.tensor_add(_d, _a, _b),
                       reads=[bank[1], dstp.buf], writes=[dstp.buf],
                       nm="fftadd")
        for half, pl in ((0, plE), (1, plF)):
            d = dap("cum_d")[:, c * NT * TC + half * NBT * TC:
                             c * NT * TC + (half + 1) * NBT * TC]
            P.dma("AQ", d, pl.ap(), reads=[pl.buf], writes=[db("cum_d")],
                  nm="st_cum")

    emit_w1eff_fold()
    emit_converts()
    NDEF0 = len(DEFER)
    NDEF2 = len(DEFER2)

    for c in range(NCH):
        p1_chunk(c, "xR_T", rcarry, False)
        drain_defer(NDEF0 // 9 + 1)
    for c in range(NCH):
        p1_chunk(c, "xT", carry, True, tap=(debug and c == 0))
        drain_defer(NDEF0 // 9 + 1)

    ar.free(sq_g)
    ar.free(plA)
    ar.free(plB)
    ar.free(plC)
    ar.free(plD)
    ar.free(plE)
    ar.free(plF)
    tmp3 = ar.alloc(TC, F32, "tmp3")
    tmp4 = ar.alloc(TC, F32, "tmp4")

    def xch_fn(g):
        xch_sem = globals()["_XCH_SEM"]
        cval = g.value_load(cid_sb.ap()[0:1, 0:1])
        sh = dap("sh_x").rearrange("(c p) m -> c p m", p=128)
        for i in g.Switch(cval, 2):
            g.dma_start(sh[i][:, 0:16], carry.ap()).then_inc(xch_sem, 16)
        g.wait_ge(xch_sem, 16)
        for i in g.Switch(cval, 2):
            g.dma_start(sh[i][:, 16:32], rcarry.ap()).then_inc(xch_sem, 16)
        g.wait_ge(xch_sem, 32)
        for i in g.Switch(cval, 2):
            g.dma_start(dap("sh_f")[i:i + 1, :],
                        mg_sb.ap()[0:1, :]).then_inc(xch_sem, 16)
        g.wait_ge(xch_sem, 48)
        with g.register("nd") as nd, g.register("cnt") as cnt, \
                g.register("acc") as acc:
            mgv = g.value_load(cid_sb.ap()[0:1, 1:2])
            g.reg_mov(nd, 1)
            g.reg_mov(cnt, 48)
            with g.While(nd):
                g.reg_add(cnt, cnt, 16)
                g.dma_start(fl_sb.ap()[0:1, :],
                            dap("sh_f").rearrange("a b -> (a b)")
                            .rearrange("(one f) -> one f", one=1)
                            ).then_inc(xch_sem, 16)
                g.wait_ge(xch_sem, cnt)
                g.reg_mov(acc, 0)
                with g.register("ecmp") as e:
                    for r_ in range(2):
                        v = g.value_load(
                            fl_sb.ap()[0:1, r_ * 16:r_ * 16 + 1])
                        g.reg_alu(e, v, mgv, mybir.AluOpType.is_equal)
                        g.reg_add(acc, acc, e)
                g.reg_alu(nd, acc, 2, mybir.AluOpType.not_equal)
            g.reg_add(cnt, cnt, 16)
            g.dma_start(ex_sb.ap().rearrange("p (c m) -> p c m", m=32),
                        dap("sh_x").rearrange("(c p) m -> p c m", p=128)
                        ).then_inc(xch_sem, 16)
            g.wait_ge(xch_sem, cnt)
            g.reg_save(pollct.ap()[0:1, 0:1], cnt)
        return g.nop()
    P.plan("gpsimd", xch_fn, reads=[carry.buf, rcarry.buf, cid_sb.buf,
                                    mg_sb.buf, db("sh_x"), db("sh_f")],
           writes=[ex_sb.buf, fl_sb.buf, pollct.buf], nm="exchange")
    op("vector", lambda v: v.memset(basep.ap(), 0.0), writes=[basep.buf])
    for s_ in range(2):
        op("vector", lambda v, _d=basep.ap(),
           _a=ex_sb.ap()[:, s_ * 32:s_ * 32 + 16],
           _m=exw_sb.ap()[:, s_:s_ + 1], _b=basep.ap():
           v.scalar_tensor_tensor(_d, _a, _m, _b, OP.mult, OP.add),
           reads=[ex_sb.buf, exw_sb.buf, basep.buf], writes=[basep.buf],
           nm="baseT")
        op("vector", lambda v, _d=basep.ap(),
           _a=ex_sb.ap()[:, s_ * 32 + 16:s_ * 32 + 32],
           _m=exw_sb.ap()[:, 2 + s_:3 + s_], _b=basep.ap():
           v.scalar_tensor_tensor(_d, _a, _m, _b, OP.mult, OP.add),
           reads=[ex_sb.buf, exw_sb.buf, basep.buf], writes=[basep.buf],
           nm="baseR")

    if debug:
        dsc = ar.alloc(64, F32, "dbgsm")
        op("vector", lambda v, _d=dsc.ap()[:, 0:16], _s=carry.ap():
           v.tensor_copy(_d, _s), reads=[carry.buf], writes=[dsc.buf])
        op("vector", lambda v, _d=dsc.ap()[:, 16:32], _s=rcarry.ap():
           v.tensor_copy(_d, _s), reads=[rcarry.buf], writes=[dsc.buf])
        op("vector", lambda v, _d=dsc.ap()[:, 32:48], _s=basep.ap():
           v.tensor_copy(_d, _s), reads=[basep.buf], writes=[dsc.buf])
        op("vector", lambda v, _d=dsc.ap()[:, 48:63],
           _s=ex_sb.ap()[:, 0:15]: v.tensor_copy(_d, _s),
           reads=[ex_sb.buf], writes=[dsc.buf])
        op("vector", lambda v, _d=dsc.ap()[0:1, 63:64],
           _s=pollct.ap()[0:1, 0:1].bitcast(F32) if False else
           pollct.ap()[0:1, 0:1]: v.tensor_copy(_d, _s),
           reads=[pollct.buf], writes=[dsc.buf])
        P.dma("AQ", dbg_sm.ap(), dsc.ap(), reads=[dsc.buf],
              writes=[dbg_sm_b], nm="st_dbgsm")
        ar.free(dsc)
        dbb = ar.alloc(4096, F32, "dbgb")
        for c in range(NCH * NT * TC // 4096):
            P.dma("SQ", dbb.ap(), dap("cum_d")[:, c * 4096:(c + 1) * 4096],
                  reads=[db("cum_d")], writes=[dbb.buf], nm="dbgld")
            P.dma("AQ", dbg_cum.ap()[:, c * 4096:(c + 1) * 4096], dbb.ap(),
                  reads=[dbb.buf], writes=[dbg_cum_b], nm="dbgst")
        ar.free(dbb)


    for c in range(NCH):
        compute_ln1(c, False, None)
        if c == 0:
            drain_defer(len(DEFER))
            if debug:
                for j in range(4):
                    P.dma("GQ", dbg_w1.ap()[:, j * 32768:(j + 1) * 32768],
                          dap("W1eff")[:, j * 32768:(j + 1) * 32768],
                          reads=[db("W1eff")], writes=[dbg_w1_b],
                          nm="tap_w1")
        else:
            drain_defer(NDEF2 // 3 + 1, DEFER2)
        pC = ar.alloc(NBT * TC, F32, "pC")
        pD = ar.alloc(NBT * TC, F32, "pD")
        trb = ar.alloc(NT * TC, BF16, "trb")
        pA = ar.alloc(NBT * TC, F32, "pA")
        pB = ar.alloc(NBT * TC, F32, "pB")
        tmp5 = ar.alloc(TC, F32, "tmp5")
        tmp6 = ar.alloc(TC, F32, "tmp6")
        for half, pl in ((0, pA), (1, pB)):
            s = dap("cum_d")[:, c * NT * TC + half * NBT * TC:
                             c * NT * TC + (half + 1) * NBT * TC]
            P.dma("SQ", pl.ap(), s, reads=[db("cum_d")], writes=[pl.buf],
                  nm="ld_cum")
            for i in range(NBT):
                ci = half * 8 + i
                op("vector", lambda v, _d=tile(pl, i), _s=tile(pl, i),
                   _b=basep.ap()[:, ci:ci + 1]:
                   v.tensor_scalar_add(_d, _s, _b),
                   reads=[pl.buf, basep.buf], writes=[pl.buf], nm="addbase")
        proj_plane("fold_sC", pC, "psC")
        proj_plane("fold_sS", pD, "psS")
        for i in range(NBT):
            op("vector", lambda v, _d=tmp1.ap()[:, 0:TC], _a=tile(pC, i),
               _b=tile(pA, i): v.tensor_mul(_d, _a, _b),
               reads=[pC.buf, pA.buf], writes=[tmp1.buf], nm="ta1")
            op("vector", lambda v, _d=tmp2.ap()[:, 0:TC], _a=tile(pD, i),
               _b=tile(pB, i): v.tensor_mul(_d, _a, _b),
               reads=[pD.buf, pB.buf], writes=[tmp2.buf], nm="ta2")
            op("vector", lambda v, _d=tmp5.ap()[:, 0:TC],
               _a=tmp1.ap()[:, 0:TC], _b=tmp2.ap()[:, 0:TC]:
               v.tensor_add(_d, _a, _b),
               reads=[tmp1.buf, tmp2.buf], writes=[tmp5.buf], nm="ta3")
            if i == 0:
                op("vector", lambda v, _d=tmp5.ap()[0:1, 0:TC],
                   _s=tmp1.ap()[0:1, 0:TC]: v.tensor_copy(_d, _s),
                   reads=[tmp1.buf], writes=[tmp5.buf], nm="taf1")
                op("vector", lambda v, _d=tmp1.ap()[0:1, 0:TC],
                   _s=tmp2.ap()[0:1, 0:TC]: v.tensor_copy(_d, _s),
                   reads=[tmp2.buf], writes=[tmp1.buf], nm="taf2")
            op("vector", lambda v, _d=tmp2.ap()[:, 0:TC], _a=tile(pC, i),
               _b=tile(pB, i): v.tensor_mul(_d, _a, _b),
               reads=[pC.buf, pB.buf], writes=[tmp2.buf], nm="ta4")
            op("vector", lambda v, _d=tmp6.ap()[:, 0:TC], _a=tile(pD, i),
               _b=tile(pA, i): v.tensor_mul(_d, _a, _b),
               reads=[pD.buf, pA.buf], writes=[tmp6.buf], nm="ta5")
            op("vector", lambda v, _d=tmp6.ap()[:, 0:TC],
               _a=tmp2.ap()[:, 0:TC], _b=tmp6.ap()[:, 0:TC]:
               v.tensor_sub(_d, _a, _b),
               reads=[tmp2.buf, tmp6.buf], writes=[tmp6.buf], nm="ta6")
            if i == 0:
                op("vector", lambda v, _d=tmp6.ap()[0:1, 0:TC],
                   _s=tmp1.ap()[0:1, 0:TC]: v.tensor_copy(_d, _s),
                   reads=[tmp1.buf], writes=[tmp6.buf], nm="taf3")
            op("scalar", lambda s, _d=tile(trb, i), _s=tmp5.ap()[:, 0:TC]:
               s.copy(_d, _s), reads=[tmp5.buf], writes=[trb.buf], nm="trbR")
            op("scalar", lambda s, _d=tile(trb, 8 + i),
               _s=tmp6.ap()[:, 0:TC]:
               s.copy(_d, _s), reads=[tmp6.buf], writes=[trb.buf], nm="trbI")
            op("vector", lambda v, _d=tmp1.ap()[:, 0:TC], _a=tile(pC, i),
               _b=tmp5.ap()[:, 0:TC]: v.tensor_mul(_d, _a, _b),
               reads=[pC.buf, tmp5.buf], writes=[tmp1.buf], nm="sf1")
            op("vector", lambda v, _d=tmp2.ap()[:, 0:TC], _a=tile(pD, i),
               _b=tmp6.ap()[:, 0:TC]: v.tensor_mul(_d, _a, _b),
               reads=[pD.buf, tmp6.buf], writes=[tmp2.buf], nm="sf2")
            op("vector", lambda v, _d=tmp3.ap()[:, 0:TC], _a=tile(pC, i),
               _b=tmp6.ap()[:, 0:TC]: v.tensor_mul(_d, _a, _b),
               reads=[pC.buf, tmp6.buf], writes=[tmp3.buf], nm="sf3")
            op("vector", lambda v, _d=tmp4.ap()[:, 0:TC], _a=tile(pD, i),
               _b=tmp5.ap()[:, 0:TC]: v.tensor_mul(_d, _a, _b),
               reads=[pD.buf, tmp5.buf], writes=[tmp4.buf], nm="sf4")
            op("vector", lambda v, _d=tmp3.ap()[:, 0:TC],
               _a=tmp3.ap()[:, 0:TC], _b=tmp4.ap()[:, 0:TC]:
               v.tensor_add(_d, _a, _b),
               reads=[tmp3.buf, tmp4.buf], writes=[tmp3.buf], nm="sf5")
            if i == 0:
                op("vector", lambda v, _d=tmp3.ap()[0:1, 0:TC],
                   _s=tmp2.ap()[0:1, 0:TC]: v.tensor_copy(_d, _s),
                   reads=[tmp2.buf], writes=[tmp3.buf], nm="sf6")
            op("vector", lambda v, _d=tmp4.ap()[:, 0:TC],
               _a=tmp1.ap()[:, 0:TC], _b=tmp2.ap()[:, 0:TC]:
               v.tensor_sub(_d, _a, _b),
               reads=[tmp1.buf, tmp2.buf], writes=[tmp4.buf], nm="sf7")
            if i == 0:
                op("vector", lambda v, _d=tmp4.ap()[0:1, 0:TC],
                   _s=tmp1.ap()[0:1, 0:TC]: v.tensor_copy(_d, _s),
                   reads=[tmp1.buf], writes=[tmp4.buf], nm="sf8")
            op("vector", lambda v, _d=tile(pC, i), _a=tile(pA, i),
               _b=tmp4.ap()[:, 0:TC]: v.tensor_sub(_d, _a, _b),
               reads=[pA.buf, tmp4.buf], writes=[pC.buf], nm="hfR")
            op("vector", lambda v, _d=tile(pD, i), _a=tile(pB, i),
               _b=tmp3.ap()[:, 0:TC]: v.tensor_sub(_d, _a, _b),
               reads=[pB.buf, tmp3.buf], writes=[pD.buf], nm="hfI")
        if debug and c == 0:
            P.dma("GQ", dbg_u.ap().rearrange("p (kt m) -> p kt m", m=TC),
                  trb.ap().rearrange("p (kt m) -> p kt m", m=TC),
                  reads=[trb.buf], writes=[dbg_u_b], nm="tap_u")
        ar.free(pA)
        ar.free(pB)
        ar.free(tmp5)
        ar.free(tmp6)
        ar.free(tmp1)
        ar.free(tmp2)
        ar.free(tmp3)
        ar.free(tmp4)
        hfb = ar.alloc(NT * TC, BF16, "hfb")
        for i in range(NBT):
            op("scalar", lambda s, _d=tile(hfb, i), _s=tile(pC, i):
               s.copy(_d, _s), reads=[pC.buf], writes=[hfb.buf], nm="hfbR")
            op("scalar", lambda s, _d=tile(hfb, 8 + i), _s=tile(pD, i):
               s.copy(_d, _s), reads=[pD.buf], writes=[hfb.buf], nm="hfbI")
        ar.free(pC)
        ar.free(pD)
        if debug and c == 0:
            P.dma("GQ", dbg_hf.ap().rearrange("p (kt m) -> p kt m", m=TC),
                  hfb.ap().rearrange("p (kt m) -> p kt m", m=TC),
                  reads=[hfb.buf], writes=[dbg_hf_b], nm="tap_hf")
        h1h = [ar.alloc(NHT * TC // 2, BF16, f"h1h{q}") for q in range(2)]

        def h1t(i):
            return h1h[i // 32].ap()[:, (i % 32) * TC:(i % 32 + 1) * TC]

        def h1b(i):
            return h1h[i // 32].buf

        for sl in range(32):
            slab = slab0 if sl % 2 == 0 else slab1
            load_image_slab("W1eff", slab, sl, 4096, nm="lw1e")
            for mt in range(2):
                gm = sl * 2 + mt
                bank = next_bank()
                pairs = []
                for kt in range(NT):
                    l = slab.ap()[:, kt * 256 + mt * 128:
                                  kt * 256 + mt * 128 + 128]
                    pairs.append((l, tile(trb, kt)))
                mm_group(bank, bank[0].ap(), pairs, [slab.buf, trb.buf],
                         nm="mlp1")
                op("scalar", lambda s, _d=h1t(gm), _s=bank[0].ap(),
                   _b=b1_sb.ap()[:, gm:gm + 1]:
                   s.activation(_d, _s, AF.Relu, bias=_b),
                   reads=[bank[1], b1_sb.buf], writes=[h1b(gm)], nm="relu")
        ar.free(trb)
        if debug and c == 0:
            for q in range(2):
                P.dma("GQ", dbg_h1.ap()[:, q * 16384:(q + 1) * 16384]
                      .rearrange("p (j m) -> p j m", m=TC),
                      h1h[q].ap().rearrange("p (j m) -> p j m", m=TC),
                      reads=[h1h[q].buf], writes=[dbg_h1_b], nm="tap_h1")
        trfq = [ar.alloc(4 * TC, BF16, f"trfq{q}") for q in range(8)]

        def trft(i):
            return trfq[i // 4].ap()[:, (i % 4) * TC:(i % 4 + 1) * TC]

        def trfb(i):
            return trfq[i // 4].buf

        for mt in range(2 * NT):
            bank = next_bank()
            for kh in range(2):
                slab = slab0 if kh == 0 else slab1
                load_image_slab("W2a_d", slab, mt * 2 + kh, 4096, nm="lw2a")
                pairs = []
                for kt in range(32):
                    pairs.append((slab.ap()[:, kt * 128:kt * 128 + 128],
                                  h1t(kh * 32 + kt)))
                mm_group(bank, bank[0].ap(), pairs,
                         [slab.buf, h1h[kh].buf],
                         nm="mlp2", start=(kh == 0), stop=(kh == 1))
            op("scalar", lambda s, _d=trft(mt), _s=bank[0].ap(),
               _b=b2_sb.ap()[:, mt:mt + 1]:
               s.activation(_d, _s, AF.Identity, bias=_b),
               reads=[bank[1], b2_sb.buf], writes=[trfb(mt)], nm="mlp2e")
        if debug and c == 0:
            for q in range(8):
                P.dma("GQ", dbg_trf.ap()[:, q * 4 * TC:(q + 1) * 4 * TC]
                      .rearrange("p (j m) -> p j m", m=TC),
                      trfq[q].ap().rearrange("p (j m) -> p j m", m=TC),
                      reads=[trfq[q].buf], writes=[dbg_trf_b], nm="tap_trf")
        ar.free(h1h[0])
        ar.free(h1h[1])
        tmp1 = ar.alloc(TC, F32, "tmp1")
        tmp2 = ar.alloc(TC, F32, "tmp2")
        tmp3 = ar.alloc(TC, F32, "tmp3")
        tmp4 = ar.alloc(TC, F32, "tmp4")
        tPR = ar.alloc(NBT * TC, F32, "tPR")
        tPI = ar.alloc(NBT * TC, F32, "tPI")
        proj_plane("fold_tC", tPR, "ptC")
        proj_plane("fold_tS", tPI, "ptS")
        if debug and c == 0:
            P.dma("AQ", dbg_t.ap()[:, 0:NBT * TC], tPR.ap(),
                  reads=[tPR.buf], writes=[dbg_t_b], nm="tap_t")
            P.dma("AQ", dbg_t.ap()[:, NBT * TC:NT * TC], tPI.ap(),
                  reads=[tPI.buf], writes=[dbg_t_b], nm="tap_t2")
        for half in range(2):
            xoff = (1 + half) * 128
            for j in range(NBT):
                dtile = half * 16 + j
                ftile = half * 16 + 15 - j
                bank = next_bank()
                mm_group(bank, bank[0].ap(),
                         [(flip_sb.ap()[:, 0:128], trft(dtile)),
                          (flip_sb.ap()[:, xoff:xoff + 128], trft(ftile))],
                         [flip_sb.buf, trfb(dtile), trfb(ftile)], nm="tfold")
                op("scalar", lambda s, _d=trft(dtile), _s=bank[0].ap():
                   s.activation(_d, _s, AF.Copy),
                   reads=[bank[1]], writes=[trfb(dtile)], nm="tfolde")
        op("scalar", lambda s, _d=trft(0)[0:1, :], _s=trft(0)[0:1, :]:
           s.activation(_d, _s, AF.Copy, scale=2.0),
           reads=[trfb(0)], writes=[trfb(0)], nm="tfx0")
        op("scalar", lambda s, _d=trft(16)[0:1, :], _s=trft(8)[0:1, :]:
           s.activation(_d, _s, AF.Copy),
           reads=[trfb(8)], writes=[trfb(16)], nm="tfx1")
        for j in range(1, NBT):
            op("vector", lambda v, _d=trft(j)[0:1, :],
               _a=trft(16 - j)[0:1, :], _b=trft(j)[0:1, :]:
               v.scalar_tensor_tensor(_d, _a, 0.5, _b, OP.mult, OP.add),
               reads=[trfb(16 - j), trfb(j)], writes=[trfb(j)], nm="tfxr")
            op("vector", lambda v, _d=trft(16 + j)[0:1, :],
               _a=trft(32 - j)[0:1, :], _b=trft(16 + j)[0:1, :]:
               v.scalar_tensor_tensor(_d, _a, -0.5, _b, OP.mult, OP.add),
               reads=[trfb(32 - j), trfb(16 + j)], writes=[trfb(16 + j)],
               nm="tfxi")
        for j in range(NBT):
            op("vector", lambda v, _d=tmp1.ap()[:, 0:TC],
               _a=tile(tPR, j), _b=trft(j): v.tensor_mul(_d, _a, _b),
               reads=[tPR.buf, trfb(j)], writes=[tmp1.buf], nm="Q1")
            op("vector", lambda v, _d=tmp2.ap()[:, 0:TC],
               _a=tile(tPI, j), _b=trft(16 + j): v.tensor_mul(_d, _a, _b),
               reads=[tPI.buf, trfb(16 + j)], writes=[tmp2.buf], nm="Q2")
            op("vector", lambda v, _d=tmp3.ap()[:, 0:TC],
               _a=tmp1.ap()[:, 0:TC], _b=tmp2.ap()[:, 0:TC]:
               v.tensor_sub(_d, _a, _b),
               reads=[tmp1.buf, tmp2.buf], writes=[tmp3.buf], nm="Q3")
            if j == 0:
                op("vector", lambda v, _d=tmp3.ap()[0:1, 0:TC],
                   _s=tmp1.ap()[0:1, 0:TC]: v.tensor_copy(_d, _s),
                   reads=[tmp1.buf], writes=[tmp3.buf], nm="Qf1")
            op("vector", lambda v, _d=tmp1.ap()[:, 0:TC],
               _a=tile(tPR, j), _b=trft(16 + j): v.tensor_mul(_d, _a, _b),
               reads=[tPR.buf, trfb(16 + j)], writes=[tmp1.buf], nm="Q4")
            op("vector", lambda v, _d=tmp4.ap()[:, 0:TC],
               _a=tile(tPI, j), _b=trft(j): v.tensor_mul(_d, _a, _b),
               reads=[tPI.buf, trfb(j)], writes=[tmp4.buf], nm="Q5")
            op("vector", lambda v, _d=tmp1.ap()[:, 0:TC],
               _a=tmp1.ap()[:, 0:TC], _b=tmp4.ap()[:, 0:TC]:
               v.tensor_add(_d, _a, _b),
               reads=[tmp1.buf, tmp4.buf], writes=[tmp1.buf], nm="Q6")
            if j == 0:
                op("vector", lambda v, _d=tmp1.ap()[0:1, 0:TC],
                   _s=tmp2.ap()[0:1, 0:TC]: v.tensor_copy(_d, _s),
                   reads=[tmp2.buf], writes=[tmp1.buf], nm="Qf2")
            op("vector", lambda v, _d=tmp3.ap()[:, 0:TC],
               _a=tmp3.ap()[:, 0:TC], _b=tile(hfb, j):
               v.tensor_add(_d, _a, _b),
               reads=[tmp3.buf, hfb.buf], writes=[tmp3.buf], nm="Qh1")
            op("vector", lambda v, _d=tmp1.ap()[:, 0:TC],
               _a=tmp1.ap()[:, 0:TC], _b=tile(hfb, 8 + j):
               v.tensor_add(_d, _a, _b),
               reads=[tmp1.buf, hfb.buf], writes=[tmp1.buf], nm="Qh2")
            op("scalar", lambda s, _d=trft(8 + j), _s=tmp3.ap()[:, 0:TC]:
               s.copy(_d, _s), reads=[tmp3.buf], writes=[trfb(8 + j)],
               nm="Qc1")
            op("scalar", lambda s, _d=trft(24 + j), _s=tmp1.ap()[:, 0:TC]:
               s.copy(_d, _s), reads=[tmp1.buf], writes=[trfb(24 + j)],
               nm="Qc2")
        ar.free(tPR)
        ar.free(tPI)
        ar.free(hfb)
        if debug and c == 0:
            for q in range(8):
                P.dma("GQ", dbg_P.ap()[:, q * 4 * TC:(q + 1) * 4 * TC]
                      .rearrange("p (j m) -> p j m", m=TC),
                      trfq[q].ap().rearrange("p (j m) -> p j m", m=TC),
                      reads=[trfq[q].buf], writes=[dbg_P_b], nm="tap_P")
        attn_o = ar.alloc(NT * TC, F32, "attn_o")
        ic2s = ar.alloc(NBT * 256, BF16, "ic2s")
        is2s = ar.alloc(NBT * 256, BF16, "is2s")
        for msl in range(8):
            load_image_slab("IC2", ic2s, msl, 2048, nm="lic2")
            load_image_slab("IS2", is2s, msl, 2048, nm="lis2")
            for mt in range(2):
                gm = msl * 2 + mt
                bank = next_bank()
                pairs = []
                for kt in range(NBT):
                    pairs.append((ic2s.ap()[:, kt * 256 + mt * 128:
                                            kt * 256 + mt * 128 + 128],
                                  trft(8 + kt)))
                for kt in range(NBT):
                    pairs.append((is2s.ap()[:, kt * 256 + mt * 128:
                                            kt * 256 + mt * 128 + 128],
                                  trft(24 + kt)))
                mm_group(bank, bank[0].ap(), pairs,
                         [ic2s.buf, is2s.buf] + [q.buf for q in trfq],
                         nm="idft")
                op("scalar", lambda s, _d=tile(attn_o, gm), _s=bank[0].ap():
                   s.activation(_d, _s, AF.Copy),
                   reads=[bank[1]], writes=[attn_o.buf], nm="idfte")
        ar.free(ic2s)
        ar.free(is2s)
        for q in trfq:
            ar.free(q)
        if debug and c == 0:
            P.dma("AQ", dbg_ao.ap(), attn_o.ap(), reads=[attn_o.buf],
                  writes=[dbg_ao_b], nm="tap_ao")
        sqb = ar.alloc(8 * TC, BF16, "sqb")
        lm = ar.alloc(TC, F32, "lm2")
        lr = ar.alloc(TC, F32, "lr2")
        ln_stats([tile(attn_o, i) for i in range(NT)], [attn_o.buf], sqb, TC,
                 lm.ap()[0:1, :], lr.ap()[0:1, :], lm.buf, lr.buf, False)
        ar.free(sqb)
        bm = bcast_row_sb(lm.ap()[0:1, :], lm.buf, TC)
        br = bcast_row_sb(lr.ap()[0:1, :], lr.buf, TC)
        xof = ar.alloc(NT * TC, F32, "xof")
        P.dma("SQ", xof.ap(), dap("xT")[:, c * NT * TC:(c + 1) * NT * TC],
              reads=[db("xT")], writes=[xof.buf], nm="ld_xf32")
        for h2_ in range(2):
            aov = attn_o.ap()[:, h2_ * 8 * TC:(h2_ + 1) * 8 * TC].rearrange(
                "p (t m) -> p t m", m=TC)
            xov = xof.ap()[:, h2_ * 8 * TC:(h2_ + 1) * 8 * TC].rearrange(
                "p (t m) -> p t m", m=TC)
            op("vector", lambda v, _d=aov, _a=aov,
               _b=bm[0].ap()[:, 0:TC].unsqueeze(1)
               .broadcast_to([128, NBT, TC]):
               v.tensor_sub(_d, _a, _b),
               reads=[attn_o.buf, bm[1]], writes=[attn_o.buf], nm="r1")
            op("vector", lambda v, _d=aov, _a=aov,
               _b=br[0].ap()[:, 0:TC].unsqueeze(1)
               .broadcast_to([128, NBT, TC]):
               v.tensor_mul(_d, _a, _b),
               reads=[attn_o.buf, br[1]], writes=[attn_o.buf], nm="r2")
            op("vector", lambda v, _d=aov, _a=aov, _b=xov:
               v.tensor_add(_d, _a, _b),
               reads=[attn_o.buf, xof.buf], writes=[attn_o.buf], nm="r3")
        for h2 in range(2):
            cf = 2 * c + h2
            d = dap("x1_d")[:, cf * NT * TCF:(cf + 1) * NT * TCF].rearrange(
                "p (kt m) -> p kt m", m=TCF)
            s2_ = attn_o.ap().rearrange("p (kt m) -> p kt m", m=TC)[
                :, :, h2 * TCF:(h2 + 1) * TCF]
            P.dma("AQ", d, s2_, reads=[attn_o.buf], writes=[db("x1_d")],
                  nm="st_x1")
        ar.free(lm)
        ar.free(lr)
        ar.free(xof)
        ar.free(attn_o)

    if debug:
        dbb2 = ar.alloc(8192, F32, "dbgb2")
        for c in range(NCHF * NT * TCF // 8192):
            P.dma("SQ", dbb2.ap(), dap("x1_d")[:, c * 8192:(c + 1) * 8192],
                  reads=[db("x1_d")], writes=[dbb2.buf], nm="dbgld2")
            P.dma("AQ", dbg_x1.ap()[:, c * 8192:(c + 1) * 8192], dbb2.ap(),
                  reads=[dbb2.buf], writes=[dbg_x1_b], nm="dbgst2")
        ar.free(dbb2)

    drain_defer(len(DEFER))
    drain_defer(len(DEFER2), DEFER2)
    ar.free(w1pool[0])
    ar.free(w1pool[1])
    ar.free(w1pool[2])

    ar.free(ln1)
    ar.free(onesTC)
    ar.free(slab0)
    ar.free(slab1)
    ar.free(tmp1)
    ar.free(tmp2)
    ar.free(tmp3)
    ar.free(tmp4)
    ar.free(stats_m)
    ar.free(stats_r)

    sqf = None
    t_x1 = ar.alloc(TCF, BF16, "t_x1")
    t_s1 = ar.alloc(TCF, BF16, "t_s1")
    t_x3 = ar.alloc(TCF, BF16, "t_x3")
    out_acc = ar.alloc(NT * TCF, F32, "out_acc")
    tmpf = ar.alloc(TCF, F32, "tmpf")
    stk = ar.alloc(96, F32, "stk")
    rw_row = ar.alloc(TCF, F32, "rw_row")
    mk_bf = ar.alloc(2 * 256, BF16, "mk_bf")
    mexp = ar.alloc(4 * TCF, F32, "mexp")
    rwb = ar.alloc(2 * TCF, F32, "rwb")
    a1a = ar.alloc(2 * TCF, F32, "a1a")
    a3a = ar.alloc(2 * TCF, F32, "a3a")
    amk = [ar.alloc(2 * TCF, BF16, f"am{k}") for k in range(2)]
    am3k = [ar.alloc(2 * TCF, BF16, f"am3{k}") for k in range(2)]
    am2k = [ar.alloc(2 * TCF, BF16, f"am2{k}") for k in range(2)]
    b1s = ar.alloc(2 * 512, BF16, "b1s")
    b3s = ar.alloc(2 * 512, BF16, "b3s")
    b2s = ar.alloc(2 * 512, BF16, "b2s")
    a2s0 = ar.alloc(8 * 256, BF16, "a2s0")

    def tilef(a, i):
        return a.ap()[:, i * TCF:(i + 1) * TCF]

    for c in range(NCHF):
        ln2 = ar.alloc(NT * TCF, BF16, "ln2")
        sqf = ar.alloc(8 * TCF, BF16, "sqf")
        x1b = ar.alloc(NT * TCF, BF16, "x1b")
        P.dma("GQ", x1b.ap(), dap("x1_d")[:, c * NT * TCF:(c + 1) * NT * TCF],
              reads=[db("x1_d")], writes=[x1b.buf], nm="ld_x1b")
        lm = ar.alloc(TCF, F32, "lmf")
        lr = ar.alloc(TCF, F32, "lrf")
        ln_stats([tilef(x1b, i) for i in range(NT)], [x1b.buf], sqf, TCF,
                 lm.ap()[0:1, :], lr.ap()[0:1, :], lm.buf, lr.buf, True)
        bm = bcast_row(lm.ap()[0:1, :], lm.buf, TCF)
        br = bcast_row(lr.ap()[0:1, :], lr.buf, TCF)
        for i in range(NT):
            op("vector", lambda v, _d=tilef(ln2, i), _a=tilef(x1b, i),
               _b=bm[0].ap()[:, 0:TCF]: v.tensor_sub(_d, _a, _b),
               reads=[x1b.buf, bm[1]], writes=[ln2.buf], nm="l2a")
            op("vector", lambda v, _d=tilef(ln2, i), _a=tilef(ln2, i),
               _b=br[0].ap()[:, 0:TCF]: v.tensor_mul(_d, _a, _b),
               reads=[ln2.buf, br[1]], writes=[ln2.buf], nm="l2b")
        ar.free(lm)
        ar.free(lr)
        ar.free(x1b)
        ar.free(sqf)
        for tt in range(2):
            bank = next_bank()
            pairs = []
            for kt in range(NT):
                l = ln2.ap()[:, kt * TCF + tt * 128:kt * TCF + tt * 128 + 128]
                pairs.append((l, wg_cast.ap()[:, kt * E:(kt + 1) * E]))
            mm_group(bank, bank[0].ap()[:, 0:E], pairs,
                     [ln2.buf, wg_cast.buf], nm="logits")
            lgt = stk.ap()[:, 80:88]
            m1 = stk.ap()[:, 88:89]
            m2 = stk.ap()[:, 72:73]
            msk1 = stk.ap()[:, 0:8]
            msk2 = stk.ap()[:, 32:40]
            l2_ = stk.ap()[:, 40:48]
            dcol = stk.ap()[:, 73:74]
            rw2c = stk.ap()[:, 74:75]
            rw1c = stk.ap()[:, 64:65]
            P.plan("vector", lambda v, _d=lgt, _s=bank[0].ap()[:, 0:E]:
                   v.tensor_copy(_d, _s), reads=[bank[1]], writes=[stk.buf],
                   nm="lgc", selfsync=True)
            P.plan("vector", lambda v, _d=m1, _s=lgt:
                   v.reduce_max(_d, _s, AX),
                   reads=[stk.buf], writes=[stk.buf], nm="m1",
                   selfsync=True)
            P.plan("vector", lambda v, _d=msk1, _s=lgt, _m=m1:
                   v.tensor_single_scalar(_d, _s, _m, OP.is_ge),
                   reads=[stk.buf], writes=[stk.buf], nm="msk1",
                   selfsync=True)
            P.plan("vector", lambda v, _d=l2_, _a=msk1, _b=lgt:
                   v.scalar_tensor_tensor(_d, _a, -1e30, _b, OP.mult,
                                          OP.add),
                   reads=[stk.buf], writes=[stk.buf], nm="l2m",
                   selfsync=True)
            P.plan("vector", lambda v, _d=m2, _s=l2_:
                   v.reduce_max(_d, _s, AX),
                   reads=[stk.buf], writes=[stk.buf], nm="m2",
                   selfsync=True)
            P.plan("vector", lambda v, _d=msk2, _s=l2_, _m=m2:
                   v.tensor_single_scalar(_d, _s, _m, OP.is_ge),
                   reads=[stk.buf], writes=[stk.buf], nm="msk2",
                   selfsync=True)
            P.plan("vector", lambda v, _d=dcol, _a=m2, _b=m1:
                   v.tensor_sub(_d, _a, _b), reads=[stk.buf],
                   writes=[stk.buf], nm="d12", selfsync=True)
            op("scalar", lambda s, _d=rw2c, _s=dcol:
               s.activation(_d, _s, AF.Sigmoid),
               reads=[stk.buf], writes=[stk.buf], nm="sig")
            P.plan("vector", lambda v, _d=rw1c, _s=rw2c:
                   v.tensor_scalar(_d, _s, -1.0, 1.0, OP.mult, OP.add),
                   reads=[stk.buf], writes=[stk.buf], nm="rw1",
                   selfsync=True)
            if debug and c == 0:
                P.dma("AQ", dbg_stk.ap()[:, tt * 96:(tt + 1) * 96],
                      stk.ap()[:, 0:96], reads=[stk.buf],
                      writes=[dbg_stk_b], nm="dstk")
            bank2 = next_bank()
            op("tensor", lambda t, _o=bank2[0].ap()[0:96, 0:128],
               _i=stk.ap()[:, 0:96], _id=idn_sb.ap():
               t.transpose(_o, _i, _id),
               reads=[stk.buf, idn_sb.buf], writes=[bank2[1]], nm="sttr")
            op("vector", lambda v,
               _d=mk_bf.ap()[0:8, tt * 128:(tt + 1) * 128],
               _s=bank2[0].ap()[0:8, 0:128]: v.tensor_copy(_d, _s),
               reads=[bank2[1]], writes=[mk_bf.buf], nm="mk1")
            op("vector", lambda v,
               _d=mk_bf.ap()[0:8, 256 + tt * 128:256 + (tt + 1) * 128],
               _s=bank2[0].ap()[32:40, 0:128]: v.tensor_copy(_d, _s),
               reads=[bank2[1]], writes=[mk_bf.buf], nm="mk2")
            op("vector", lambda v,
               _d=rw_row.ap()[0:1, tt * 128:(tt + 1) * 128],
               _s=bank2[0].ap()[64:65, 0:128]: v.tensor_copy(_d, _s),
               reads=[bank2[1]], writes=[rw_row.buf], nm="rwrow")
        for k in range(2):
            for mt in range(2):
                bank = next_bank()
                mm_group(bank, bank[0].ap()[:, 0:TCF],
                         [(expm_sb.ap()[0:8, mt * 128:(mt + 1) * 128],
                           mk_bf.ap()[0:8, k * 256:k * 256 + TCF])],
                         [expm_sb.buf, mk_bf.buf], nm="mexp")
                op("vector", lambda v,
                   _d=mexp.ap()[:, (k * 2 + mt) * TCF:(k * 2 + mt + 1) * TCF],
                   _s=bank[0].ap()[:, 0:TCF]: v.tensor_copy(_d, _s),
                   reads=[bank[1]], writes=[mexp.buf], nm="mexpe")
        bankr = bcast_row(rw_row.ap()[0:1, 0:TCF], rw_row.buf, TCF)
        op("vector", lambda v, _d=rwb.ap()[:, 0:TCF],
           _s=bankr[0].ap()[:, 0:TCF]: v.tensor_copy(_d, _s),
           reads=[bankr[1]], writes=[rwb.buf], nm="rwbe")
        op("vector", lambda v, _d=rwb.ap()[:, TCF:2 * TCF],
           _s=rwb.ap()[:, 0:TCF]:
           v.tensor_scalar(_d, _s, -1.0, 1.0, OP.mult, OP.add),
           reads=[rwb.buf], writes=[rwb.buf], nm="rwb2")
        if debug and c == 0:
            P.dma("AQ", dbg_ffn.ap()[:, 0:1024], mexp.ap(),
                  reads=[mexp.buf], writes=[dbg_ffn_b], nm="dffn1")
            P.dma("AQ", dbg_ffn.ap()[:, 1024:1536], rwb.ap(),
                  reads=[rwb.buf], writes=[dbg_ffn_b], nm="dffn2")
        aslab = ar.alloc(NT * 256, BF16, "aslab")
        for aname, dsta in (("A1d", a1a), ("A3d", a3a)):
            load_image_slab(aname, aslab, 0, NT * 256, nm="ld_a")
            for mt in range(2):
                bank = next_bank()
                pairs = []
                for kt in range(NT):
                    l = aslab.ap()[:, kt * 256 + mt * 128:
                                   kt * 256 + mt * 128 + 128]
                    pairs.append((l, tilef(ln2, kt)))
                mm_group(bank, bank[0].ap()[:, 0:TCF], pairs,
                         [aslab.buf, ln2.buf], nm="a_all")
                op("vector", lambda v, _d=tilef(dsta, mt),
                   _s=bank[0].ap()[:, 0:TCF]: v.tensor_copy(_d, _s),
                   reads=[bank[1]], writes=[dsta.buf], nm="a_e")
        if debug and c == 0:
            P.dma("AQ", dbg_ffn.ap()[:, 1536:2048],
                  a1a.ap(), reads=[a1a.buf], writes=[dbg_ffn_b], nm="dffn3")
        ar.free(aslab)
        wslab0 = ar.alloc(NT * 512, BF16, "wslab0")
        wslab1 = ar.alloc(NT * 512, BF16, "wslab1")
        base1 = ar.alloc(NFT * TCF, BF16, "base1")
        base3 = ar.alloc(NFT * TCF, BF16, "base3")
        for wname, dstb in (("W1f_d", base1), ("W3f_d", base3)):
            for sl in range(14):
                slab = wslab0 if sl % 2 == 0 else wslab1
                load_image_slab(wname, slab, sl, 8192, nm="ld_wf")
                for mt in range(4):
                    gm = sl * 4 + mt
                    bank = next_bank()
                    pairs = []
                    for kt in range(NT):
                        l = slab.ap()[:, kt * 512 + mt * 128:
                                      kt * 512 + mt * 128 + 128]
                        pairs.append((l, tilef(ln2, kt)))
                    mm_group(bank, bank[0].ap()[:, 0:TCF], pairs,
                             [slab.buf, ln2.buf], nm="baseff")
                    op("scalar", lambda s, _d=tilef(dstb, gm),
                       _s=bank[0].ap()[:, 0:TCF]:
                       s.activation(_d, _s, AF.Copy),
                       reads=[bank[1]], writes=[dstb.buf], nm="basee")
        ar.free(ln2)
        ar.free(wslab0)
        ar.free(wslab1)
        x2k = [ar.alloc(NFT * TCF, BF16, f"x2_{k}") for k in range(2)]
        for k in range(2):
            for mt in range(2):
                mexp_ap = mexp.ap()[:, (k * 2 + mt) * TCF:
                                    (k * 2 + mt + 1) * TCF]
                op("vector", lambda v, _d=tilef(amk[k], mt),
                   _a=tilef(a1a, mt), _b=mexp_ap: v.tensor_mul(_d, _a, _b),
                   reads=[a1a.buf, mexp.buf], writes=[amk[k].buf], nm="am1")
                op("vector", lambda v, _d=tilef(am3k[k], mt),
                   _a=tilef(a3a, mt), _b=mexp_ap: v.tensor_mul(_d, _a, _b),
                   reads=[a3a.buf, mexp.buf], writes=[am3k[k].buf], nm="am3")
        if debug and c == 0:
            P.dma("GQ", dbg_f2.ap()[:, 1536:2048], amk[0].ap()[:, 0:512],
                  reads=[amk[0].buf], writes=[dbg_f2_b], nm="df2d")
        for msl in range(14):
            load_image_slab("B1d", b1s, msl, 1024, nm="ld_b1")
            load_image_slab("B3d", b3s, msl, 1024, nm="ld_b3")
            for mt4 in range(4):
                mt = msl * 4 + mt4
                for k in range(2):
                    bank = next_bank()
                    pairs = []
                    for kt in range(2):
                        l = b1s.ap()[:, kt * 512 + mt4 * 128:
                                     kt * 512 + mt4 * 128 + 128]
                        pairs.append((l, tilef(amk[k], kt)))
                    mm_group(bank, bank[0].ap()[:, 0:TCF], pairs,
                             [b1s.buf, amk[k].buf], nm="lora1")
                    op("vector", lambda v, _d=t_x1.ap()[:, 0:TCF],
                       _a=bank[0].ap()[:, 0:TCF], _b=tilef(base1, mt):
                       v.tensor_add(_d, _a, _b),
                       reads=[bank[1], base1.buf], writes=[t_x1.buf],
                       nm="x1k")
                    op("scalar", lambda s, _d=t_s1.ap()[:, 0:TCF],
                       _s=t_x1.ap()[:, 0:TCF]: s.activation(_d, _s, AF.Silu),
                       reads=[t_x1.buf], writes=[t_s1.buf], nm="silu")
                    bank3 = next_bank()
                    pairs = []
                    for kt in range(2):
                        l = b3s.ap()[:, kt * 512 + mt4 * 128:
                                     kt * 512 + mt4 * 128 + 128]
                        pairs.append((l, tilef(am3k[k], kt)))
                    mm_group(bank3, bank3[0].ap()[:, 0:TCF], pairs,
                             [b3s.buf, am3k[k].buf], nm="lora3")
                    op("vector", lambda v, _d=t_x3.ap()[:, 0:TCF],
                       _a=bank3[0].ap()[:, 0:TCF], _b=tilef(base3, mt):
                       v.tensor_add(_d, _a, _b),
                       reads=[bank3[1], base3.buf], writes=[t_x3.buf],
                       nm="x3k")
                    op("vector", lambda v, _d=tilef(x2k[k], mt),
                       _a=t_s1.ap()[:, 0:TCF], _b=t_x3.ap()[:, 0:TCF]:
                       v.tensor_mul(_d, _a, _b),
                       reads=[t_s1.buf, t_x3.buf], writes=[x2k[k].buf],
                       nm="x2k")
        if debug and c == 0:
            P.dma("GQ", dbg_f2.ap()[:, 0:256], t_x1.ap()[:, 0:TCF],
                  reads=[t_x1.buf], writes=[dbg_f2_b], nm="df3a")
            P.dma("GQ", dbg_f2.ap()[:, 256:512], t_s1.ap()[:, 0:TCF],
                  reads=[t_s1.buf], writes=[dbg_f2_b], nm="df3b")
        ar.free(base1)
        ar.free(base3)
        banks_a2 = [[next_bank() for _ in range(2)] for _ in range(2)]
        for sl in range(7):
            load_image_slab("A2d", a2s0, sl, 2048, nm="ld_a2")
            for k in range(2):
                for mt in range(2):
                    bank = banks_a2[k][mt]
                    pairs = []
                    for kt in range(8):
                        l = a2s0.ap()[:, kt * 256 + mt * 128:
                                      kt * 256 + mt * 128 + 128]
                        pairs.append((l, tilef(x2k[k], sl * 8 + kt)))
                    mm_group(bank, bank[0].ap()[:, 0:TCF], pairs,
                             [a2s0.buf, x2k[k].buf], nm="a2all",
                             start=(sl == 0), stop=(sl == 6))
        for k in range(2):
            for mt in range(2):
                bank = banks_a2[k][mt]
                mexp_ap = mexp.ap()[:, (k * 2 + mt) * TCF:
                                    (k * 2 + mt + 1) * TCF]
                op("vector", lambda v, _d=tilef(am2k[k], mt),
                   _a=bank[0].ap()[:, 0:TCF], _b=mexp_ap:
                   v.tensor_mul(_d, _a, _b),
                   reads=[bank[1], mexp.buf], writes=[am2k[k].buf], nm="am2")
        wslab0 = ar.alloc(NT * 512, BF16, "wslab0")
        wslab1 = ar.alloc(NT * 512, BF16, "wslab1")
        for gm in range(NT):
            slab = wslab0 if gm % 2 == 0 else wslab1
            load_image_slab("W2f_d", slab, gm, 7168, nm="ld_w2f")
            load_image_slab("B2d", b2s, gm, 256, nm="ld_b2")
            for k in range(2):
                bank = next_bank()
                pairs = []
                for kt in range(NFT):
                    l = slab.ap()[:, kt * 128:kt * 128 + 128]
                    pairs.append((l, tilef(x2k[k], kt)))
                for kt in range(2):
                    l = b2s.ap()[:, kt * 128:kt * 128 + 128]
                    pairs.append((l, tilef(am2k[k], kt)))
                mm_group(bank, bank[0].ap()[:, 0:TCF], pairs,
                         [slab.buf, b2s.buf, x2k[k].buf, am2k[k].buf],
                         nm="down")
                rw_ap = rwb.ap()[:, k * TCF:(k + 1) * TCF]
                if k == 0:
                    op("vector", lambda v, _d=tilef(out_acc, gm),
                       _a=bank[0].ap()[:, 0:TCF], _b=rw_ap:
                       v.tensor_mul(_d, _a, _b),
                       reads=[bank[1], rwb.buf], writes=[out_acc.buf],
                       nm="ok0")
                else:
                    op("vector", lambda v, _d=tmpf.ap()[:, 0:TCF],
                       _a=bank[0].ap()[:, 0:TCF], _b=rw_ap:
                       v.tensor_mul(_d, _a, _b),
                       reads=[bank[1], rwb.buf], writes=[tmpf.buf],
                       nm="ok1a")
                    op("vector", lambda v, _d=tilef(out_acc, gm),
                       _a=tilef(out_acc, gm), _b=tmpf.ap()[:, 0:TCF]:
                       v.tensor_add(_d, _a, _b),
                       reads=[out_acc.buf, tmpf.buf],
                       writes=[out_acc.buf], nm="ok1b")
        ar.free(x2k[0])
        ar.free(x2k[1])
        ar.free(wslab0)
        ar.free(wslab1)
        lm = ar.alloc(TCF, F32, "lmf2")
        lr = ar.alloc(TCF, F32, "lrf2")
        sqf = ar.alloc(8 * TCF, BF16, "sqf2")
        ln_stats([tilef(out_acc, i) for i in range(NT)], [out_acc.buf], sqf,
                 TCF, lm.ap()[0:1, :], lr.ap()[0:1, :], lm.buf, lr.buf, False)
        bm = bcast_row(lm.ap()[0:1, :], lm.buf, TCF)
        br = bcast_row(lr.ap()[0:1, :], lr.buf, TCF)
        xrf = ar.alloc(NT * TCF, F32, "xrf")
        P.dma("SQ", xrf.ap(), dap("x1_d")[:, c * NT * TCF:(c + 1) * NT * TCF],
              reads=[db("x1_d")], writes=[xrf.buf], nm="ld_x1f")
        for i in range(NT):
            op("vector", lambda v, _d=tilef(out_acc, i), _a=tilef(out_acc, i),
               _b=bm[0].ap()[:, 0:TCF]: v.tensor_sub(_d, _a, _b),
               reads=[out_acc.buf, bm[1]], writes=[out_acc.buf], nm="f1")
            op("vector", lambda v, _d=tilef(out_acc, i), _a=tilef(out_acc, i),
               _b=br[0].ap()[:, 0:TCF]: v.tensor_mul(_d, _a, _b),
               reads=[out_acc.buf, br[1]], writes=[out_acc.buf], nm="f2")
            op("vector", lambda v, _d=tilef(out_acc, i), _a=tilef(out_acc, i),
               _b=tilef(xrf, i): v.tensor_add(_d, _a, _b),
               reads=[out_acc.buf, xrf.buf], writes=[out_acc.buf], nm="f3")
        P.dma("AQ", out_h.ap()[:, c * NT * TCF:(c + 1) * NT * TCF],
              out_acc.ap(), reads=[out_acc.buf], writes=[out_b],
              nm="st_out")
        ar.free(lm)
        ar.free(lr)
        ar.free(xrf)
        ar.free(sqf)

    with contextlib.ExitStack() as stack:
        sems = {}
        for name in list(ENGINES) + list(QUEUES):
            sems[name] = stack.enter_context(nc.semaphore(f"sem_{name}"))
        for name in ENGINES:
            sems["vs_" + name] = stack.enter_context(
                nc.semaphore(f"sem_vs_{name}"))
        nonlocal_xch = stack.enter_context(nc.semaphore("sem_xch"))
        globals()["_XCH_SEM"] = nonlocal_xch
        P.finalize_and_emit(sems)
    return nc


def prep_inputs(inputs):
    g = inputs
    x = np.asarray(g["L0_attn_x"], np.float32).reshape(T, D)
    tabs = make_tables()
    W1 = np.asarray(g["L0_attn_W1"], np.float32)
    zero = np.zeros((1, HID), np.float32)
    w2a_slabs = [(kh * 32, 32, mt * 128, 128)
                 for mt in range(32) for kh in range(2)]
    xslabs = [(0, NT, c * TC, TC) for c in range(NCH)]
    com = {
        "Wq_T": np.ascontiguousarray(np.asarray(g["L0_attn_Wq"], np.float32).T),
        "Wk_T": np.ascontiguousarray(np.asarray(g["L0_attn_Wk"], np.float32).T),
        "Wv_T": np.ascontiguousarray(np.asarray(g["L0_attn_Wv"], np.float32).T),
        "Ws_T": np.ascontiguousarray(np.asarray(g["L0_attn_Ws"], np.float32).T),
        "Wt_T": np.ascontiguousarray(np.asarray(g["L0_attn_Wt"], np.float32).T),
        "W1e_a": np.ascontiguousarray(W1[0:NB]),
        "W1e_b": np.ascontiguousarray(np.vstack([zero, W1[2047:1024:-1]])),
        "W1e_c": np.ascontiguousarray(np.vstack([W1[1024:1025],
                                                 W1[2049:3072]])),
        "W1e_d": np.ascontiguousarray(np.vstack([zero, W1[4095:3072:-1]])),
        "W2a": image_tab(np.asarray(g["L0_attn_W2"], np.float32), w2a_slabs),
        "b1t": np.ascontiguousarray(
            np.asarray(g["L0_attn_b1"], np.float32).reshape(NHT, 128).T),
        "b2t": np.ascontiguousarray(
            np.asarray(g["L0_attn_b2"], np.float32).reshape(2 * NT, 128).T),
        "W1f_T": image_tab(np.asarray(g["L0_ffn_W1"], np.float32).T,
                           geom_mslabs(D, FF, 512)),
        "W3f_T": image_tab(np.asarray(g["L0_ffn_W3"], np.float32).T,
                           geom_mslabs(D, FF, 512)),
        "W2f_T": image_tab(np.asarray(g["L0_ffn_W2"], np.float32).T,
                           geom_mslabs(FF, D, 128)),
        "Wg_tiled": np.ascontiguousarray(
            np.asarray(g["L0_ffn_Wg"], np.float32).T.reshape(NT, 128, E)
            .transpose(1, 0, 2).reshape(128, NT * E)),
        "A1p": image_tab(np.asarray(g["L0_ffn_A1"], np.float32)
                         .reshape(ER, D).T, geom_mslabs(D, 256, 256)),
        "A3p": image_tab(np.asarray(g["L0_ffn_A3"], np.float32)
                         .reshape(ER, D).T, geom_mslabs(D, 256, 256)),
        "A2p": image_tab(np.asarray(g["L0_ffn_A2"], np.float32)
                         .reshape(ER, FF).T, geom_kslabs(FF, 256, 8)),
        "B1p": image_tab(np.asarray(g["L0_ffn_B1"], np.float32)
                         .transpose(0, 2, 1).reshape(ER, FF),
                         geom_mslabs(256, FF, 512)),
        "B3p": image_tab(np.asarray(g["L0_ffn_B3"], np.float32)
                         .transpose(0, 2, 1).reshape(ER, FF),
                         geom_mslabs(256, FF, 512)),
        "B2p": image_tab(np.asarray(g["L0_ffn_B2"], np.float32)
                         .transpose(0, 2, 1).reshape(ER, D),
                         geom_mslabs(256, D, 128)),
        "EXPM": expansion_mat(),
        "idn": np.eye(128, dtype=np.float32),
        "flipm": make_flipm(),
    }
    tabs2 = {
        "Ch": image_tab(tabs["Ch"], geom_mslabs(D, NB, 512)),
        "Sm": image_tab(tabs["Sm"], geom_mslabs(D, NB, 512)),

        "IC2": image_tab(tabs["IC2"], geom_mslabs(NB, D, 256)),
        "IS2": image_tab(tabs["IS2"], geom_mslabs(NB, D, 256)),

    }
    com.update(tabs2)
    magic = int(np.random.randint(1, 2**30))
    in_maps = []
    for c in range(NCORES):
        j = c % 4
        m = dict(com)
        m["xT"] = image_tab(
            np.ascontiguousarray(x[c * TLOC:(c + 1) * TLOC].T), xslabs)
        rsrc = c - 2 if j >= 2 else c
        m["xR_T"] = image_tab(
            np.ascontiguousarray(x[rsrc * TLOC:(rsrc + 1) * TLOC].T), xslabs)
        m["cid2"] = np.array([[c % 2, magic]], np.int32)
        w = np.zeros(4, np.float32)
        if j == 1:
            w[0] = 1.0
        elif j == 2:
            w[2] = 1.0
            w[3] = 1.0
        elif j == 3:
            w[0] = 1.0
            w[2] = 1.0
            w[3] = 1.0
        m["exw"] = np.ascontiguousarray(
            np.broadcast_to(w[None, :], (128, 4)).copy())
        in_maps.append(m)
    return in_maps


_prog_cache = {}


def kernel(**inputs):
    in_maps = prep_inputs(inputs)
    if "nc" not in _prog_cache:
        _prog_cache["nc"] = build_program()
    nc = _prog_cache["nc"]
    res = run_bass_kernel_spmd(nc, in_maps, core_ids=list(range(NCORES)))
    shards = []
    for r in res.results:
        img = np.asarray(r["outT"]).reshape(128, NCHF, NT, TCF)
        shards.append(img.transpose(2, 0, 1, 3).reshape(D, TLOC).T)
    out = np.concatenate(shards, axis=0).reshape(B, S, D)
    return np.ascontiguousarray(out.astype(np.float32))


if __name__ == "__main__":
    import time
    t0 = time.time()
    nc = build_program()
    print(f"build {time.time()-t0:.1f}s; "
          f"n_instructions={nc.n_instructions()}")


# version 24
# speedup vs baseline: 1.0950x; 1.0950x over previous
import sys
import contextlib
for _p in ("/opt/trn_rl_repo",):
    if _p not in sys.path:
        sys.path.append(_p)
import numpy as np
import ml_dtypes
import concourse.bass as bass
import concourse.mybir as mybir
from concourse.mybir import AluOpType as OP
from concourse.bass_utils import run_bass_kernel_spmd

F32, BF16 = mybir.dt.float32, mybir.dt.bfloat16
AF = mybir.ActivationFunctionType
BF = ml_dtypes.bfloat16
AX = mybir.AxisListType.X

B, S, D = 2, 8192, 2048
T = B * S
NCORES = 8
TLOC = T // NCORES
NB = D // 2
NT = D // 128
NBT = NB // 128
TC = 512
NCH = TLOC // TC
TCF = 256
NCHF = TLOC // TCF
FF = 7168
NFT = FF // 128
E, R = 8, 32
ER = E * R
HID = 8192
NHT = HID // 128
EPS = 1e-5


def make_tables():
    f = np.arange(D, dtype=np.float64)[:, None]
    bh = np.arange(NB, dtype=np.float64)[None, :]
    bf_ = np.arange(D, dtype=np.float64)[None, :]
    ah = 2 * np.pi * f * bh / D
    af = 2 * np.pi * f * bf_ / D
    Ch = np.cos(ah)
    Sm = -np.sin(ah)
    Sm[:, 0] = np.cos(np.pi * np.arange(D))
    Cf = np.cos(af)
    Sf = -np.sin(af)
    jh = np.arange(NB, dtype=np.float64)[:, None]
    n = np.arange(D, dtype=np.float64)[None, :]
    IC2 = 2 * np.cos(2 * np.pi * jh * n / D) / D
    IC2[0, :] = 1.0 / D
    IS2 = -2 * np.sin(2 * np.pi * jh * n / D) / D
    IS2[0, :] = np.cos(np.pi * np.arange(D)) / D
    jf = np.arange(D, dtype=np.float64)[:, None]
    ICf = np.cos(2 * np.pi * jf * n / D) / D
    ISf = -np.sin(2 * np.pi * jf * n / D) / D
    t = {"Ch": Ch, "Sm": Sm, "Cf": Cf, "Sf": Sf,
         "IC2": IC2, "IS2": IS2, "ICf": ICf, "ISf": ISf}
    return {k: np.ascontiguousarray(v.astype(BF)) for k, v in t.items()}


def make_flipm():
    idh = 0.5 * np.eye(128)
    xf = np.zeros((128, 128))
    for m in range(1, 128):
        xf[128 - m, m] = 0.5
    return np.ascontiguousarray(
        np.concatenate([idh, xf, -xf], axis=1).astype(BF))


def image_tab(tab, slabs):
    cols = sum(nkt * mc for (_, nkt, _, mc) in slabs)
    img = np.empty((128, cols), tab.dtype)
    off = 0
    for (kt0, nkt, m0, mc) in slabs:
        blk = tab[kt0 * 128:(kt0 + nkt) * 128, m0:m0 + mc]
        img[:, off:off + nkt * mc] = (
            blk.reshape(nkt, 128, mc).transpose(1, 0, 2).reshape(128, -1))
        off += nkt * mc
    return np.ascontiguousarray(img)


def geom_mslabs(K, M, mcols):
    return [(0, K // 128, m0, mcols) for m0 in range(0, M, mcols)]


def geom_kslabs(K, M, nkt):
    return [(k0, nkt, 0, M) for k0 in range(0, K // 128, nkt)]


def expansion_mat():
    m = np.zeros((8, 256), np.float32)
    for e in range(8):
        for r_ in range(32):
            m[e, (e % 4) * 32 + r_ + (e // 4) * 128] = 1.0
    return np.ascontiguousarray(m.astype(BF))


ENGINES = ("tensor", "vector", "scalar", "gpsimd", "sync")
QUEUES = {"SQ": "sync", "GQ": "gpsimd", "AQ": "scalar"}


class Buf:
    __slots__ = ("name", "wtok", "rtoks")

    def __init__(self, name):
        self.name = name
        self.wtok = None
        self.rtoks = []


class PlanOp:
    __slots__ = ("engine", "fn", "deps", "queue", "idx", "needs_inc", "nm",
                 "selfsync")

    def __init__(self, engine, fn, deps, queue, nm="", selfsync=False):
        self.engine = engine
        self.fn = fn
        self.deps = deps
        self.queue = queue
        self.idx = None
        self.needs_inc = False
        self.nm = nm
        self.selfsync = selfsync


class Prog:
    def __init__(self, nc):
        self.nc = nc
        self.ops = {e: [] for e in ENGINES}
        self.qcount = {q: 0 for q in QUEUES}

    def plan(self, engine, fn, reads=(), writes=(), queue=None, nm="",
             selfsync=False):
        deps = []
        for b in reads:
            if b.wtok is not None:
                deps.append(b.wtok)
        for b in writes:
            if b.wtok is not None:
                deps.append(b.wtok)
            deps.extend(b.rtoks)
        op_ = PlanOp(engine, fn, deps, queue, nm, selfsync)
        if queue is not None:
            self.qcount[queue] += 1
            op_.idx = self.qcount[queue]
        self.ops[engine].append(op_)
        for b in reads:
            b.rtoks.append(op_)
        for b in writes:
            b.wtok = op_
            b.rtoks = []
        return op_

    def dma(self, queue, dst_ap, src_ap, reads=(), writes=(), nm="", **kw):
        def fn(e, _dst=dst_ap, _src=src_ap, _kw=kw):
            return e.dma_start(_dst, _src, **_kw)
        return self.plan(QUEUES[queue], fn, reads=reads, writes=writes,
                         queue=queue, nm=nm or f"dma_{queue}")

    def finalize_and_emit(self, sems):
        for e in ENGINES:
            for op_ in self.ops[e]:
                for d in op_.deps:
                    if d.queue is None and d.engine != op_.engine:
                        d.needs_inc = True
        ecount = {e: 0 for e in ENGINES}
        for e in ENGINES:
            for op_ in self.ops[e]:
                if op_.queue is None and op_.needs_inc:
                    ecount[e] += 1
                    op_.idx = ecount[e]
        self.final_counts = dict(ecount)
        nc = self.nc

        def emit_stream(eng_obj, ename):
            high = {}
            vs_n = 0
            vs_sem = sems.get("vs_" + ename)
            for op_ in self.ops[ename]:
                for d in op_.deps:
                    if d.queue is not None:
                        sem, val = sems[d.queue], 16 * d.idx
                    elif d.engine != ename:
                        sem, val = sems[d.engine], d.idx
                    else:
                        continue
                    if high.get(sem.name, 0) < val:
                        eng_obj.wait_ge(sem, val)
                        high[sem.name] = val
                inst = op_.fn(eng_obj)
                if op_.queue is not None:
                    inst.then_inc(sems[op_.queue], 16)
                elif op_.selfsync:
                    vs_n += 1
                    inst.then_inc(vs_sem, 1)
                    eng_obj.wait_ge(vs_sem, vs_n)
                    if op_.needs_inc:
                        eng_obj.sem_inc(sems[ename], 1)
                elif op_.needs_inc:
                    inst.then_inc(sems[ename], 1)

        with nc.Block() as block:
            block.tensor(lambda t: emit_stream(t, "tensor"))
            block.vector(lambda v: emit_stream(v, "vector"))
            block.scalar(lambda s: emit_stream(s, "scalar"))
            block.gpsimd(lambda g: emit_stream(g, "gpsimd"))

            def sync_fn(s):
                emit_stream(s, "sync")
                for q in QUEUES:
                    if self.qcount[q]:
                        s.wait_ge(sems[q], 16 * self.qcount[q])
            block.sync(sync_fn)


class ArAlloc:
    __slots__ = ("t", "buf", "off", "size")

    def __init__(self, t, buf, off, size):
        self.t = t
        self.buf = buf
        self.off = off
        self.size = size

    def ap(self):
        return self.t.ap()


class Arena:
    def __init__(self, nc, base, top):
        self.nc = nc
        self.base = base
        self.top = top
        self.free_list = [[base, top, []]]
        self.n = 0
        self.peak = base
        self.active = {}

    def alloc(self, cols, dtype, name):
        esz = 4 if dtype == F32 else 2
        nbytes = (cols * esz + 511) & ~511
        for seg in self.free_list:
            if seg[1] - seg[0] >= nbytes:
                off = seg[0]
                preds = list(seg[2])
                seg[0] += nbytes
                if seg[0] >= seg[1]:
                    self.free_list.remove(seg)
                self.n += 1
                self.peak = max(self.peak, off + nbytes)
                for (ao, asz), anm in self.active.items():
                    assert off + nbytes <= ao or off >= ao + asz, (
                        f"arena overlap: {name}@{off}+{nbytes} vs "
                        f"{anm}@{ao}+{asz}")
                t = self.nc.alloc_sbuf_tensor_at(
                    f"ar_{name}_{self.n}", [128, cols], dtype, offset=off)
                buf = Buf(f"{name}@{off}")
                buf.rtoks = preds
                a = ArAlloc(t, buf, off, nbytes)
                self.active[(off, nbytes)] = name
                return a
        raise MemoryError(
            f"arena OOM for {name} ({nbytes}B); free="
            f"{[(s[1]-s[0]) for s in self.free_list]}")

    def free(self, a):
        assert (a.off, a.size) in self.active, \
            f"double free {a.buf.name}@{a.off}+{a.size}"
        for seg in self.free_list:
            assert a.off + a.size <= seg[0] or a.off >= seg[1], \
                f"free overlap {a.buf.name}@{a.off}+{a.size} vs seg {seg[:2]}"
        self.active.pop((a.off, a.size), None)
        toks = list(a.buf.rtoks)
        if a.buf.wtok is not None:
            toks.append(a.buf.wtok)
        self.free_list.append([a.off, a.off + a.size, toks])
        self.free_list.sort(key=lambda s: s[0])
        merged = []
        for seg in self.free_list:
            if merged and merged[-1][1] == seg[0]:
                merged[-1][1] = seg[1]
                merged[-1][2] = merged[-1][2] + seg[2]
            else:
                merged.append(seg)
        self.free_list = merged


def build_program(debug=False):
    nc = bass.Bass("TRN2", target_bir_lowering=False)
    P = Prog(nc)
    din = {}

    def ein(name, shape, dtype):
        din[name] = (nc.dram_tensor(name, shape, dtype, kind="ExternalInput"),
                     Buf("d_" + name))

    def internal(name, shape, dtype):
        din[name] = (nc.dram_tensor(name, shape, dtype, kind="Internal"),
                     Buf("i_" + name))

    ein("xT", (128, NCH * NT * TC), F32)
    for w in ("Wq_T", "Wk_T", "Wv_T", "Ws_T", "Wt_T"):
        ein(w, (D, D), F32)
    for w in ("W1e_a", "W1e_b", "W1e_c", "W1e_d"):
        ein(w, (NB, HID), F32)
    ein("W2a", (128, 64 * 4096), F32)
    ein("b1t", (128, NHT), F32)
    ein("b2t", (128, 2 * NT), F32)
    ein("W1f_T", (128, 14 * 8192), F32)
    ein("W3f_T", (128, 14 * 8192), F32)
    ein("W2f_T", (128, 16 * 7168), F32)
    ein("Wg_tiled", (128, NT * E), F32)
    ein("A1p", (128, NT * 256), F32)
    ein("A3p", (128, NT * 256), F32)
    ein("A2p", (128, 7 * 2048), F32)
    ein("B1p", (128, 14 * 1024), F32)
    ein("B3p", (128, 14 * 1024), F32)
    ein("B2p", (128, 16 * 256), F32)
    for tname, shp in (("Ch", (128, 2 * 8192)), ("Sm", (128, 2 * 8192)),
                       ("IC2", (128, 8 * 2048)), ("IS2", (128, 8 * 2048))):
        ein(tname, shp, BF16)
    ein("EXPM", (8, 256), BF16)
    ein("idn", (128, 128), F32)
    ein("flipm", (128, 384), BF16)
    ein("xR_T", (128, NCH * NT * TC), F32)
    ein("cid2", (1, 2), mybir.dt.int32)
    ein("exw", (128, 4), F32)

    out_h = nc.dram_tensor("outT", (128, NCHF * NT * TCF), F32,
                           kind="ExternalOutput")
    out_b = Buf("outT")
    if debug:
        dbg_x1 = nc.dram_tensor("dbg_x1", (128, NCHF * NT * TCF), F32,
                                kind="ExternalOutput")
        dbg_x1_b = Buf("dbg_x1")
        dbg_cum = nc.dram_tensor("dbg_cum", (128, NCH * NT * TC), F32,
                                 kind="ExternalOutput")
        dbg_cum_b = Buf("dbg_cum")
        dbg_sm = nc.dram_tensor("dbg_sm", (128, 64), F32,
                                kind="ExternalOutput")
        dbg_sm_b = Buf("dbg_sm")
        dbg_ln1 = nc.dram_tensor("dbg_ln1", (D, TC), F32,
                                 kind="ExternalOutput")
        dbg_ln1_b = Buf("dbg_ln1")
        dbg_q = nc.dram_tensor("dbg_q", (D, TC), F32, kind="ExternalOutput")
        dbg_q_b = Buf("dbg_q")
        dbg_p = nc.dram_tensor("dbg_p", (D, TC), F32, kind="ExternalOutput")
        dbg_p_b = Buf("dbg_p")
        dbg_ffn = nc.dram_tensor("dbg_ffn", (128, 2048), F32,
                                 kind="ExternalOutput")
        dbg_ffn_b = Buf("dbg_ffn")
        dbg_f2 = nc.dram_tensor("dbg_f2", (128, 2048), F32,
                                kind="ExternalOutput")
        dbg_f2_b = Buf("dbg_f2")
        dbg_stk = nc.dram_tensor("dbg_stk", (128, 192), F32,
                                 kind="ExternalOutput")
        dbg_stk_b = Buf("dbg_stk")
        dbg_u = nc.dram_tensor("dbg_u", (128, NT * TC), F32,
                               kind="ExternalOutput")
        dbg_u_b = Buf("dbg_u")
        dbg_hf = nc.dram_tensor("dbg_hf", (128, NT * TC), F32,
                                kind="ExternalOutput")
        dbg_hf_b = Buf("dbg_hf")
        dbg_trf = nc.dram_tensor("dbg_trf", (128, 2 * NT * TC), F32,
                                 kind="ExternalOutput")
        dbg_trf_b = Buf("dbg_trf")
        dbg_t = nc.dram_tensor("dbg_t", (128, 2 * NT * TC), F32,
                               kind="ExternalOutput")
        dbg_t_b = Buf("dbg_t")
        dbg_P = nc.dram_tensor("dbg_P", (128, 2 * NT * TC), F32,
                               kind="ExternalOutput")
        dbg_P_b = Buf("dbg_P")
        dbg_ao = nc.dram_tensor("dbg_ao", (128, NT * TC), F32,
                                kind="ExternalOutput")
        dbg_ao_b = Buf("dbg_ao")
        dbg_w1 = nc.dram_tensor("dbg_w1", (128, 32 * 4096), F32,
                                kind="ExternalOutput")
        dbg_w1_b = Buf("dbg_w1")
        dbg_h1 = nc.dram_tensor("dbg_h1", (128, NHT * TC), F32,
                                kind="ExternalOutput")
        dbg_h1_b = Buf("dbg_h1")

    for w in ("q", "k", "v", "s"):
        internal(f"fold_{w}C", (128, 4 * 4096), BF16)
        internal(f"fold_{w}S", (128, 4 * 4096), BF16)
    internal("fold_tC", (128, 4 * 4096), BF16)
    internal("fold_tS", (128, 4 * 4096), BF16)
    internal("W1eff", (128, 32 * 4096), BF16)
    internal("W2a_d", (128, 64 * 4096), BF16)
    internal("W1f_d", (128, 14 * 8192), BF16)
    internal("W3f_d", (128, 14 * 8192), BF16)
    internal("W2f_d", (128, 16 * 7168), BF16)
    internal("A1d", (128, NT * 256), BF16)
    internal("A3d", (128, NT * 256), BF16)
    internal("A2d", (128, 7 * 2048), BF16)
    internal("B1d", (128, 14 * 1024), BF16)
    internal("B3d", (128, 14 * 1024), BF16)
    internal("B2d", (128, 16 * 256), BF16)
    internal("cum_d", (128, NCH * NT * TC), F32)
    internal("x1_d", (128, NCHF * NT * TCF), F32)
    din["sh_x"] = (nc.dram_tensor("sh_x", (256, 32), F32, kind="Internal",
                                  addr_space="Shared"), Buf("i_sh_x"))
    din["sh_f"] = (nc.dram_tensor("sh_f", (2, 16), mybir.dt.int32,
                                  kind="Internal", addr_space="Shared"),
                   Buf("i_sh_f"))

    def dh(name):
        return din[name][0]

    def db(name):
        return din[name][1]

    def dap(name):
        return din[name][0].ap()

    ar = Arena(nc, (nc.sbuf_base + 511) & ~511, nc.sbuf_top & ~511)
    psum = [(nc.alloc_psum_tensor(f"ps{i}", [128, 512], F32), Buf(f"ps{i}"))
            for i in range(8)]
    ps_rr = [0]

    def next_bank():
        i = ps_rr[0] % 8
        ps_rr[0] += 1
        return psum[i]

    def op(engine, fn, reads=(), writes=(), nm=""):
        return P.plan(engine, fn, reads=reads, writes=writes, nm=nm)

    ones_bf = ar.alloc(16, BF16, "ones_bf")
    ones_f1 = ar.alloc(128, F32, "ones_f1")
    wg_cast = ar.alloc(NT * E, BF16, "wgc")
    b1_sb = ar.alloc(NHT, F32, "b1")
    b2_sb = ar.alloc(2 * NT, F32, "b2")
    cid_sb = ar.alloc(2, mybir.dt.int32, "cid_sb")
    mg_sb = ar.alloc(16, mybir.dt.int32, "mg_sb")
    fl_sb = ar.alloc(32, mybir.dt.int32, "fl_sb")
    ex_sb = ar.alloc(64, F32, "ex_sb")
    pollct = ar.alloc(2, mybir.dt.int32, "pollct")
    exw_sb = ar.alloc(4, F32, "exw_sb")
    idn_sb = ar.alloc(128, F32, "idn")
    flip_sb = ar.alloc(384, BF16, "flipm")
    expm_sb = ar.alloc(256, BF16, "expm")
    stats_m = ar.alloc(TLOC, F32, "stats_m")
    stats_r = ar.alloc(TLOC, F32, "stats_r")
    carry = ar.alloc(16, F32, "carry")
    rcarry = ar.alloc(16, F32, "rcarry")
    basep = ar.alloc(16, F32, "base")
    w1pool = [ar.alloc(512, F32, "w1pa"), ar.alloc(512, F32, "w1pb"),
              ar.alloc(512, BF16, "w1po")]
    sr_sm = ar.alloc(512, F32, "sr_sm")
    sr_s2 = ar.alloc(512, F32, "sr_s2")
    sr_tmp = ar.alloc(512, F32, "sr_tmp")

    P.dma("SQ", b1_sb.ap(), dap("b1t"), reads=[db("b1t")], writes=[b1_sb.buf])
    P.dma("SQ", b2_sb.ap(), dap("b2t"), reads=[db("b2t")], writes=[b2_sb.buf])
    P.dma("SQ", cid_sb.ap()[0:1, :], dap("cid2"), reads=[db("cid2")],
          writes=[cid_sb.buf])
    P.dma("SQ", exw_sb.ap(), dap("exw"), reads=[db("exw")],
          writes=[exw_sb.buf])
    P.dma("SQ", idn_sb.ap(), dap("idn"), reads=[db("idn")],
          writes=[idn_sb.buf])
    P.dma("SQ", flip_sb.ap(), dap("flipm"), reads=[db("flipm")],
          writes=[flip_sb.buf])
    P.dma("SQ", expm_sb.ap()[0:8, :], dap("EXPM"), reads=[db("EXPM")],
          writes=[expm_sb.buf])
    P.dma("GQ", wg_cast.ap(), dap("Wg_tiled"), reads=[db("Wg_tiled")],
          writes=[wg_cast.buf])
    eps_sb = ar.alloc(2, F32, "eps")
    op("vector", lambda v: v.memset(eps_sb.ap()[0:1, 0:1], EPS),
       writes=[eps_sb.buf])
    op("vector", lambda v: v.memset(ones_bf.ap(), 1.0), writes=[ones_bf.buf])
    op("vector", lambda v: v.memset(ones_f1.ap()[0:1, :], 1.0),
       writes=[ones_f1.buf])
    op("vector", lambda v: v.memset(carry.ap(), 0.0), writes=[carry.buf])
    op("vector", lambda v: v.memset(rcarry.ap(), 0.0), writes=[rcarry.buf])
    op("vector", lambda v, _d=None: v.tensor_copy(
        mg_sb.ap()[0:1, :],
        cid_sb.ap()[0:1, 1:2].broadcast_to([1, 16])),
        reads=[cid_sb.buf], writes=[mg_sb.buf], nm="mgrep")

    def mm_group(bank, bank_ap, pairs, reads, nm="mm", start=True, stop=True):
        def fn(t, _pairs=tuple(pairs), _ap=bank_ap, _sa=start, _so=stop):
            n = len(_pairs)
            inst = None
            for i, (l, r_) in enumerate(_pairs):
                inst = t.matmul(_ap, l, r_, start=(_sa and i == 0),
                                stop=(_so and i == n - 1))
            return inst
        return op("tensor", fn, reads=list(reads), writes=[bank[1]], nm=nm)

    def load_image_slab(dramname, dst, s, W, nm="ldw"):
        return P.dma("SQ", dst.ap()[:, 0:W],
                     dap(dramname)[:, s * W:(s + 1) * W],
                     reads=[db(dramname)], writes=[dst.buf], nm=nm)

    def ln_stats(src_tiles, src_bufs, scratch, w, m_ap, r_ap, mbuf, rbuf,
                 src_is_bf16):
        nt = len(src_tiles)
        if src_is_bf16:
            bank = next_bank()
            mm_group(bank, bank[0].ap()[0:1, 0:w],
                     [(ones_bf.ap()[:, 0:1], a) for a in src_tiles],
                     list(src_bufs) + [ones_bf.buf], nm="ln_sum")
        else:
            bank = next_bank()
            for hf in range(2):
                for i in range(nt // 2):
                    g = hf * (nt // 2) + i
                    op("scalar", lambda s,
                       _d=scratch.ap()[:, i * w:(i + 1) * w],
                       _s=src_tiles[g]: s.activation(_d, _s, AF.Copy),
                       reads=list(src_bufs), writes=[scratch.buf],
                       nm="lncast")
                mm_group(bank, bank[0].ap()[0:1, 0:w],
                         [(ones_bf.ap()[:, 0:1],
                           scratch.ap()[:, i * w:(i + 1) * w])
                          for i in range(nt // 2)],
                         [scratch.buf, ones_bf.buf], nm="ln_sum",
                         start=(hf == 0), stop=(hf == 1))
        op("scalar", lambda s, _d=sr_sm.ap()[0:1, 0:w],
           _s=bank[0].ap()[0:1, 0:w]:
           s.activation(_d, _s, AF.Copy, scale=1.0 / D),
           reads=[bank[1]], writes=[sr_sm.buf], nm="ln_m")
        bank2 = next_bank()
        for hf in range(2):
            for i in range(nt // 2):
                g = hf * (nt // 2) + i
                op("scalar", lambda s, _d=scratch.ap()[:, i * w:(i + 1) * w],
                   _s=src_tiles[g]: s.activation(_d, _s, AF.Square),
                   reads=list(src_bufs), writes=[scratch.buf], nm="lnsq")
            mm_group(bank2, bank2[0].ap()[0:1, 0:w],
                     [(ones_bf.ap()[:, 0:1],
                       scratch.ap()[:, i * w:(i + 1) * w])
                      for i in range(nt // 2)],
                     [scratch.buf, ones_bf.buf], nm="ln_sum2",
                     start=(hf == 0), stop=(hf == 1))
        op("scalar", lambda s, _d=sr_s2.ap()[0:1, 0:w],
           _s=bank2[0].ap()[0:1, 0:w]:
           s.activation(_d, _s, AF.Copy, scale=1.0 / D),
           reads=[bank2[1]], writes=[sr_s2.buf], nm="ln_e2")
        op("vector", lambda v, _t=sr_tmp.ap()[0:1, 0:w],
           _m=sr_sm.ap()[0:1, 0:w]: v.tensor_mul(_t, _m, _m),
           reads=[sr_sm.buf], writes=[sr_tmp.buf], nm="ln_msq")
        op("vector", lambda v, _t=sr_tmp.ap()[0:1, 0:w],
           _a=sr_s2.ap()[0:1, 0:w], _b=sr_tmp.ap()[0:1, 0:w]:
           v.tensor_sub(_t, _a, _b),
           reads=[sr_s2.buf, sr_tmp.buf], writes=[sr_tmp.buf], nm="ln_var")
        op("scalar", lambda s, _t=sr_tmp.ap()[0:1, 0:w],
           _e=eps_sb.ap()[0:1, 0:1]:
           s.activation(_t, _t, AF.Sqrt, bias=_e),
           reads=[sr_tmp.buf, eps_sb.buf], writes=[sr_tmp.buf], nm="ln_sqrt")
        P.plan("vector", lambda v, _d=r_ap, _s=sr_tmp.ap()[0:1, 0:w]:
               v.reciprocal(_d, _s), reads=[sr_tmp.buf], writes=[rbuf],
               nm="ln_rs", selfsync=True)
        op("vector", lambda v, _d=m_ap, _s=sr_sm.ap()[0:1, 0:w]:
           v.tensor_copy(_d, _s), reads=[sr_sm.buf], writes=[mbuf], nm="ln_mc")

    def bcast_row(row_ap, rowbuf, w, bf=False):
        bank = next_bank()
        lo = ones_f1
        mm_group(bank, bank[0].ap()[:, 0:w],
                 [(lo.ap()[0:1, 0:128], row_ap)],
                 [rowbuf, lo.buf], nm="bcast")
        return bank

    bcast_rr = [0]

    def bcast_row_sb(row_ap, rowbuf, w):
        bank = bcast_row(row_ap, rowbuf, w)
        dst = sr_tmp if bcast_rr[0] % 2 == 0 else sr_sm
        bcast_rr[0] += 1
        op("scalar", lambda s, _d=dst.ap()[:, 0:w],
           _s=bank[0].ap()[:, 0:w]: s.activation(_d, _s, AF.Copy),
           reads=[bank[1]], writes=[dst.buf], nm="bcastc")
        return (dst, dst.buf)

    wt_sb = ar.alloc(NT * D, BF16, "wt_sb")
    tab_sb = ar.alloc(NT * D, BF16, "tab_sb")
    stg = ar.alloc(2048, BF16, "stage")

    def fold_one(wname, tname, outname, ncols, reload_w):
        if reload_w:
            src = dap(wname).rearrange("(kt p) m -> p kt m", p=128)
            P.dma("GQ", wt_sb.ap().rearrange("p (kt m) -> p kt m", m=D), src,
                  reads=[db(wname)], writes=[wt_sb.buf], nm=f"cWT_{wname}")
        P.dma("SQ", tab_sb.ap()[:, 0:NT * ncols], dap(tname),
              reads=[db(tname)], writes=[tab_sb.buf], nm=f"ldtab_{tname}")
        for mt in range(NT):
            for nch0 in range(0, ncols, 512):
                sl = nch0 // 512
                bank = next_bank()
                pairs = []
                for kt in range(NT):
                    l = wt_sb.ap()[:, kt * D + mt * 128:kt * D + mt * 128 + 128]
                    r_ = tab_sb.ap()[:, sl * 8192 + kt * 512:
                                     sl * 8192 + kt * 512 + 512]
                    pairs.append((l, r_))
                mm_group(bank, bank[0].ap(), pairs,
                         [wt_sb.buf, tab_sb.buf], nm=f"fo_{outname}")
                op("scalar", lambda s, _d=stg.ap()[:, nch0:nch0 + 512],
                   _s=bank[0].ap(): s.activation(_d, _s, AF.Copy),
                   reads=[bank[1]], writes=[stg.buf], nm="fo_e")
            nsl = ncols // 256
            d = dap(outname)[:, 0:nsl * 4096].rearrange(
                "p (s km) -> p s km", km=4096)[:, :, mt * 256:(mt + 1) * 256]
            s_ = stg.ap()[:, 0:ncols].rearrange("p (s m) -> p s m", m=256)
            P.dma("AQ", d, s_, reads=[stg.buf], writes=[db(outname)],
                  nm=f"st_{outname}")

    for w, wn in (("q", "Wq_T"), ("k", "Wk_T"), ("v", "Wv_T"), ("s", "Ws_T")):
        fold_one(wn, "Ch", f"fold_{w}C", NB, True)
        fold_one(wn, "Sm", f"fold_{w}S", NB, False)
    fold_one("Wt_T", "Ch", "fold_tC", NB, True)
    fold_one("Wt_T", "Sm", "fold_tS", NB, False)
    ar.free(wt_sb)
    ar.free(tab_sb)
    ar.free(stg)

    DEFER = []
    DEFER2 = []

    def drain_defer(n, lst=None):
        lst = DEFER if lst is None else lst
        for _ in range(n):
            if lst:
                lst.pop(0)()

    def emit_w1eff_fold():
        QW = 512
        pa, pb, po = w1pool
        for grp, (na, nb_, sgn) in enumerate((("W1e_a", "W1e_b", OP.add),
                                              ("W1e_c", "W1e_d",
                                               OP.subtract))):
            for kt in range(NBT):
                for hh in range(HID // QW):

                    def piece(na=na, nb_=nb_, sgn=sgn, kt=kt, hh=hh,
                              grp=grp):
                        P.dma("SQ", pa.ap(),
                              dap(na)[kt * 128:(kt + 1) * 128,
                                      hh * QW:(hh + 1) * QW],
                              reads=[db(na)], writes=[pa.buf], nm="ld_w1a")
                        P.dma("SQ", pb.ap(),
                              dap(nb_)[kt * 128:(kt + 1) * 128,
                                       hh * QW:(hh + 1) * QW],
                              reads=[db(nb_)], writes=[pb.buf], nm="ld_w1b")
                        op("vector", lambda v, _o=po.ap(), _a=pa.ap(),
                           _b=pb.ap(), _s=sgn:
                           v.tensor_tensor(_o, _a, _b, _s),
                           reads=[pa.buf, pb.buf], writes=[po.buf],
                           nm="w1fold")
                        ktg = grp * NBT + kt
                        d = dap("W1eff").rearrange(
                            "p (s km) -> p s km", km=4096)[
                            :, hh * (QW // 256):(hh + 1) * (QW // 256),
                            ktg * 256:(ktg + 1) * 256]
                        P.dma("AQ", d,
                              po.ap().rearrange("p (s m) -> p s m", m=256),
                              reads=[po.buf], writes=[db("W1eff")],
                              nm="st_w1eff")
                    DEFER.append(piece)

    def emit_converts():
        for s_, d_ in (("W2a", "W2a_d"), ("W1f_T", "W1f_d"),
                       ("W3f_T", "W3f_d"), ("W2f_T", "W2f_d"),
                       ("A1p", "A1d"), ("A3p", "A3d"), ("A2p", "A2d"),
                       ("B1p", "B1d"), ("B3p", "B3d"), ("B2p", "B2d")):
            C = dh(s_).shape[1]
            step = 32768
            for c0 in range(0, C, step):
                w = min(step, C - c0)

                def piece(s_=s_, d_=d_, c0=c0, w=w):
                    P.dma("GQ", dap(d_)[:, c0:c0 + w], dap(s_)[:, c0:c0 + w],
                          reads=[db(s_)], writes=[db(d_)], nm=f"cv_{s_}")
                (DEFER if s_ == "W2a" else DEFER2).append(piece)

    ln1 = ar.alloc(NT * TC, BF16, "ln1")
    slab0 = ar.alloc(NT * 256, BF16, "slab0")
    slab1 = ar.alloc(NT * 256, BF16, "slab1")
    onesTC = ar.alloc(TC, F32, "onesTC")
    tmp1 = ar.alloc(TC, F32, "tmp1")
    tmp2 = ar.alloc(TC, F32, "tmp2")
    tmp34 = [None, None]
    op("vector", lambda v: v.memset(onesTC.ap(), 1.0), writes=[onesTC.buf])

    def tile(a, i, w=TC):
        return a.ap()[:, i * w:(i + 1) * w]

    xbf_bufs = [ar.alloc(NT * TC, BF16, "x_bf0")]
    xseq = ([("xR_T", c) for c in range(NCH)] +
            [("xT", c) for c in range(NCH)] +
            [("xT", c) for c in range(NCH)])
    xseq_i = [0]

    def issue_xload(i):
        if i >= len(xseq):
            return
        xsrc, c = xseq[i]
        xb = xbf_bufs[0]
        P.dma("GQ", xb.ap(), dap(xsrc)[:, c * NT * TC:(c + 1) * NT * TC],
              reads=[db(xsrc)], writes=[xb.buf], nm="ld_xbf")

    issue_xload(0)

    def compute_ln1(c, want_stats, sq_scratch, xsrc="xT"):
        i = xseq_i[0]
        assert xseq[i] == (xsrc, c), (xseq[i], xsrc, c)
        xseq_i[0] += 1
        x_bf = xbf_bufs[0]
        m_ap = stats_m.ap()[0:1, c * TC:(c + 1) * TC]
        r_ap = stats_r.ap()[0:1, c * TC:(c + 1) * TC]
        if want_stats:
            ln_stats([tile(x_bf, i2) for i2 in range(NT)], [x_bf.buf],
                     sq_scratch, TC, m_ap, r_ap, stats_m.buf, stats_r.buf,
                     True)
        bm = bcast_row_sb(m_ap, stats_m.buf, TC)
        br = bcast_row_sb(r_ap, stats_r.buf, TC)
        lnv = ln1.ap().rearrange("p (t m) -> p t m", m=TC)
        xv = x_bf.ap().rearrange("p (t m) -> p t m", m=TC)
        op("vector", lambda v, _d=lnv, _a=xv,
           _b=bm[0].ap()[:, 0:TC].unsqueeze(1).broadcast_to([128, NT, TC]):
           v.tensor_sub(_d, _a, _b),
           reads=[x_bf.buf, bm[1]], writes=[ln1.buf], nm="ln1a")
        op("vector", lambda v, _d=lnv, _a=lnv,
           _b=br[0].ap()[:, 0:TC].unsqueeze(1).broadcast_to([128, NT, TC]):
           v.tensor_mul(_d, _a, _b),
           reads=[ln1.buf, br[1]], writes=[ln1.buf], nm="ln1b")
        issue_xload(i + 1)

    def proj_plane(foldname, outplane, nm):
        for q4 in range(4):
            slab = slab0 if q4 % 2 == 0 else slab1
            load_image_slab(foldname, slab, q4, 4096, nm=f"l{nm}")
            for mt in range(2):
                bank = next_bank()
                pairs = []
                for kt in range(NT):
                    l = slab.ap()[:, kt * 256 + mt * 128:
                                  kt * 256 + mt * 128 + 128]
                    pairs.append((l, tile(ln1, kt)))
                mm_group(bank, bank[0].ap(), pairs, [slab.buf, ln1.buf], nm=nm)
                gi = q4 * 2 + mt
                op("scalar", lambda s, _d=tile(outplane, gi),
                   _s=bank[0].ap(): s.activation(_d, _s, AF.Copy),
                   reads=[bank[1]], writes=[outplane.buf], nm=nm + "e")

    def pk_mul_planes(aR, aI, bR, bI, oR, oI, conj, nm):
        s1 = OP.add if conj else OP.subtract
        s2 = OP.subtract if conj else OP.add
        for i in range(NBT):
            op("vector", lambda v, _d=tmp1.ap()[:, 0:TC], _a=tile(aR, i),
               _b=tile(bR, i): v.tensor_mul(_d, _a, _b),
               reads=[aR.buf, bR.buf], writes=[tmp1.buf], nm=nm + "1")
            op("vector", lambda v, _d=tmp2.ap()[:, 0:TC], _a=tile(aI, i),
               _b=tile(bI, i): v.tensor_mul(_d, _a, _b),
               reads=[aI.buf, bI.buf], writes=[tmp2.buf], nm=nm + "2")
            op("vector", lambda v, _d=tile(oR, i), _a=tmp1.ap()[:, 0:TC],
               _b=tmp2.ap()[:, 0:TC], _s=s1: v.tensor_tensor(_d, _a, _b, _s),
               reads=[tmp1.buf, tmp2.buf], writes=[oR.buf], nm=nm + "3")
            if i == 0:
                op("vector", lambda v, _d=tile(oR, 0)[0:1, :],
                   _s=tmp1.ap()[0:1, 0:TC]: v.tensor_copy(_d, _s),
                   reads=[tmp1.buf], writes=[oR.buf], nm=nm + "f1")
                op("vector", lambda v, _d=tmp1.ap()[0:1, 0:TC],
                   _s=tmp2.ap()[0:1, 0:TC]: v.tensor_copy(_d, _s),
                   reads=[tmp2.buf], writes=[tmp1.buf], nm=nm + "f2")
            op("vector", lambda v, _d=tmp2.ap()[:, 0:TC], _a=tile(aR, i),
               _b=tile(bI, i): v.tensor_mul(_d, _a, _b),
               reads=[aR.buf, bI.buf], writes=[tmp2.buf], nm=nm + "4")
            op("vector", lambda v, _d=tile(oI, i), _a=tile(aI, i),
               _b=tile(bR, i): v.tensor_mul(_d, _a, _b),
               reads=[aI.buf, bR.buf], writes=[oI.buf], nm=nm + "5")
            op("vector", lambda v, _d=tile(oI, i), _a=tmp2.ap()[:, 0:TC],
               _b=tile(oI, i), _s=s2: v.tensor_tensor(_d, _a, _b, _s),
               reads=[tmp2.buf, oI.buf], writes=[oI.buf], nm=nm + "6")
            if i == 0:
                op("vector", lambda v, _d=tile(oI, 0)[0:1, :],
                   _s=tmp1.ap()[0:1, 0:TC]: v.tensor_copy(_d, _s),
                   reads=[tmp1.buf], writes=[oI.buf], nm=nm + "f3")

    sq_g = ar.alloc(8 * TC, BF16, "sq_g")
    plA = ar.alloc(NBT * TC, F32, "plA")
    plB = ar.alloc(NBT * TC, F32, "plB")
    plC = ar.alloc(NBT * TC, F32, "plC")
    plD = ar.alloc(NBT * TC, F32, "plD")
    plE = ar.alloc(NBT * TC, F32, "plE")
    plF = ar.alloc(NBT * TC, F32, "plF")

    def p1_chunk(c, xsrc, carrybuf, full, tap=False):
        compute_ln1(c, True, sq_g, xsrc)
        if tap:
            P.dma("GQ", dbg_ln1.ap().rearrange("(kt p) m -> p kt m", p=128),
                  ln1.ap().rearrange("p (kt m) -> p kt m", m=TC),
                  reads=[ln1.buf], writes=[dbg_ln1_b], nm="tap_ln1")
        proj_plane("fold_qC", plA, "pqC")
        proj_plane("fold_qS", plB, "pqS")
        proj_plane("fold_kC", plC, "pkC")
        proj_plane("fold_kS", plD, "pkS")
        if tap:
            for half, pl in ((0, plA), (1, plB)):
                P.dma("AQ", dbg_q.ap()[half * NB:(half + 1) * NB, :]
                      .rearrange("(kt p) m -> p kt m", p=128),
                      pl.ap().rearrange("p (kt m) -> p kt m", m=TC),
                      reads=[pl.buf], writes=[dbg_q_b], nm="tap_q")
        pk_mul_planes(plA, plB, plC, plD, plE, plF, True, "m")
        proj_plane("fold_vC", plA, "pvC")
        proj_plane("fold_vS", plB, "pvS")
        pk_mul_planes(plE, plF, plA, plB, plC, plD, False, "p")
        if tap:
            for half, pl in ((0, plC), (1, plD)):
                P.dma("AQ", dbg_p.ap()[half * NB:(half + 1) * NB, :]
                      .rearrange("(kt p) m -> p kt m", p=128),
                      pl.ap().rearrange("p (kt m) -> p kt m", m=TC),
                      reads=[pl.buf], writes=[dbg_p_b], nm="tap_p")
        for half, (src, dstp) in enumerate(((plC, plE), (plD, plF))):
            for i in range(NBT):
                ci = half * 8 + i
                P.plan("vector", lambda v, _d=tile(dstp, i), _o=onesTC.ap(),
                       _s=tile(src, i), _in=carrybuf.ap()[:, ci:ci + 1]:
                       v.tensor_tensor_scan(_d, _o, _s, _in, OP.mult, OP.add),
                       reads=[src.buf, carrybuf.buf, onesTC.buf],
                       writes=[dstp.buf], nm="scan", selfsync=True)
                P.plan("vector", lambda v, _d=carrybuf.ap()[:, ci:ci + 1],
                       _s=tile(dstp, i)[:, TC - 1:TC]: v.tensor_copy(_d, _s),
                       reads=[dstp.buf], writes=[carrybuf.buf], nm="carry",
                       selfsync=True)
        if not full:
            return
        for tabname, dstp in (("Ch", plE), ("Sm", plF)):
            for s8 in range(2):
                P.dma("SQ", slab0.ap()[:, 0:4096],
                      dap(tabname)[:, s8 * 8192:s8 * 8192 + 4096],
                      reads=[db(tabname)], writes=[slab0.buf], nm="lfft0")
                P.dma("SQ", slab1.ap()[:, 0:4096],
                      dap(tabname)[:, s8 * 8192 + 4096:(s8 + 1) * 8192],
                      reads=[db(tabname)], writes=[slab1.buf], nm="lfft1")
                for mt in range(4):
                    bank = next_bank()
                    pairs = []
                    for kt in range(NT):
                        sb = slab0 if kt < 8 else slab1
                        l = sb.ap()[:, (kt % 8) * 512 + mt * 128:
                                    (kt % 8) * 512 + mt * 128 + 128]
                        pairs.append((l, tile(ln1, kt)))
                    mm_group(bank, bank[0].ap(), pairs,
                             [slab0.buf, slab1.buf, ln1.buf], nm="fft")
                    gi = s8 * 4 + mt
                    op("vector", lambda v, _d=tile(dstp, gi),
                       _a=bank[0].ap(), _b=tile(dstp, gi):
                       v.tensor_add(_d, _a, _b),
                       reads=[bank[1], dstp.buf], writes=[dstp.buf],
                       nm="fftadd")
        for half, pl in ((0, plE), (1, plF)):
            d = dap("cum_d")[:, c * NT * TC + half * NBT * TC:
                             c * NT * TC + (half + 1) * NBT * TC]
            P.dma("AQ", d, pl.ap(), reads=[pl.buf], writes=[db("cum_d")],
                  nm="st_cum")

    emit_w1eff_fold()
    emit_converts()
    NDEF0 = len(DEFER)
    NDEF2 = len(DEFER2)

    for c in range(NCH):
        p1_chunk(c, "xR_T", rcarry, False)
        drain_defer(NDEF0 // 9 + 1)
    for c in range(NCH):
        p1_chunk(c, "xT", carry, True, tap=(debug and c == 0))
        drain_defer(NDEF0 // 9 + 1)

    ar.free(sq_g)
    ar.free(plA)
    ar.free(plB)
    ar.free(plC)
    ar.free(plD)
    ar.free(plE)
    ar.free(plF)
    tmp3 = ar.alloc(TC, F32, "tmp3")
    tmp4 = ar.alloc(TC, F32, "tmp4")

    def xch_fn(g):
        xch_sem = globals()["_XCH_SEM"]
        cval = g.value_load(cid_sb.ap()[0:1, 0:1])
        sh = dap("sh_x").rearrange("(c p) m -> c p m", p=128)
        for i in g.Switch(cval, 2):
            g.dma_start(sh[i][:, 0:16], carry.ap()).then_inc(xch_sem, 16)
        g.wait_ge(xch_sem, 16)
        for i in g.Switch(cval, 2):
            g.dma_start(sh[i][:, 16:32], rcarry.ap()).then_inc(xch_sem, 16)
        g.wait_ge(xch_sem, 32)
        for i in g.Switch(cval, 2):
            g.dma_start(dap("sh_f")[i:i + 1, :],
                        mg_sb.ap()[0:1, :]).then_inc(xch_sem, 16)
        g.wait_ge(xch_sem, 48)
        with g.register("nd") as nd, g.register("cnt") as cnt, \
                g.register("acc") as acc:
            mgv = g.value_load(cid_sb.ap()[0:1, 1:2])
            g.reg_mov(nd, 1)
            g.reg_mov(cnt, 48)
            with g.While(nd):
                g.reg_add(cnt, cnt, 16)
                g.dma_start(fl_sb.ap()[0:1, :],
                            dap("sh_f").rearrange("a b -> (a b)")
                            .rearrange("(one f) -> one f", one=1)
                            ).then_inc(xch_sem, 16)
                g.wait_ge(xch_sem, cnt)
                g.reg_mov(acc, 0)
                with g.register("ecmp") as e:
                    for r_ in range(2):
                        v = g.value_load(
                            fl_sb.ap()[0:1, r_ * 16:r_ * 16 + 1])
                        g.reg_alu(e, v, mgv, mybir.AluOpType.is_equal)
                        g.reg_add(acc, acc, e)
                g.reg_alu(nd, acc, 2, mybir.AluOpType.not_equal)
            g.reg_add(cnt, cnt, 16)
            g.dma_start(ex_sb.ap().rearrange("p (c m) -> p c m", m=32),
                        dap("sh_x").rearrange("(c p) m -> p c m", p=128)
                        ).then_inc(xch_sem, 16)
            g.wait_ge(xch_sem, cnt)
            g.reg_save(pollct.ap()[0:1, 0:1], cnt)
        return g.nop()
    P.plan("gpsimd", xch_fn, reads=[carry.buf, rcarry.buf, cid_sb.buf,
                                    mg_sb.buf, db("sh_x"), db("sh_f")],
           writes=[ex_sb.buf, fl_sb.buf, pollct.buf], nm="exchange")
    op("vector", lambda v: v.memset(basep.ap(), 0.0), writes=[basep.buf])
    for s_ in range(2):
        op("vector", lambda v, _d=basep.ap(),
           _a=ex_sb.ap()[:, s_ * 32:s_ * 32 + 16],
           _m=exw_sb.ap()[:, s_:s_ + 1], _b=basep.ap():
           v.scalar_tensor_tensor(_d, _a, _m, _b, OP.mult, OP.add),
           reads=[ex_sb.buf, exw_sb.buf, basep.buf], writes=[basep.buf],
           nm="baseT")
        op("vector", lambda v, _d=basep.ap(),
           _a=ex_sb.ap()[:, s_ * 32 + 16:s_ * 32 + 32],
           _m=exw_sb.ap()[:, 2 + s_:3 + s_], _b=basep.ap():
           v.scalar_tensor_tensor(_d, _a, _m, _b, OP.mult, OP.add),
           reads=[ex_sb.buf, exw_sb.buf, basep.buf], writes=[basep.buf],
           nm="baseR")

    if debug:
        dsc = ar.alloc(64, F32, "dbgsm")
        op("vector", lambda v, _d=dsc.ap()[:, 0:16], _s=carry.ap():
           v.tensor_copy(_d, _s), reads=[carry.buf], writes=[dsc.buf])
        op("vector", lambda v, _d=dsc.ap()[:, 16:32], _s=rcarry.ap():
           v.tensor_copy(_d, _s), reads=[rcarry.buf], writes=[dsc.buf])
        op("vector", lambda v, _d=dsc.ap()[:, 32:48], _s=basep.ap():
           v.tensor_copy(_d, _s), reads=[basep.buf], writes=[dsc.buf])
        op("vector", lambda v, _d=dsc.ap()[:, 48:63],
           _s=ex_sb.ap()[:, 0:15]: v.tensor_copy(_d, _s),
           reads=[ex_sb.buf], writes=[dsc.buf])
        op("vector", lambda v, _d=dsc.ap()[0:1, 63:64],
           _s=pollct.ap()[0:1, 0:1].bitcast(F32) if False else
           pollct.ap()[0:1, 0:1]: v.tensor_copy(_d, _s),
           reads=[pollct.buf], writes=[dsc.buf])
        P.dma("AQ", dbg_sm.ap(), dsc.ap(), reads=[dsc.buf],
              writes=[dbg_sm_b], nm="st_dbgsm")
        ar.free(dsc)
        dbb = ar.alloc(4096, F32, "dbgb")
        for c in range(NCH * NT * TC // 4096):
            P.dma("SQ", dbb.ap(), dap("cum_d")[:, c * 4096:(c + 1) * 4096],
                  reads=[db("cum_d")], writes=[dbb.buf], nm="dbgld")
            P.dma("AQ", dbg_cum.ap()[:, c * 4096:(c + 1) * 4096], dbb.ap(),
                  reads=[dbb.buf], writes=[dbg_cum_b], nm="dbgst")
        ar.free(dbb)


    for c in range(NCH):
        compute_ln1(c, False, None)
        if c == 0:
            drain_defer(len(DEFER))
            if debug:
                for j in range(4):
                    P.dma("GQ", dbg_w1.ap()[:, j * 32768:(j + 1) * 32768],
                          dap("W1eff")[:, j * 32768:(j + 1) * 32768],
                          reads=[db("W1eff")], writes=[dbg_w1_b],
                          nm="tap_w1")
        else:
            drain_defer(NDEF2 // 3 + 1, DEFER2)
        pC = ar.alloc(NBT * TC, F32, "pC")
        pD = ar.alloc(NBT * TC, F32, "pD")
        trb = ar.alloc(NT * TC, BF16, "trb")
        pA = ar.alloc(NBT * TC, F32, "pA")
        pB = ar.alloc(NBT * TC, F32, "pB")
        tmp5 = ar.alloc(TC, F32, "tmp5")
        tmp6 = ar.alloc(TC, F32, "tmp6")
        for half, pl in ((0, pA), (1, pB)):
            s = dap("cum_d")[:, c * NT * TC + half * NBT * TC:
                             c * NT * TC + (half + 1) * NBT * TC]
            P.dma("SQ", pl.ap(), s, reads=[db("cum_d")], writes=[pl.buf],
                  nm="ld_cum")
            for i in range(NBT):
                ci = half * 8 + i
                op("vector", lambda v, _d=tile(pl, i), _s=tile(pl, i),
                   _b=basep.ap()[:, ci:ci + 1]:
                   v.tensor_scalar_add(_d, _s, _b),
                   reads=[pl.buf, basep.buf], writes=[pl.buf], nm="addbase")
        proj_plane("fold_sC", pC, "psC")
        proj_plane("fold_sS", pD, "psS")
        for i in range(NBT):
            op("vector", lambda v, _d=tmp1.ap()[:, 0:TC], _a=tile(pC, i),
               _b=tile(pA, i): v.tensor_mul(_d, _a, _b),
               reads=[pC.buf, pA.buf], writes=[tmp1.buf], nm="ta1")
            op("vector", lambda v, _d=tmp2.ap()[:, 0:TC], _a=tile(pD, i),
               _b=tile(pB, i): v.tensor_mul(_d, _a, _b),
               reads=[pD.buf, pB.buf], writes=[tmp2.buf], nm="ta2")
            op("vector", lambda v, _d=tmp5.ap()[:, 0:TC],
               _a=tmp1.ap()[:, 0:TC], _b=tmp2.ap()[:, 0:TC]:
               v.tensor_add(_d, _a, _b),
               reads=[tmp1.buf, tmp2.buf], writes=[tmp5.buf], nm="ta3")
            if i == 0:
                op("vector", lambda v, _d=tmp5.ap()[0:1, 0:TC],
                   _s=tmp1.ap()[0:1, 0:TC]: v.tensor_copy(_d, _s),
                   reads=[tmp1.buf], writes=[tmp5.buf], nm="taf1")
                op("vector", lambda v, _d=tmp1.ap()[0:1, 0:TC],
                   _s=tmp2.ap()[0:1, 0:TC]: v.tensor_copy(_d, _s),
                   reads=[tmp2.buf], writes=[tmp1.buf], nm="taf2")
            op("vector", lambda v, _d=tmp2.ap()[:, 0:TC], _a=tile(pC, i),
               _b=tile(pB, i): v.tensor_mul(_d, _a, _b),
               reads=[pC.buf, pB.buf], writes=[tmp2.buf], nm="ta4")
            op("vector", lambda v, _d=tmp6.ap()[:, 0:TC], _a=tile(pD, i),
               _b=tile(pA, i): v.tensor_mul(_d, _a, _b),
               reads=[pD.buf, pA.buf], writes=[tmp6.buf], nm="ta5")
            op("vector", lambda v, _d=tmp6.ap()[:, 0:TC],
               _a=tmp2.ap()[:, 0:TC], _b=tmp6.ap()[:, 0:TC]:
               v.tensor_sub(_d, _a, _b),
               reads=[tmp2.buf, tmp6.buf], writes=[tmp6.buf], nm="ta6")
            if i == 0:
                op("vector", lambda v, _d=tmp6.ap()[0:1, 0:TC],
                   _s=tmp1.ap()[0:1, 0:TC]: v.tensor_copy(_d, _s),
                   reads=[tmp1.buf], writes=[tmp6.buf], nm="taf3")
            op("scalar", lambda s, _d=tile(trb, i), _s=tmp5.ap()[:, 0:TC]:
               s.copy(_d, _s), reads=[tmp5.buf], writes=[trb.buf], nm="trbR")
            op("scalar", lambda s, _d=tile(trb, 8 + i),
               _s=tmp6.ap()[:, 0:TC]:
               s.copy(_d, _s), reads=[tmp6.buf], writes=[trb.buf], nm="trbI")
            op("vector", lambda v, _d=tmp1.ap()[:, 0:TC], _a=tile(pC, i),
               _b=tmp5.ap()[:, 0:TC]: v.tensor_mul(_d, _a, _b),
               reads=[pC.buf, tmp5.buf], writes=[tmp1.buf], nm="sf1")
            op("vector", lambda v, _d=tmp2.ap()[:, 0:TC], _a=tile(pD, i),
               _b=tmp6.ap()[:, 0:TC]: v.tensor_mul(_d, _a, _b),
               reads=[pD.buf, tmp6.buf], writes=[tmp2.buf], nm="sf2")
            op("vector", lambda v, _d=tmp3.ap()[:, 0:TC], _a=tile(pC, i),
               _b=tmp6.ap()[:, 0:TC]: v.tensor_mul(_d, _a, _b),
               reads=[pC.buf, tmp6.buf], writes=[tmp3.buf], nm="sf3")
            op("vector", lambda v, _d=tmp4.ap()[:, 0:TC], _a=tile(pD, i),
               _b=tmp5.ap()[:, 0:TC]: v.tensor_mul(_d, _a, _b),
               reads=[pD.buf, tmp5.buf], writes=[tmp4.buf], nm="sf4")
            op("vector", lambda v, _d=tmp3.ap()[:, 0:TC],
               _a=tmp3.ap()[:, 0:TC], _b=tmp4.ap()[:, 0:TC]:
               v.tensor_add(_d, _a, _b),
               reads=[tmp3.buf, tmp4.buf], writes=[tmp3.buf], nm="sf5")
            if i == 0:
                op("vector", lambda v, _d=tmp3.ap()[0:1, 0:TC],
                   _s=tmp2.ap()[0:1, 0:TC]: v.tensor_copy(_d, _s),
                   reads=[tmp2.buf], writes=[tmp3.buf], nm="sf6")
            op("vector", lambda v, _d=tmp4.ap()[:, 0:TC],
               _a=tmp1.ap()[:, 0:TC], _b=tmp2.ap()[:, 0:TC]:
               v.tensor_sub(_d, _a, _b),
               reads=[tmp1.buf, tmp2.buf], writes=[tmp4.buf], nm="sf7")
            if i == 0:
                op("vector", lambda v, _d=tmp4.ap()[0:1, 0:TC],
                   _s=tmp1.ap()[0:1, 0:TC]: v.tensor_copy(_d, _s),
                   reads=[tmp1.buf], writes=[tmp4.buf], nm="sf8")
            op("vector", lambda v, _d=tile(pC, i), _a=tile(pA, i),
               _b=tmp4.ap()[:, 0:TC]: v.tensor_sub(_d, _a, _b),
               reads=[pA.buf, tmp4.buf], writes=[pC.buf], nm="hfR")
            op("vector", lambda v, _d=tile(pD, i), _a=tile(pB, i),
               _b=tmp3.ap()[:, 0:TC]: v.tensor_sub(_d, _a, _b),
               reads=[pB.buf, tmp3.buf], writes=[pD.buf], nm="hfI")
        if debug and c == 0:
            P.dma("GQ", dbg_u.ap().rearrange("p (kt m) -> p kt m", m=TC),
                  trb.ap().rearrange("p (kt m) -> p kt m", m=TC),
                  reads=[trb.buf], writes=[dbg_u_b], nm="tap_u")
        ar.free(pA)
        ar.free(pB)
        ar.free(tmp5)
        ar.free(tmp6)
        ar.free(tmp1)
        ar.free(tmp2)
        ar.free(tmp3)
        ar.free(tmp4)
        hfb = ar.alloc(NT * TC, BF16, "hfb")
        for i in range(NBT):
            op("scalar", lambda s, _d=tile(hfb, i), _s=tile(pC, i):
               s.copy(_d, _s), reads=[pC.buf], writes=[hfb.buf], nm="hfbR")
            op("scalar", lambda s, _d=tile(hfb, 8 + i), _s=tile(pD, i):
               s.copy(_d, _s), reads=[pD.buf], writes=[hfb.buf], nm="hfbI")
        ar.free(pC)
        ar.free(pD)
        if debug and c == 0:
            P.dma("GQ", dbg_hf.ap().rearrange("p (kt m) -> p kt m", m=TC),
                  hfb.ap().rearrange("p (kt m) -> p kt m", m=TC),
                  reads=[hfb.buf], writes=[dbg_hf_b], nm="tap_hf")
        h1h = [ar.alloc(NHT * TC // 2, BF16, f"h1h{q}") for q in range(2)]

        def h1t(i):
            return h1h[i // 32].ap()[:, (i % 32) * TC:(i % 32 + 1) * TC]

        def h1b(i):
            return h1h[i // 32].buf

        for sl in range(32):
            slab = slab0 if sl % 2 == 0 else slab1
            load_image_slab("W1eff", slab, sl, 4096, nm="lw1e")
            for mt in range(2):
                gm = sl * 2 + mt
                bank = next_bank()
                pairs = []
                for kt in range(NT):
                    l = slab.ap()[:, kt * 256 + mt * 128:
                                  kt * 256 + mt * 128 + 128]
                    pairs.append((l, tile(trb, kt)))
                mm_group(bank, bank[0].ap(), pairs, [slab.buf, trb.buf],
                         nm="mlp1")
                op("scalar", lambda s, _d=h1t(gm), _s=bank[0].ap(),
                   _b=b1_sb.ap()[:, gm:gm + 1]:
                   s.activation(_d, _s, AF.Relu, bias=_b),
                   reads=[bank[1], b1_sb.buf], writes=[h1b(gm)], nm="relu")
        ar.free(trb)
        if debug and c == 0:
            for q in range(2):
                P.dma("GQ", dbg_h1.ap()[:, q * 16384:(q + 1) * 16384]
                      .rearrange("p (j m) -> p j m", m=TC),
                      h1h[q].ap().rearrange("p (j m) -> p j m", m=TC),
                      reads=[h1h[q].buf], writes=[dbg_h1_b], nm="tap_h1")
        trfq = [ar.alloc(4 * TC, BF16, f"trfq{q}") for q in range(8)]

        def trft(i):
            return trfq[i // 4].ap()[:, (i % 4) * TC:(i % 4 + 1) * TC]

        def trfb(i):
            return trfq[i // 4].buf

        for mt in range(2 * NT):
            bank = next_bank()
            for kh in range(2):
                slab = slab0 if kh == 0 else slab1
                load_image_slab("W2a_d", slab, mt * 2 + kh, 4096, nm="lw2a")
                pairs = []
                for kt in range(32):
                    pairs.append((slab.ap()[:, kt * 128:kt * 128 + 128],
                                  h1t(kh * 32 + kt)))
                mm_group(bank, bank[0].ap(), pairs,
                         [slab.buf, h1h[kh].buf],
                         nm="mlp2", start=(kh == 0), stop=(kh == 1))
            op("scalar", lambda s, _d=trft(mt), _s=bank[0].ap(),
               _b=b2_sb.ap()[:, mt:mt + 1]:
               s.activation(_d, _s, AF.Identity, bias=_b),
               reads=[bank[1], b2_sb.buf], writes=[trfb(mt)], nm="mlp2e")
        if debug and c == 0:
            for q in range(8):
                P.dma("GQ", dbg_trf.ap()[:, q * 4 * TC:(q + 1) * 4 * TC]
                      .rearrange("p (j m) -> p j m", m=TC),
                      trfq[q].ap().rearrange("p (j m) -> p j m", m=TC),
                      reads=[trfq[q].buf], writes=[dbg_trf_b], nm="tap_trf")
        ar.free(h1h[0])
        ar.free(h1h[1])
        tmp1 = ar.alloc(TC, F32, "tmp1")
        tmp2 = ar.alloc(TC, F32, "tmp2")
        tmp3 = ar.alloc(TC, F32, "tmp3")
        tmp4 = ar.alloc(TC, F32, "tmp4")
        tPR = ar.alloc(NBT * TC, F32, "tPR")
        tPI = ar.alloc(NBT * TC, F32, "tPI")
        proj_plane("fold_tC", tPR, "ptC")
        proj_plane("fold_tS", tPI, "ptS")
        if debug and c == 0:
            P.dma("AQ", dbg_t.ap()[:, 0:NBT * TC], tPR.ap(),
                  reads=[tPR.buf], writes=[dbg_t_b], nm="tap_t")
            P.dma("AQ", dbg_t.ap()[:, NBT * TC:NT * TC], tPI.ap(),
                  reads=[tPI.buf], writes=[dbg_t_b], nm="tap_t2")
        for half in range(2):
            xoff = (1 + half) * 128
            for j in range(NBT):
                dtile = half * 16 + j
                ftile = half * 16 + 15 - j
                bank = next_bank()
                mm_group(bank, bank[0].ap(),
                         [(flip_sb.ap()[:, 0:128], trft(dtile)),
                          (flip_sb.ap()[:, xoff:xoff + 128], trft(ftile))],
                         [flip_sb.buf, trfb(dtile), trfb(ftile)], nm="tfold")
                op("scalar", lambda s, _d=trft(dtile), _s=bank[0].ap():
                   s.activation(_d, _s, AF.Copy),
                   reads=[bank[1]], writes=[trfb(dtile)], nm="tfolde")
        op("scalar", lambda s, _d=trft(0)[0:1, :], _s=trft(0)[0:1, :]:
           s.activation(_d, _s, AF.Copy, scale=2.0),
           reads=[trfb(0)], writes=[trfb(0)], nm="tfx0")
        op("scalar", lambda s, _d=trft(16)[0:1, :], _s=trft(8)[0:1, :]:
           s.activation(_d, _s, AF.Copy),
           reads=[trfb(8)], writes=[trfb(16)], nm="tfx1")
        for j in range(1, NBT):
            op("vector", lambda v, _d=trft(j)[0:1, :],
               _a=trft(16 - j)[0:1, :], _b=trft(j)[0:1, :]:
               v.scalar_tensor_tensor(_d, _a, 0.5, _b, OP.mult, OP.add),
               reads=[trfb(16 - j), trfb(j)], writes=[trfb(j)], nm="tfxr")
            op("vector", lambda v, _d=trft(16 + j)[0:1, :],
               _a=trft(32 - j)[0:1, :], _b=trft(16 + j)[0:1, :]:
               v.scalar_tensor_tensor(_d, _a, -0.5, _b, OP.mult, OP.add),
               reads=[trfb(32 - j), trfb(16 + j)], writes=[trfb(16 + j)],
               nm="tfxi")
        for j in range(NBT):
            op("vector", lambda v, _d=tmp1.ap()[:, 0:TC],
               _a=tile(tPR, j), _b=trft(j): v.tensor_mul(_d, _a, _b),
               reads=[tPR.buf, trfb(j)], writes=[tmp1.buf], nm="Q1")
            op("vector", lambda v, _d=tmp2.ap()[:, 0:TC],
               _a=tile(tPI, j), _b=trft(16 + j): v.tensor_mul(_d, _a, _b),
               reads=[tPI.buf, trfb(16 + j)], writes=[tmp2.buf], nm="Q2")
            op("vector", lambda v, _d=tmp3.ap()[:, 0:TC],
               _a=tmp1.ap()[:, 0:TC], _b=tmp2.ap()[:, 0:TC]:
               v.tensor_sub(_d, _a, _b),
               reads=[tmp1.buf, tmp2.buf], writes=[tmp3.buf], nm="Q3")
            if j == 0:
                op("vector", lambda v, _d=tmp3.ap()[0:1, 0:TC],
                   _s=tmp1.ap()[0:1, 0:TC]: v.tensor_copy(_d, _s),
                   reads=[tmp1.buf], writes=[tmp3.buf], nm="Qf1")
            op("vector", lambda v, _d=tmp1.ap()[:, 0:TC],
               _a=tile(tPR, j), _b=trft(16 + j): v.tensor_mul(_d, _a, _b),
               reads=[tPR.buf, trfb(16 + j)], writes=[tmp1.buf], nm="Q4")
            op("vector", lambda v, _d=tmp4.ap()[:, 0:TC],
               _a=tile(tPI, j), _b=trft(j): v.tensor_mul(_d, _a, _b),
               reads=[tPI.buf, trfb(j)], writes=[tmp4.buf], nm="Q5")
            op("vector", lambda v, _d=tmp1.ap()[:, 0:TC],
               _a=tmp1.ap()[:, 0:TC], _b=tmp4.ap()[:, 0:TC]:
               v.tensor_add(_d, _a, _b),
               reads=[tmp1.buf, tmp4.buf], writes=[tmp1.buf], nm="Q6")
            if j == 0:
                op("vector", lambda v, _d=tmp1.ap()[0:1, 0:TC],
                   _s=tmp2.ap()[0:1, 0:TC]: v.tensor_copy(_d, _s),
                   reads=[tmp2.buf], writes=[tmp1.buf], nm="Qf2")
            op("vector", lambda v, _d=tmp3.ap()[:, 0:TC],
               _a=tmp3.ap()[:, 0:TC], _b=tile(hfb, j):
               v.tensor_add(_d, _a, _b),
               reads=[tmp3.buf, hfb.buf], writes=[tmp3.buf], nm="Qh1")
            op("vector", lambda v, _d=tmp1.ap()[:, 0:TC],
               _a=tmp1.ap()[:, 0:TC], _b=tile(hfb, 8 + j):
               v.tensor_add(_d, _a, _b),
               reads=[tmp1.buf, hfb.buf], writes=[tmp1.buf], nm="Qh2")
            op("scalar", lambda s, _d=trft(8 + j), _s=tmp3.ap()[:, 0:TC]:
               s.copy(_d, _s), reads=[tmp3.buf], writes=[trfb(8 + j)],
               nm="Qc1")
            op("scalar", lambda s, _d=trft(24 + j), _s=tmp1.ap()[:, 0:TC]:
               s.copy(_d, _s), reads=[tmp1.buf], writes=[trfb(24 + j)],
               nm="Qc2")
        ar.free(tPR)
        ar.free(tPI)
        ar.free(hfb)
        if debug and c == 0:
            for q in range(8):
                P.dma("GQ", dbg_P.ap()[:, q * 4 * TC:(q + 1) * 4 * TC]
                      .rearrange("p (j m) -> p j m", m=TC),
                      trfq[q].ap().rearrange("p (j m) -> p j m", m=TC),
                      reads=[trfq[q].buf], writes=[dbg_P_b], nm="tap_P")
        attn_o = ar.alloc(NT * TC, F32, "attn_o")
        ic2s = ar.alloc(NBT * 256, BF16, "ic2s")
        is2s = ar.alloc(NBT * 256, BF16, "is2s")
        for msl in range(8):
            load_image_slab("IC2", ic2s, msl, 2048, nm="lic2")
            load_image_slab("IS2", is2s, msl, 2048, nm="lis2")
            for mt in range(2):
                gm = msl * 2 + mt
                bank = next_bank()
                pairs = []
                for kt in range(NBT):
                    pairs.append((ic2s.ap()[:, kt * 256 + mt * 128:
                                            kt * 256 + mt * 128 + 128],
                                  trft(8 + kt)))
                for kt in range(NBT):
                    pairs.append((is2s.ap()[:, kt * 256 + mt * 128:
                                            kt * 256 + mt * 128 + 128],
                                  trft(24 + kt)))
                mm_group(bank, bank[0].ap(), pairs,
                         [ic2s.buf, is2s.buf] + [q.buf for q in trfq],
                         nm="idft")
                op("scalar", lambda s, _d=tile(attn_o, gm), _s=bank[0].ap():
                   s.activation(_d, _s, AF.Copy),
                   reads=[bank[1]], writes=[attn_o.buf], nm="idfte")
        ar.free(ic2s)
        ar.free(is2s)
        for q in trfq:
            ar.free(q)
        if debug and c == 0:
            P.dma("AQ", dbg_ao.ap(), attn_o.ap(), reads=[attn_o.buf],
                  writes=[dbg_ao_b], nm="tap_ao")
        sqb = ar.alloc(8 * TC, BF16, "sqb")
        lm = ar.alloc(TC, F32, "lm2")
        lr = ar.alloc(TC, F32, "lr2")
        ln_stats([tile(attn_o, i) for i in range(NT)], [attn_o.buf], sqb, TC,
                 lm.ap()[0:1, :], lr.ap()[0:1, :], lm.buf, lr.buf, False)
        ar.free(sqb)
        bm = bcast_row_sb(lm.ap()[0:1, :], lm.buf, TC)
        br = bcast_row_sb(lr.ap()[0:1, :], lr.buf, TC)
        xof = ar.alloc(NT * TC, F32, "xof")
        P.dma("SQ", xof.ap(), dap("xT")[:, c * NT * TC:(c + 1) * NT * TC],
              reads=[db("xT")], writes=[xof.buf], nm="ld_xf32")
        for h2_ in range(2):
            aov = attn_o.ap()[:, h2_ * 8 * TC:(h2_ + 1) * 8 * TC].rearrange(
                "p (t m) -> p t m", m=TC)
            xov = xof.ap()[:, h2_ * 8 * TC:(h2_ + 1) * 8 * TC].rearrange(
                "p (t m) -> p t m", m=TC)
            op("vector", lambda v, _d=aov, _a=aov,
               _b=bm[0].ap()[:, 0:TC].unsqueeze(1)
               .broadcast_to([128, NBT, TC]):
               v.tensor_sub(_d, _a, _b),
               reads=[attn_o.buf, bm[1]], writes=[attn_o.buf], nm="r1")
            op("vector", lambda v, _d=aov, _a=aov,
               _b=br[0].ap()[:, 0:TC].unsqueeze(1)
               .broadcast_to([128, NBT, TC]):
               v.tensor_mul(_d, _a, _b),
               reads=[attn_o.buf, br[1]], writes=[attn_o.buf], nm="r2")
            op("vector", lambda v, _d=aov, _a=aov, _b=xov:
               v.tensor_add(_d, _a, _b),
               reads=[attn_o.buf, xof.buf], writes=[attn_o.buf], nm="r3")
        for h2 in range(2):
            cf = 2 * c + h2
            d = dap("x1_d")[:, cf * NT * TCF:(cf + 1) * NT * TCF].rearrange(
                "p (kt m) -> p kt m", m=TCF)
            s2_ = attn_o.ap().rearrange("p (kt m) -> p kt m", m=TC)[
                :, :, h2 * TCF:(h2 + 1) * TCF]
            P.dma("AQ", d, s2_, reads=[attn_o.buf], writes=[db("x1_d")],
                  nm="st_x1")
        ar.free(lm)
        ar.free(lr)
        ar.free(xof)
        ar.free(attn_o)

    if debug:
        dbb2 = ar.alloc(8192, F32, "dbgb2")
        for c in range(NCHF * NT * TCF // 8192):
            P.dma("SQ", dbb2.ap(), dap("x1_d")[:, c * 8192:(c + 1) * 8192],
                  reads=[db("x1_d")], writes=[dbb2.buf], nm="dbgld2")
            P.dma("AQ", dbg_x1.ap()[:, c * 8192:(c + 1) * 8192], dbb2.ap(),
                  reads=[dbb2.buf], writes=[dbg_x1_b], nm="dbgst2")
        ar.free(dbb2)

    drain_defer(len(DEFER))
    drain_defer(len(DEFER2), DEFER2)
    ar.free(w1pool[0])
    ar.free(w1pool[1])
    ar.free(w1pool[2])

    ar.free(ln1)
    ar.free(onesTC)
    ar.free(slab0)
    ar.free(slab1)
    ar.free(tmp1)
    ar.free(tmp2)
    ar.free(tmp3)
    ar.free(tmp4)
    ar.free(stats_m)
    ar.free(stats_r)

    sqf = None
    t_x1 = ar.alloc(TCF, BF16, "t_x1")
    t_s1 = ar.alloc(TCF, BF16, "t_s1")
    t_x3 = ar.alloc(TCF, BF16, "t_x3")
    out_acc = ar.alloc(NT * TCF, F32, "out_acc")
    tmpf = ar.alloc(TCF, F32, "tmpf")
    stk = ar.alloc(96, F32, "stk")
    rw_row = ar.alloc(TCF, F32, "rw_row")
    mk_bf = ar.alloc(2 * 256, BF16, "mk_bf")
    mexp = ar.alloc(4 * TCF, F32, "mexp")
    rwb = ar.alloc(2 * TCF, F32, "rwb")
    a1a = ar.alloc(2 * TCF, F32, "a1a")
    a3a = ar.alloc(2 * TCF, F32, "a3a")
    amk = [ar.alloc(2 * TCF, BF16, f"am{k}") for k in range(2)]
    am3k = [ar.alloc(2 * TCF, BF16, f"am3{k}") for k in range(2)]
    am2k = [ar.alloc(2 * TCF, BF16, f"am2{k}") for k in range(2)]
    b1s = ar.alloc(2 * 512, BF16, "b1s")
    b3s = ar.alloc(2 * 512, BF16, "b3s")
    b2s = ar.alloc(2 * 512, BF16, "b2s")
    a2s0 = ar.alloc(8 * 256, BF16, "a2s0")

    def tilef(a, i):
        return a.ap()[:, i * TCF:(i + 1) * TCF]

    for c in range(NCHF):
        ln2 = ar.alloc(NT * TCF, BF16, "ln2")
        sqf = ar.alloc(8 * TCF, BF16, "sqf")
        x1b = ar.alloc(NT * TCF, BF16, "x1b")
        P.dma("GQ", x1b.ap(), dap("x1_d")[:, c * NT * TCF:(c + 1) * NT * TCF],
              reads=[db("x1_d")], writes=[x1b.buf], nm="ld_x1b")
        lm = ar.alloc(TCF, F32, "lmf")
        lr = ar.alloc(TCF, F32, "lrf")
        ln_stats([tilef(x1b, i) for i in range(NT)], [x1b.buf], sqf, TCF,
                 lm.ap()[0:1, :], lr.ap()[0:1, :], lm.buf, lr.buf, True)
        bm = bcast_row_sb(lm.ap()[0:1, :], lm.buf, TCF)
        br = bcast_row_sb(lr.ap()[0:1, :], lr.buf, TCF)
        l2v = ln2.ap().rearrange("p (t m) -> p t m", m=TCF)
        x1v = x1b.ap().rearrange("p (t m) -> p t m", m=TCF)
        op("vector", lambda v, _d=l2v, _a=x1v,
           _b=bm[0].ap()[:, 0:TCF].unsqueeze(1).broadcast_to([128, NT, TCF]):
           v.tensor_sub(_d, _a, _b),
           reads=[x1b.buf, bm[1]], writes=[ln2.buf], nm="l2a")
        op("vector", lambda v, _d=l2v, _a=l2v,
           _b=br[0].ap()[:, 0:TCF].unsqueeze(1).broadcast_to([128, NT, TCF]):
           v.tensor_mul(_d, _a, _b),
           reads=[ln2.buf, br[1]], writes=[ln2.buf], nm="l2b")
        ar.free(lm)
        ar.free(lr)
        ar.free(x1b)
        ar.free(sqf)
        for tt in range(2):
            bank = next_bank()
            pairs = []
            for kt in range(NT):
                l = ln2.ap()[:, kt * TCF + tt * 128:kt * TCF + tt * 128 + 128]
                pairs.append((l, wg_cast.ap()[:, kt * E:(kt + 1) * E]))
            mm_group(bank, bank[0].ap()[:, 0:E], pairs,
                     [ln2.buf, wg_cast.buf], nm="logits")
            lgt = stk.ap()[:, 80:88]
            m1 = stk.ap()[:, 88:89]
            m2 = stk.ap()[:, 72:73]
            msk1 = stk.ap()[:, 0:8]
            msk2 = stk.ap()[:, 32:40]
            l2_ = stk.ap()[:, 40:48]
            dcol = stk.ap()[:, 73:74]
            rw2c = stk.ap()[:, 74:75]
            rw1c = stk.ap()[:, 64:65]
            P.plan("vector", lambda v, _d=lgt, _s=bank[0].ap()[:, 0:E]:
                   v.tensor_copy(_d, _s), reads=[bank[1]], writes=[stk.buf],
                   nm="lgc", selfsync=True)
            P.plan("vector", lambda v, _d=m1, _s=lgt:
                   v.reduce_max(_d, _s, AX),
                   reads=[stk.buf], writes=[stk.buf], nm="m1",
                   selfsync=True)
            P.plan("vector", lambda v, _d=msk1, _s=lgt, _m=m1:
                   v.tensor_single_scalar(_d, _s, _m, OP.is_ge),
                   reads=[stk.buf], writes=[stk.buf], nm="msk1",
                   selfsync=True)
            P.plan("vector", lambda v, _d=l2_, _a=msk1, _b=lgt:
                   v.scalar_tensor_tensor(_d, _a, -1e30, _b, OP.mult,
                                          OP.add),
                   reads=[stk.buf], writes=[stk.buf], nm="l2m",
                   selfsync=True)
            P.plan("vector", lambda v, _d=m2, _s=l2_:
                   v.reduce_max(_d, _s, AX),
                   reads=[stk.buf], writes=[stk.buf], nm="m2",
                   selfsync=True)
            P.plan("vector", lambda v, _d=msk2, _s=l2_, _m=m2:
                   v.tensor_single_scalar(_d, _s, _m, OP.is_ge),
                   reads=[stk.buf], writes=[stk.buf], nm="msk2",
                   selfsync=True)
            P.plan("vector", lambda v, _d=dcol, _a=m2, _b=m1:
                   v.tensor_sub(_d, _a, _b), reads=[stk.buf],
                   writes=[stk.buf], nm="d12", selfsync=True)
            op("scalar", lambda s, _d=rw2c, _s=dcol:
               s.activation(_d, _s, AF.Sigmoid),
               reads=[stk.buf], writes=[stk.buf], nm="sig")
            P.plan("vector", lambda v, _d=rw1c, _s=rw2c:
                   v.tensor_scalar(_d, _s, -1.0, 1.0, OP.mult, OP.add),
                   reads=[stk.buf], writes=[stk.buf], nm="rw1",
                   selfsync=True)
            if debug and c == 0:
                P.dma("AQ", dbg_stk.ap()[:, tt * 96:(tt + 1) * 96],
                      stk.ap()[:, 0:96], reads=[stk.buf],
                      writes=[dbg_stk_b], nm="dstk")
            bank2 = next_bank()
            op("tensor", lambda t, _o=bank2[0].ap()[0:96, 0:128],
               _i=stk.ap()[:, 0:96], _id=idn_sb.ap():
               t.transpose(_o, _i, _id),
               reads=[stk.buf, idn_sb.buf], writes=[bank2[1]], nm="sttr")
            op("vector", lambda v,
               _d=mk_bf.ap()[0:8, tt * 128:(tt + 1) * 128],
               _s=bank2[0].ap()[0:8, 0:128]: v.tensor_copy(_d, _s),
               reads=[bank2[1]], writes=[mk_bf.buf], nm="mk1")
            op("vector", lambda v,
               _d=mk_bf.ap()[0:8, 256 + tt * 128:256 + (tt + 1) * 128],
               _s=bank2[0].ap()[32:40, 0:128]: v.tensor_copy(_d, _s),
               reads=[bank2[1]], writes=[mk_bf.buf], nm="mk2")
            op("vector", lambda v,
               _d=rw_row.ap()[0:1, tt * 128:(tt + 1) * 128],
               _s=bank2[0].ap()[64:65, 0:128]: v.tensor_copy(_d, _s),
               reads=[bank2[1]], writes=[rw_row.buf], nm="rwrow")
        for k in range(2):
            for mt in range(2):
                bank = next_bank()
                mm_group(bank, bank[0].ap()[:, 0:TCF],
                         [(expm_sb.ap()[0:8, mt * 128:(mt + 1) * 128],
                           mk_bf.ap()[0:8, k * 256:k * 256 + TCF])],
                         [expm_sb.buf, mk_bf.buf], nm="mexp")
                op("vector", lambda v,
                   _d=mexp.ap()[:, (k * 2 + mt) * TCF:(k * 2 + mt + 1) * TCF],
                   _s=bank[0].ap()[:, 0:TCF]: v.tensor_copy(_d, _s),
                   reads=[bank[1]], writes=[mexp.buf], nm="mexpe")
        bankr = bcast_row(rw_row.ap()[0:1, 0:TCF], rw_row.buf, TCF)
        op("vector", lambda v, _d=rwb.ap()[:, 0:TCF],
           _s=bankr[0].ap()[:, 0:TCF]: v.tensor_copy(_d, _s),
           reads=[bankr[1]], writes=[rwb.buf], nm="rwbe")
        op("vector", lambda v, _d=rwb.ap()[:, TCF:2 * TCF],
           _s=rwb.ap()[:, 0:TCF]:
           v.tensor_scalar(_d, _s, -1.0, 1.0, OP.mult, OP.add),
           reads=[rwb.buf], writes=[rwb.buf], nm="rwb2")
        if debug and c == 0:
            P.dma("AQ", dbg_ffn.ap()[:, 0:1024], mexp.ap(),
                  reads=[mexp.buf], writes=[dbg_ffn_b], nm="dffn1")
            P.dma("AQ", dbg_ffn.ap()[:, 1024:1536], rwb.ap(),
                  reads=[rwb.buf], writes=[dbg_ffn_b], nm="dffn2")
        aslab = ar.alloc(NT * 256, BF16, "aslab")
        for aname, dsta in (("A1d", a1a), ("A3d", a3a)):
            load_image_slab(aname, aslab, 0, NT * 256, nm="ld_a")
            for mt in range(2):
                bank = next_bank()
                pairs = []
                for kt in range(NT):
                    l = aslab.ap()[:, kt * 256 + mt * 128:
                                   kt * 256 + mt * 128 + 128]
                    pairs.append((l, tilef(ln2, kt)))
                mm_group(bank, bank[0].ap()[:, 0:TCF], pairs,
                         [aslab.buf, ln2.buf], nm="a_all")
                op("vector", lambda v, _d=tilef(dsta, mt),
                   _s=bank[0].ap()[:, 0:TCF]: v.tensor_copy(_d, _s),
                   reads=[bank[1]], writes=[dsta.buf], nm="a_e")
        if debug and c == 0:
            P.dma("AQ", dbg_ffn.ap()[:, 1536:2048],
                  a1a.ap(), reads=[a1a.buf], writes=[dbg_ffn_b], nm="dffn3")
        ar.free(aslab)
        wslab0 = ar.alloc(NT * 512, BF16, "wslab0")
        wslab1 = ar.alloc(NT * 512, BF16, "wslab1")
        base1 = ar.alloc(NFT * TCF, BF16, "base1")
        base3 = ar.alloc(NFT * TCF, BF16, "base3")
        for wname, dstb in (("W1f_d", base1), ("W3f_d", base3)):
            for sl in range(14):
                slab = wslab0 if sl % 2 == 0 else wslab1
                load_image_slab(wname, slab, sl, 8192, nm="ld_wf")
                for mt in range(4):
                    gm = sl * 4 + mt
                    bank = next_bank()
                    pairs = []
                    for kt in range(NT):
                        l = slab.ap()[:, kt * 512 + mt * 128:
                                      kt * 512 + mt * 128 + 128]
                        pairs.append((l, tilef(ln2, kt)))
                    mm_group(bank, bank[0].ap()[:, 0:TCF], pairs,
                             [slab.buf, ln2.buf], nm="baseff")
                    op("scalar", lambda s, _d=tilef(dstb, gm),
                       _s=bank[0].ap()[:, 0:TCF]:
                       s.activation(_d, _s, AF.Copy),
                       reads=[bank[1]], writes=[dstb.buf], nm="basee")
        ar.free(ln2)
        ar.free(wslab0)
        ar.free(wslab1)
        x2k = [ar.alloc(NFT * TCF, BF16, f"x2_{k}") for k in range(2)]
        for k in range(2):
            for mt in range(2):
                mexp_ap = mexp.ap()[:, (k * 2 + mt) * TCF:
                                    (k * 2 + mt + 1) * TCF]
                op("vector", lambda v, _d=tilef(amk[k], mt),
                   _a=tilef(a1a, mt), _b=mexp_ap: v.tensor_mul(_d, _a, _b),
                   reads=[a1a.buf, mexp.buf], writes=[amk[k].buf], nm="am1")
                op("vector", lambda v, _d=tilef(am3k[k], mt),
                   _a=tilef(a3a, mt), _b=mexp_ap: v.tensor_mul(_d, _a, _b),
                   reads=[a3a.buf, mexp.buf], writes=[am3k[k].buf], nm="am3")
        if debug and c == 0:
            P.dma("GQ", dbg_f2.ap()[:, 1536:2048], amk[0].ap()[:, 0:512],
                  reads=[amk[0].buf], writes=[dbg_f2_b], nm="df2d")
        for msl in range(14):
            load_image_slab("B1d", b1s, msl, 1024, nm="ld_b1")
            load_image_slab("B3d", b3s, msl, 1024, nm="ld_b3")
            for mt4 in range(4):
                mt = msl * 4 + mt4
                for k in range(2):
                    bank = next_bank()
                    pairs = []
                    for kt in range(2):
                        l = b1s.ap()[:, kt * 512 + mt4 * 128:
                                     kt * 512 + mt4 * 128 + 128]
                        pairs.append((l, tilef(amk[k], kt)))
                    mm_group(bank, bank[0].ap()[:, 0:TCF], pairs,
                             [b1s.buf, amk[k].buf], nm="lora1")
                    op("vector", lambda v, _d=t_x1.ap()[:, 0:TCF],
                       _a=bank[0].ap()[:, 0:TCF], _b=tilef(base1, mt):
                       v.tensor_add(_d, _a, _b),
                       reads=[bank[1], base1.buf], writes=[t_x1.buf],
                       nm="x1k")
                    op("scalar", lambda s, _d=t_s1.ap()[:, 0:TCF],
                       _s=t_x1.ap()[:, 0:TCF]: s.activation(_d, _s, AF.Silu),
                       reads=[t_x1.buf], writes=[t_s1.buf], nm="silu")
                    bank3 = next_bank()
                    pairs = []
                    for kt in range(2):
                        l = b3s.ap()[:, kt * 512 + mt4 * 128:
                                     kt * 512 + mt4 * 128 + 128]
                        pairs.append((l, tilef(am3k[k], kt)))
                    mm_group(bank3, bank3[0].ap()[:, 0:TCF], pairs,
                             [b3s.buf, am3k[k].buf], nm="lora3")
                    op("vector", lambda v, _d=t_x3.ap()[:, 0:TCF],
                       _a=bank3[0].ap()[:, 0:TCF], _b=tilef(base3, mt):
                       v.tensor_add(_d, _a, _b),
                       reads=[bank3[1], base3.buf], writes=[t_x3.buf],
                       nm="x3k")
                    op("vector", lambda v, _d=tilef(x2k[k], mt),
                       _a=t_s1.ap()[:, 0:TCF], _b=t_x3.ap()[:, 0:TCF]:
                       v.tensor_mul(_d, _a, _b),
                       reads=[t_s1.buf, t_x3.buf], writes=[x2k[k].buf],
                       nm="x2k")
        if debug and c == 0:
            P.dma("GQ", dbg_f2.ap()[:, 0:256], t_x1.ap()[:, 0:TCF],
                  reads=[t_x1.buf], writes=[dbg_f2_b], nm="df3a")
            P.dma("GQ", dbg_f2.ap()[:, 256:512], t_s1.ap()[:, 0:TCF],
                  reads=[t_s1.buf], writes=[dbg_f2_b], nm="df3b")
        ar.free(base1)
        ar.free(base3)
        banks_a2 = [[next_bank() for _ in range(2)] for _ in range(2)]
        for sl in range(7):
            load_image_slab("A2d", a2s0, sl, 2048, nm="ld_a2")
            for k in range(2):
                for mt in range(2):
                    bank = banks_a2[k][mt]
                    pairs = []
                    for kt in range(8):
                        l = a2s0.ap()[:, kt * 256 + mt * 128:
                                      kt * 256 + mt * 128 + 128]
                        pairs.append((l, tilef(x2k[k], sl * 8 + kt)))
                    mm_group(bank, bank[0].ap()[:, 0:TCF], pairs,
                             [a2s0.buf, x2k[k].buf], nm="a2all",
                             start=(sl == 0), stop=(sl == 6))
        for k in range(2):
            for mt in range(2):
                bank = banks_a2[k][mt]
                mexp_ap = mexp.ap()[:, (k * 2 + mt) * TCF:
                                    (k * 2 + mt + 1) * TCF]
                op("vector", lambda v, _d=tilef(am2k[k], mt),
                   _a=bank[0].ap()[:, 0:TCF], _b=mexp_ap:
                   v.tensor_mul(_d, _a, _b),
                   reads=[bank[1], mexp.buf], writes=[am2k[k].buf], nm="am2")
        wslab0 = ar.alloc(NT * 512, BF16, "wslab0")
        wslab1 = ar.alloc(NT * 512, BF16, "wslab1")
        for gm in range(NT):
            slab = wslab0 if gm % 2 == 0 else wslab1
            load_image_slab("W2f_d", slab, gm, 7168, nm="ld_w2f")
            load_image_slab("B2d", b2s, gm, 256, nm="ld_b2")
            for k in range(2):
                bank = next_bank()
                pairs = []
                for kt in range(NFT):
                    l = slab.ap()[:, kt * 128:kt * 128 + 128]
                    pairs.append((l, tilef(x2k[k], kt)))
                for kt in range(2):
                    l = b2s.ap()[:, kt * 128:kt * 128 + 128]
                    pairs.append((l, tilef(am2k[k], kt)))
                mm_group(bank, bank[0].ap()[:, 0:TCF], pairs,
                         [slab.buf, b2s.buf, x2k[k].buf, am2k[k].buf],
                         nm="down")
                rw_ap = rwb.ap()[:, k * TCF:(k + 1) * TCF]
                if k == 0:
                    op("vector", lambda v, _d=tilef(out_acc, gm),
                       _a=bank[0].ap()[:, 0:TCF], _b=rw_ap:
                       v.tensor_mul(_d, _a, _b),
                       reads=[bank[1], rwb.buf], writes=[out_acc.buf],
                       nm="ok0")
                else:
                    op("vector", lambda v, _d=tmpf.ap()[:, 0:TCF],
                       _a=bank[0].ap()[:, 0:TCF], _b=rw_ap:
                       v.tensor_mul(_d, _a, _b),
                       reads=[bank[1], rwb.buf], writes=[tmpf.buf],
                       nm="ok1a")
                    op("vector", lambda v, _d=tilef(out_acc, gm),
                       _a=tilef(out_acc, gm), _b=tmpf.ap()[:, 0:TCF]:
                       v.tensor_add(_d, _a, _b),
                       reads=[out_acc.buf, tmpf.buf],
                       writes=[out_acc.buf], nm="ok1b")
        ar.free(x2k[0])
        ar.free(x2k[1])
        ar.free(wslab0)
        ar.free(wslab1)
        lm = ar.alloc(TCF, F32, "lmf2")
        lr = ar.alloc(TCF, F32, "lrf2")
        sqf = ar.alloc(8 * TCF, BF16, "sqf2")
        ln_stats([tilef(out_acc, i) for i in range(NT)], [out_acc.buf], sqf,
                 TCF, lm.ap()[0:1, :], lr.ap()[0:1, :], lm.buf, lr.buf, False)
        bm = bcast_row(lm.ap()[0:1, :], lm.buf, TCF)
        br = bcast_row(lr.ap()[0:1, :], lr.buf, TCF)
        xrf = ar.alloc(NT * TCF, F32, "xrf")
        P.dma("SQ", xrf.ap(), dap("x1_d")[:, c * NT * TCF:(c + 1) * NT * TCF],
              reads=[db("x1_d")], writes=[xrf.buf], nm="ld_x1f")
        for i in range(NT):
            op("vector", lambda v, _d=tilef(out_acc, i), _a=tilef(out_acc, i),
               _b=bm[0].ap()[:, 0:TCF]: v.tensor_sub(_d, _a, _b),
               reads=[out_acc.buf, bm[1]], writes=[out_acc.buf], nm="f1")
            op("vector", lambda v, _d=tilef(out_acc, i), _a=tilef(out_acc, i),
               _b=br[0].ap()[:, 0:TCF]: v.tensor_mul(_d, _a, _b),
               reads=[out_acc.buf, br[1]], writes=[out_acc.buf], nm="f2")
            op("vector", lambda v, _d=tilef(out_acc, i), _a=tilef(out_acc, i),
               _b=tilef(xrf, i): v.tensor_add(_d, _a, _b),
               reads=[out_acc.buf, xrf.buf], writes=[out_acc.buf], nm="f3")
        P.dma("AQ", out_h.ap()[:, c * NT * TCF:(c + 1) * NT * TCF],
              out_acc.ap(), reads=[out_acc.buf], writes=[out_b],
              nm="st_out")
        ar.free(lm)
        ar.free(lr)
        ar.free(xrf)
        ar.free(sqf)

    with contextlib.ExitStack() as stack:
        sems = {}
        for name in list(ENGINES) + list(QUEUES):
            sems[name] = stack.enter_context(nc.semaphore(f"sem_{name}"))
        for name in ENGINES:
            sems["vs_" + name] = stack.enter_context(
                nc.semaphore(f"sem_vs_{name}"))
        nonlocal_xch = stack.enter_context(nc.semaphore("sem_xch"))
        globals()["_XCH_SEM"] = nonlocal_xch
        P.finalize_and_emit(sems)
    return nc


def prep_inputs(inputs):
    g = inputs
    x = np.asarray(g["L0_attn_x"], np.float32).reshape(T, D)
    tabs = make_tables()
    W1 = np.asarray(g["L0_attn_W1"], np.float32)
    zero = np.zeros((1, HID), np.float32)
    w2a_slabs = [(kh * 32, 32, mt * 128, 128)
                 for mt in range(32) for kh in range(2)]
    xslabs = [(0, NT, c * TC, TC) for c in range(NCH)]
    com = {
        "Wq_T": np.ascontiguousarray(np.asarray(g["L0_attn_Wq"], np.float32).T),
        "Wk_T": np.ascontiguousarray(np.asarray(g["L0_attn_Wk"], np.float32).T),
        "Wv_T": np.ascontiguousarray(np.asarray(g["L0_attn_Wv"], np.float32).T),
        "Ws_T": np.ascontiguousarray(np.asarray(g["L0_attn_Ws"], np.float32).T),
        "Wt_T": np.ascontiguousarray(np.asarray(g["L0_attn_Wt"], np.float32).T),
        "W1e_a": np.ascontiguousarray(W1[0:NB]),
        "W1e_b": np.ascontiguousarray(np.vstack([zero, W1[2047:1024:-1]])),
        "W1e_c": np.ascontiguousarray(np.vstack([W1[1024:1025],
                                                 W1[2049:3072]])),
        "W1e_d": np.ascontiguousarray(np.vstack([zero, W1[4095:3072:-1]])),
        "W2a": image_tab(np.asarray(g["L0_attn_W2"], np.float32), w2a_slabs),
        "b1t": np.ascontiguousarray(
            np.asarray(g["L0_attn_b1"], np.float32).reshape(NHT, 128).T),
        "b2t": np.ascontiguousarray(
            np.asarray(g["L0_attn_b2"], np.float32).reshape(2 * NT, 128).T),
        "W1f_T": image_tab(np.asarray(g["L0_ffn_W1"], np.float32).T,
                           geom_mslabs(D, FF, 512)),
        "W3f_T": image_tab(np.asarray(g["L0_ffn_W3"], np.float32).T,
                           geom_mslabs(D, FF, 512)),
        "W2f_T": image_tab(np.asarray(g["L0_ffn_W2"], np.float32).T,
                           geom_mslabs(FF, D, 128)),
        "Wg_tiled": np.ascontiguousarray(
            np.asarray(g["L0_ffn_Wg"], np.float32).T.reshape(NT, 128, E)
            .transpose(1, 0, 2).reshape(128, NT * E)),
        "A1p": image_tab(np.asarray(g["L0_ffn_A1"], np.float32)
                         .reshape(ER, D).T, geom_mslabs(D, 256, 256)),
        "A3p": image_tab(np.asarray(g["L0_ffn_A3"], np.float32)
                         .reshape(ER, D).T, geom_mslabs(D, 256, 256)),
        "A2p": image_tab(np.asarray(g["L0_ffn_A2"], np.float32)
                         .reshape(ER, FF).T, geom_kslabs(FF, 256, 8)),
        "B1p": image_tab(np.asarray(g["L0_ffn_B1"], np.float32)
                         .transpose(0, 2, 1).reshape(ER, FF),
                         geom_mslabs(256, FF, 512)),
        "B3p": image_tab(np.asarray(g["L0_ffn_B3"], np.float32)
                         .transpose(0, 2, 1).reshape(ER, FF),
                         geom_mslabs(256, FF, 512)),
        "B2p": image_tab(np.asarray(g["L0_ffn_B2"], np.float32)
                         .transpose(0, 2, 1).reshape(ER, D),
                         geom_mslabs(256, D, 128)),
        "EXPM": expansion_mat(),
        "idn": np.eye(128, dtype=np.float32),
        "flipm": make_flipm(),
    }
    tabs2 = {
        "Ch": image_tab(tabs["Ch"], geom_mslabs(D, NB, 512)),
        "Sm": image_tab(tabs["Sm"], geom_mslabs(D, NB, 512)),

        "IC2": image_tab(tabs["IC2"], geom_mslabs(NB, D, 256)),
        "IS2": image_tab(tabs["IS2"], geom_mslabs(NB, D, 256)),

    }
    com.update(tabs2)
    magic = int(np.random.randint(1, 2**30))
    in_maps = []
    for c in range(NCORES):
        j = c % 4
        m = dict(com)
        m["xT"] = image_tab(
            np.ascontiguousarray(x[c * TLOC:(c + 1) * TLOC].T), xslabs)
        rsrc = c - 2 if j >= 2 else c
        m["xR_T"] = image_tab(
            np.ascontiguousarray(x[rsrc * TLOC:(rsrc + 1) * TLOC].T), xslabs)
        m["cid2"] = np.array([[c % 2, magic]], np.int32)
        w = np.zeros(4, np.float32)
        if j == 1:
            w[0] = 1.0
        elif j == 2:
            w[2] = 1.0
            w[3] = 1.0
        elif j == 3:
            w[0] = 1.0
            w[2] = 1.0
            w[3] = 1.0
        m["exw"] = np.ascontiguousarray(
            np.broadcast_to(w[None, :], (128, 4)).copy())
        in_maps.append(m)
    return in_maps


_prog_cache = {}


def kernel(**inputs):
    in_maps = prep_inputs(inputs)
    if "nc" not in _prog_cache:
        _prog_cache["nc"] = build_program()
    nc = _prog_cache["nc"]
    res = run_bass_kernel_spmd(nc, in_maps, core_ids=list(range(NCORES)))
    shards = []
    for r in res.results:
        img = np.asarray(r["outT"]).reshape(128, NCHF, NT, TCF)
        shards.append(img.transpose(2, 0, 1, 3).reshape(D, TLOC).T)
    out = np.concatenate(shards, axis=0).reshape(B, S, D)
    return np.ascontiguousarray(out.astype(np.float32))


if __name__ == "__main__":
    import time
    t0 = time.time()
    nc = build_program()
    print(f"build {time.time()-t0:.1f}s; "
          f"n_instructions={nc.n_instructions()}")


# version 25
# speedup vs baseline: 1.1277x; 1.0299x over previous
import sys
import contextlib
for _p in ("/opt/trn_rl_repo",):
    if _p not in sys.path:
        sys.path.append(_p)
import numpy as np
import ml_dtypes
import concourse.bass as bass
import concourse.mybir as mybir
from concourse.mybir import AluOpType as OP
from concourse.bass_utils import run_bass_kernel_spmd

F32, BF16 = mybir.dt.float32, mybir.dt.bfloat16
AF = mybir.ActivationFunctionType
BF = ml_dtypes.bfloat16
AX = mybir.AxisListType.X

B, S, D = 2, 8192, 2048
T = B * S
NCORES = 8
TLOC = T // NCORES
NB = D // 2
NT = D // 128
NBT = NB // 128
TC = 512
NCH = TLOC // TC
TCF = 256
NCHF = TLOC // TCF
FF = 7168
NFT = FF // 128
E, R = 8, 32
ER = E * R
HID = 8192
NHT = HID // 128
EPS = 1e-5


def make_tables():
    f = np.arange(D, dtype=np.float64)[:, None]
    bh = np.arange(NB, dtype=np.float64)[None, :]
    bf_ = np.arange(D, dtype=np.float64)[None, :]
    ah = 2 * np.pi * f * bh / D
    af = 2 * np.pi * f * bf_ / D
    Ch = np.cos(ah)
    Sm = -np.sin(ah)
    Sm[:, 0] = np.cos(np.pi * np.arange(D))
    Cf = np.cos(af)
    Sf = -np.sin(af)
    jh = np.arange(NB, dtype=np.float64)[:, None]
    n = np.arange(D, dtype=np.float64)[None, :]
    IC2 = 2 * np.cos(2 * np.pi * jh * n / D) / D
    IC2[0, :] = 1.0 / D
    IS2 = -2 * np.sin(2 * np.pi * jh * n / D) / D
    IS2[0, :] = np.cos(np.pi * np.arange(D)) / D
    jf = np.arange(D, dtype=np.float64)[:, None]
    ICf = np.cos(2 * np.pi * jf * n / D) / D
    ISf = -np.sin(2 * np.pi * jf * n / D) / D
    t = {"Ch": Ch, "Sm": Sm, "Cf": Cf, "Sf": Sf,
         "IC2": IC2, "IS2": IS2, "ICf": ICf, "ISf": ISf}
    return {k: np.ascontiguousarray(v.astype(BF)) for k, v in t.items()}


def make_flipm():
    idh = 0.5 * np.eye(128)
    xf = np.zeros((128, 128))
    for m in range(1, 128):
        xf[128 - m, m] = 0.5
    return np.ascontiguousarray(
        np.concatenate([idh, xf, -xf], axis=1).astype(BF))


def image_tab(tab, slabs):
    cols = sum(nkt * mc for (_, nkt, _, mc) in slabs)
    img = np.empty((128, cols), tab.dtype)
    off = 0
    for (kt0, nkt, m0, mc) in slabs:
        blk = tab[kt0 * 128:(kt0 + nkt) * 128, m0:m0 + mc]
        img[:, off:off + nkt * mc] = (
            blk.reshape(nkt, 128, mc).transpose(1, 0, 2).reshape(128, -1))
        off += nkt * mc
    return np.ascontiguousarray(img)


def geom_mslabs(K, M, mcols):
    return [(0, K // 128, m0, mcols) for m0 in range(0, M, mcols)]


def geom_kslabs(K, M, nkt):
    return [(k0, nkt, 0, M) for k0 in range(0, K // 128, nkt)]


def expansion_mat():
    m = np.zeros((8, 256), np.float32)
    for e in range(8):
        for r_ in range(32):
            m[e, (e % 4) * 32 + r_ + (e // 4) * 128] = 1.0
    return np.ascontiguousarray(m.astype(BF))


ENGINES = ("tensor", "vector", "scalar", "gpsimd", "sync")
QUEUES = {"SQ": "sync", "GQ": "gpsimd", "AQ": "scalar"}


class Buf:
    __slots__ = ("name", "wtok", "rtoks")

    def __init__(self, name):
        self.name = name
        self.wtok = None
        self.rtoks = []


class PlanOp:
    __slots__ = ("engine", "fn", "deps", "queue", "idx", "needs_inc", "nm",
                 "selfsync")

    def __init__(self, engine, fn, deps, queue, nm="", selfsync=False):
        self.engine = engine
        self.fn = fn
        self.deps = deps
        self.queue = queue
        self.idx = None
        self.needs_inc = False
        self.nm = nm
        self.selfsync = selfsync


class Prog:
    def __init__(self, nc):
        self.nc = nc
        self.ops = {e: [] for e in ENGINES}
        self.qcount = {q: 0 for q in QUEUES}

    def plan(self, engine, fn, reads=(), writes=(), queue=None, nm="",
             selfsync=False):
        deps = []
        for b in reads:
            if b.wtok is not None:
                deps.append(b.wtok)
        for b in writes:
            if b.wtok is not None:
                deps.append(b.wtok)
            deps.extend(b.rtoks)
        op_ = PlanOp(engine, fn, deps, queue, nm, selfsync)
        if queue is not None:
            self.qcount[queue] += 1
            op_.idx = self.qcount[queue]
        self.ops[engine].append(op_)
        for b in reads:
            b.rtoks.append(op_)
        for b in writes:
            b.wtok = op_
            b.rtoks = []
        return op_

    def dma(self, queue, dst_ap, src_ap, reads=(), writes=(), nm="", **kw):
        def fn(e, _dst=dst_ap, _src=src_ap, _kw=kw):
            return e.dma_start(_dst, _src, **_kw)
        return self.plan(QUEUES[queue], fn, reads=reads, writes=writes,
                         queue=queue, nm=nm or f"dma_{queue}")

    def finalize_and_emit(self, sems):
        for e in ENGINES:
            for op_ in self.ops[e]:
                for d in op_.deps:
                    if d.queue is None and d.engine != op_.engine:
                        d.needs_inc = True
        ecount = {e: 0 for e in ENGINES}
        for e in ENGINES:
            for op_ in self.ops[e]:
                if op_.queue is None and op_.needs_inc:
                    ecount[e] += 1
                    op_.idx = ecount[e]
        self.final_counts = dict(ecount)
        nc = self.nc

        def emit_stream(eng_obj, ename):
            high = {}
            vs_n = 0
            vs_sem = sems.get("vs_" + ename)
            for op_ in self.ops[ename]:
                for d in op_.deps:
                    if d.queue is not None:
                        sem, val = sems[d.queue], 16 * d.idx
                    elif d.engine != ename:
                        sem, val = sems[d.engine], d.idx
                    else:
                        continue
                    if high.get(sem.name, 0) < val:
                        eng_obj.wait_ge(sem, val)
                        high[sem.name] = val
                inst = op_.fn(eng_obj)
                if op_.queue is not None:
                    inst.then_inc(sems[op_.queue], 16)
                elif op_.selfsync:
                    vs_n += 1
                    inst.then_inc(vs_sem, 1)
                    eng_obj.wait_ge(vs_sem, vs_n)
                    if op_.needs_inc:
                        eng_obj.sem_inc(sems[ename], 1)
                elif op_.needs_inc:
                    inst.then_inc(sems[ename], 1)

        with nc.Block() as block:
            block.tensor(lambda t: emit_stream(t, "tensor"))
            block.vector(lambda v: emit_stream(v, "vector"))
            block.scalar(lambda s: emit_stream(s, "scalar"))
            block.gpsimd(lambda g: emit_stream(g, "gpsimd"))

            def sync_fn(s):
                emit_stream(s, "sync")
                for q in QUEUES:
                    if self.qcount[q]:
                        s.wait_ge(sems[q], 16 * self.qcount[q])
            block.sync(sync_fn)


class ArAlloc:
    __slots__ = ("t", "buf", "off", "size")

    def __init__(self, t, buf, off, size):
        self.t = t
        self.buf = buf
        self.off = off
        self.size = size

    def ap(self):
        return self.t.ap()


class Arena:
    def __init__(self, nc, base, top):
        self.nc = nc
        self.base = base
        self.top = top
        self.free_list = [[base, top, []]]
        self.n = 0
        self.peak = base
        self.active = {}

    def alloc(self, cols, dtype, name):
        esz = 4 if dtype == F32 else 2
        nbytes = (cols * esz + 511) & ~511
        for seg in self.free_list:
            if seg[1] - seg[0] >= nbytes:
                off = seg[0]
                preds = list(seg[2])
                seg[0] += nbytes
                if seg[0] >= seg[1]:
                    self.free_list.remove(seg)
                self.n += 1
                self.peak = max(self.peak, off + nbytes)
                for (ao, asz), anm in self.active.items():
                    assert off + nbytes <= ao or off >= ao + asz, (
                        f"arena overlap: {name}@{off}+{nbytes} vs "
                        f"{anm}@{ao}+{asz}")
                t = self.nc.alloc_sbuf_tensor_at(
                    f"ar_{name}_{self.n}", [128, cols], dtype, offset=off)
                buf = Buf(f"{name}@{off}")
                buf.rtoks = preds
                a = ArAlloc(t, buf, off, nbytes)
                self.active[(off, nbytes)] = name
                return a
        raise MemoryError(
            f"arena OOM for {name} ({nbytes}B); free="
            f"{[(s[1]-s[0]) for s in self.free_list]}")

    def free(self, a):
        assert (a.off, a.size) in self.active, \
            f"double free {a.buf.name}@{a.off}+{a.size}"
        for seg in self.free_list:
            assert a.off + a.size <= seg[0] or a.off >= seg[1], \
                f"free overlap {a.buf.name}@{a.off}+{a.size} vs seg {seg[:2]}"
        self.active.pop((a.off, a.size), None)
        toks = list(a.buf.rtoks)
        if a.buf.wtok is not None:
            toks.append(a.buf.wtok)
        self.free_list.append([a.off, a.off + a.size, toks])
        self.free_list.sort(key=lambda s: s[0])
        merged = []
        for seg in self.free_list:
            if merged and merged[-1][1] == seg[0]:
                merged[-1][1] = seg[1]
                merged[-1][2] = merged[-1][2] + seg[2]
            else:
                merged.append(seg)
        self.free_list = merged


def build_program(debug=False):
    nc = bass.Bass("TRN2", target_bir_lowering=False)
    P = Prog(nc)
    din = {}

    def ein(name, shape, dtype):
        din[name] = (nc.dram_tensor(name, shape, dtype, kind="ExternalInput"),
                     Buf("d_" + name))

    def internal(name, shape, dtype):
        din[name] = (nc.dram_tensor(name, shape, dtype, kind="Internal"),
                     Buf("i_" + name))

    ein("xT", (128, NCH * NT * TC), F32)
    for w in ("Wq_T", "Wk_T", "Wv_T", "Ws_T", "Wt_T"):
        ein(w, (D, D), F32)
    for w in ("W1e_a", "W1e_b", "W1e_c", "W1e_d"):
        ein(w, (NB, HID), F32)
    ein("W2a", (128, 64 * 4096), F32)
    ein("b1t", (128, NHT), F32)
    ein("b2t", (128, 2 * NT), F32)
    ein("W1f_T", (128, 14 * 8192), F32)
    ein("W3f_T", (128, 14 * 8192), F32)
    ein("W2f_T", (128, 16 * 7168), F32)
    ein("Wg_tiled", (128, NT * E), F32)
    ein("A1p", (128, NT * 256), F32)
    ein("A3p", (128, NT * 256), F32)
    ein("A2p", (128, 7 * 2048), F32)
    ein("B1p", (128, 14 * 1024), F32)
    ein("B3p", (128, 14 * 1024), F32)
    ein("B2p", (128, 16 * 256), F32)
    for tname, shp in (("Ch", (128, 2 * 8192)), ("Sm", (128, 2 * 8192)),
                       ("IC2", (128, 8 * 2048)), ("IS2", (128, 8 * 2048))):
        ein(tname, shp, BF16)
    ein("EXPM", (8, 256), BF16)
    ein("idn", (128, 128), F32)
    ein("flipm", (128, 384), BF16)
    ein("xR_T", (128, NCH * NT * TC), F32)
    ein("cid2", (1, 2), mybir.dt.int32)
    ein("exw", (128, 4), F32)

    out_h = nc.dram_tensor("outT", (128, NCHF * NT * TCF), F32,
                           kind="ExternalOutput")
    out_b = Buf("outT")
    if debug:
        dbg_x1 = nc.dram_tensor("dbg_x1", (128, NCHF * NT * TCF), F32,
                                kind="ExternalOutput")
        dbg_x1_b = Buf("dbg_x1")
        dbg_cum = nc.dram_tensor("dbg_cum", (128, NCH * NT * TC), F32,
                                 kind="ExternalOutput")
        dbg_cum_b = Buf("dbg_cum")
        dbg_sm = nc.dram_tensor("dbg_sm", (128, 64), F32,
                                kind="ExternalOutput")
        dbg_sm_b = Buf("dbg_sm")
        dbg_ln1 = nc.dram_tensor("dbg_ln1", (D, TC), F32,
                                 kind="ExternalOutput")
        dbg_ln1_b = Buf("dbg_ln1")
        dbg_q = nc.dram_tensor("dbg_q", (D, TC), F32, kind="ExternalOutput")
        dbg_q_b = Buf("dbg_q")
        dbg_p = nc.dram_tensor("dbg_p", (D, TC), F32, kind="ExternalOutput")
        dbg_p_b = Buf("dbg_p")
        dbg_ffn = nc.dram_tensor("dbg_ffn", (128, 2048), F32,
                                 kind="ExternalOutput")
        dbg_ffn_b = Buf("dbg_ffn")
        dbg_f2 = nc.dram_tensor("dbg_f2", (128, 2048), F32,
                                kind="ExternalOutput")
        dbg_f2_b = Buf("dbg_f2")
        dbg_stk = nc.dram_tensor("dbg_stk", (128, 192), F32,
                                 kind="ExternalOutput")
        dbg_stk_b = Buf("dbg_stk")
        dbg_u = nc.dram_tensor("dbg_u", (128, NT * TC), F32,
                               kind="ExternalOutput")
        dbg_u_b = Buf("dbg_u")
        dbg_hf = nc.dram_tensor("dbg_hf", (128, NT * TC), F32,
                                kind="ExternalOutput")
        dbg_hf_b = Buf("dbg_hf")
        dbg_trf = nc.dram_tensor("dbg_trf", (128, 2 * NT * TC), F32,
                                 kind="ExternalOutput")
        dbg_trf_b = Buf("dbg_trf")
        dbg_t = nc.dram_tensor("dbg_t", (128, 2 * NT * TC), F32,
                               kind="ExternalOutput")
        dbg_t_b = Buf("dbg_t")
        dbg_P = nc.dram_tensor("dbg_P", (128, 2 * NT * TC), F32,
                               kind="ExternalOutput")
        dbg_P_b = Buf("dbg_P")
        dbg_ao = nc.dram_tensor("dbg_ao", (128, NT * TC), F32,
                                kind="ExternalOutput")
        dbg_ao_b = Buf("dbg_ao")
        dbg_w1 = nc.dram_tensor("dbg_w1", (128, 32 * 4096), F32,
                                kind="ExternalOutput")
        dbg_w1_b = Buf("dbg_w1")
        dbg_h1 = nc.dram_tensor("dbg_h1", (128, NHT * TC), F32,
                                kind="ExternalOutput")
        dbg_h1_b = Buf("dbg_h1")

    for w in ("q", "k", "v", "s"):
        internal(f"fold_{w}C", (128, 4 * 4096), BF16)
        internal(f"fold_{w}S", (128, 4 * 4096), BF16)
    internal("fold_tC", (128, 4 * 4096), BF16)
    internal("fold_tS", (128, 4 * 4096), BF16)
    internal("W1eff", (128, 32 * 4096), BF16)
    internal("W2a_d", (128, 64 * 4096), BF16)
    internal("W1f_d", (128, 14 * 8192), BF16)
    internal("W3f_d", (128, 14 * 8192), BF16)
    internal("W2f_d", (128, 16 * 7168), BF16)
    internal("A1d", (128, NT * 256), BF16)
    internal("A3d", (128, NT * 256), BF16)
    internal("A2d", (128, 7 * 2048), BF16)
    internal("B1d", (128, 14 * 1024), BF16)
    internal("B3d", (128, 14 * 1024), BF16)
    internal("B2d", (128, 16 * 256), BF16)
    internal("cum_d", (128, NCH * NT * TC), F32)
    internal("x1_d", (128, NCHF * NT * TCF), F32)
    din["sh_x"] = (nc.dram_tensor("sh_x", (256, 32), F32, kind="Internal",
                                  addr_space="Shared"), Buf("i_sh_x"))
    din["sh_f"] = (nc.dram_tensor("sh_f", (2, 16), mybir.dt.int32,
                                  kind="Internal", addr_space="Shared"),
                   Buf("i_sh_f"))

    def dh(name):
        return din[name][0]

    def db(name):
        return din[name][1]

    def dap(name):
        return din[name][0].ap()

    ar = Arena(nc, (nc.sbuf_base + 511) & ~511, nc.sbuf_top & ~511)
    psum = [(nc.alloc_psum_tensor(f"ps{i}", [128, 512], F32), Buf(f"ps{i}"))
            for i in range(8)]
    ps_rr = [0]

    def next_bank():
        i = ps_rr[0] % 8
        ps_rr[0] += 1
        return psum[i]

    def op(engine, fn, reads=(), writes=(), nm=""):
        return P.plan(engine, fn, reads=reads, writes=writes, nm=nm)

    ones_bf = ar.alloc(16, BF16, "ones_bf")
    ones_f1 = ar.alloc(128, F32, "ones_f1")
    wg_cast = ar.alloc(NT * E, BF16, "wgc")
    b1_sb = ar.alloc(NHT, F32, "b1")
    b2_sb = ar.alloc(2 * NT, F32, "b2")
    cid_sb = ar.alloc(2, mybir.dt.int32, "cid_sb")
    mg_sb = ar.alloc(16, mybir.dt.int32, "mg_sb")
    fl_sb = ar.alloc(32, mybir.dt.int32, "fl_sb")
    ex_sb = ar.alloc(64, F32, "ex_sb")
    pollct = ar.alloc(2, mybir.dt.int32, "pollct")
    exw_sb = ar.alloc(4, F32, "exw_sb")
    idn_sb = ar.alloc(128, F32, "idn")
    flip_sb = ar.alloc(384, BF16, "flipm")
    expm_sb = ar.alloc(256, BF16, "expm")
    stats_m = ar.alloc(TLOC, F32, "stats_m")
    stats_r = ar.alloc(TLOC, F32, "stats_r")
    carry = ar.alloc(16, F32, "carry")
    rcarry = ar.alloc(16, F32, "rcarry")
    basep = ar.alloc(16, F32, "base")
    w1pool = [ar.alloc(512, F32, "w1pa"), ar.alloc(512, F32, "w1pb"),
              ar.alloc(512, BF16, "w1po")]
    sr_sm = ar.alloc(512, F32, "sr_sm")
    sr_s2 = ar.alloc(512, F32, "sr_s2")
    sr_tmp = ar.alloc(512, F32, "sr_tmp")

    P.dma("SQ", b1_sb.ap(), dap("b1t"), reads=[db("b1t")], writes=[b1_sb.buf])
    P.dma("SQ", b2_sb.ap(), dap("b2t"), reads=[db("b2t")], writes=[b2_sb.buf])
    P.dma("SQ", cid_sb.ap()[0:1, :], dap("cid2"), reads=[db("cid2")],
          writes=[cid_sb.buf])
    P.dma("SQ", exw_sb.ap(), dap("exw"), reads=[db("exw")],
          writes=[exw_sb.buf])
    P.dma("SQ", idn_sb.ap(), dap("idn"), reads=[db("idn")],
          writes=[idn_sb.buf])
    P.dma("SQ", flip_sb.ap(), dap("flipm"), reads=[db("flipm")],
          writes=[flip_sb.buf])
    P.dma("SQ", expm_sb.ap()[0:8, :], dap("EXPM"), reads=[db("EXPM")],
          writes=[expm_sb.buf])
    P.dma("GQ", wg_cast.ap(), dap("Wg_tiled"), reads=[db("Wg_tiled")],
          writes=[wg_cast.buf])
    eps_sb = ar.alloc(2, F32, "eps")
    op("vector", lambda v: v.memset(eps_sb.ap()[0:1, 0:1], EPS),
       writes=[eps_sb.buf])
    op("vector", lambda v: v.memset(ones_bf.ap(), 1.0), writes=[ones_bf.buf])
    op("vector", lambda v: v.memset(ones_f1.ap()[0:1, :], 1.0),
       writes=[ones_f1.buf])
    op("vector", lambda v: v.memset(carry.ap(), 0.0), writes=[carry.buf])
    op("vector", lambda v: v.memset(rcarry.ap(), 0.0), writes=[rcarry.buf])
    op("vector", lambda v, _d=None: v.tensor_copy(
        mg_sb.ap()[0:1, :],
        cid_sb.ap()[0:1, 1:2].broadcast_to([1, 16])),
        reads=[cid_sb.buf], writes=[mg_sb.buf], nm="mgrep")

    def mm_group(bank, bank_ap, pairs, reads, nm="mm", start=True, stop=True):
        def fn(t, _pairs=tuple(pairs), _ap=bank_ap, _sa=start, _so=stop):
            n = len(_pairs)
            inst = None
            for i, (l, r_) in enumerate(_pairs):
                inst = t.matmul(_ap, l, r_, start=(_sa and i == 0),
                                stop=(_so and i == n - 1))
            return inst
        return op("tensor", fn, reads=list(reads), writes=[bank[1]], nm=nm)

    def load_image_slab(dramname, dst, s, W, nm="ldw"):
        return P.dma("SQ", dst.ap()[:, 0:W],
                     dap(dramname)[:, s * W:(s + 1) * W],
                     reads=[db(dramname)], writes=[dst.buf], nm=nm)

    def ln_stats(src_tiles, src_bufs, scratch, w, m_ap, r_ap, mbuf, rbuf,
                 src_is_bf16):
        nt = len(src_tiles)
        if src_is_bf16:
            bank = next_bank()
            mm_group(bank, bank[0].ap()[0:1, 0:w],
                     [(ones_bf.ap()[:, 0:1], a) for a in src_tiles],
                     list(src_bufs) + [ones_bf.buf], nm="ln_sum")
        else:
            bank = next_bank()
            for hf in range(2):
                for i in range(nt // 2):
                    g = hf * (nt // 2) + i
                    op("scalar", lambda s,
                       _d=scratch.ap()[:, i * w:(i + 1) * w],
                       _s=src_tiles[g]: s.activation(_d, _s, AF.Copy),
                       reads=list(src_bufs), writes=[scratch.buf],
                       nm="lncast")
                mm_group(bank, bank[0].ap()[0:1, 0:w],
                         [(ones_bf.ap()[:, 0:1],
                           scratch.ap()[:, i * w:(i + 1) * w])
                          for i in range(nt // 2)],
                         [scratch.buf, ones_bf.buf], nm="ln_sum",
                         start=(hf == 0), stop=(hf == 1))
        op("scalar", lambda s, _d=sr_sm.ap()[0:1, 0:w],
           _s=bank[0].ap()[0:1, 0:w]:
           s.activation(_d, _s, AF.Copy, scale=1.0 / D),
           reads=[bank[1]], writes=[sr_sm.buf], nm="ln_m")
        bank2 = next_bank()
        for hf in range(2):
            for i in range(nt // 2):
                g = hf * (nt // 2) + i
                op("scalar", lambda s, _d=scratch.ap()[:, i * w:(i + 1) * w],
                   _s=src_tiles[g]: s.activation(_d, _s, AF.Square),
                   reads=list(src_bufs), writes=[scratch.buf], nm="lnsq")
            mm_group(bank2, bank2[0].ap()[0:1, 0:w],
                     [(ones_bf.ap()[:, 0:1],
                       scratch.ap()[:, i * w:(i + 1) * w])
                      for i in range(nt // 2)],
                     [scratch.buf, ones_bf.buf], nm="ln_sum2",
                     start=(hf == 0), stop=(hf == 1))
        op("scalar", lambda s, _d=sr_s2.ap()[0:1, 0:w],
           _s=bank2[0].ap()[0:1, 0:w]:
           s.activation(_d, _s, AF.Copy, scale=1.0 / D),
           reads=[bank2[1]], writes=[sr_s2.buf], nm="ln_e2")
        op("vector", lambda v, _t=sr_tmp.ap()[0:1, 0:w],
           _m=sr_sm.ap()[0:1, 0:w]: v.tensor_mul(_t, _m, _m),
           reads=[sr_sm.buf], writes=[sr_tmp.buf], nm="ln_msq")
        op("vector", lambda v, _t=sr_tmp.ap()[0:1, 0:w],
           _a=sr_s2.ap()[0:1, 0:w], _b=sr_tmp.ap()[0:1, 0:w]:
           v.tensor_sub(_t, _a, _b),
           reads=[sr_s2.buf, sr_tmp.buf], writes=[sr_tmp.buf], nm="ln_var")
        op("scalar", lambda s, _t=sr_tmp.ap()[0:1, 0:w],
           _e=eps_sb.ap()[0:1, 0:1]:
           s.activation(_t, _t, AF.Sqrt, bias=_e),
           reads=[sr_tmp.buf, eps_sb.buf], writes=[sr_tmp.buf], nm="ln_sqrt")
        P.plan("vector", lambda v, _d=r_ap, _s=sr_tmp.ap()[0:1, 0:w]:
               v.reciprocal(_d, _s), reads=[sr_tmp.buf], writes=[rbuf],
               nm="ln_rs", selfsync=True)
        op("vector", lambda v, _d=m_ap, _s=sr_sm.ap()[0:1, 0:w]:
           v.tensor_copy(_d, _s), reads=[sr_sm.buf], writes=[mbuf], nm="ln_mc")

    def bcast_row(row_ap, rowbuf, w, bf=False):
        bank = next_bank()
        lo = ones_f1
        mm_group(bank, bank[0].ap()[:, 0:w],
                 [(lo.ap()[0:1, 0:128], row_ap)],
                 [rowbuf, lo.buf], nm="bcast")
        return bank

    bcast_rr = [0]

    def bcast_row_sb(row_ap, rowbuf, w):
        bank = bcast_row(row_ap, rowbuf, w)
        dst = sr_tmp if bcast_rr[0] % 2 == 0 else sr_sm
        bcast_rr[0] += 1
        op("scalar", lambda s, _d=dst.ap()[:, 0:w],
           _s=bank[0].ap()[:, 0:w]: s.activation(_d, _s, AF.Copy),
           reads=[bank[1]], writes=[dst.buf], nm="bcastc")
        return (dst, dst.buf)

    wt_sb = ar.alloc(NT * D, BF16, "wt_sb")
    tab_sb = ar.alloc(NT * D, BF16, "tab_sb")
    stg = ar.alloc(2048, BF16, "stage")

    def fold_one(wname, tname, outname, ncols, reload_w):
        if reload_w:
            src = dap(wname).rearrange("(kt p) m -> p kt m", p=128)
            P.dma("GQ", wt_sb.ap().rearrange("p (kt m) -> p kt m", m=D), src,
                  reads=[db(wname)], writes=[wt_sb.buf], nm=f"cWT_{wname}")
        P.dma("SQ", tab_sb.ap()[:, 0:NT * ncols], dap(tname),
              reads=[db(tname)], writes=[tab_sb.buf], nm=f"ldtab_{tname}")
        for mt in range(NT):
            for nch0 in range(0, ncols, 512):
                sl = nch0 // 512
                bank = next_bank()
                pairs = []
                for kt in range(NT):
                    l = wt_sb.ap()[:, kt * D + mt * 128:kt * D + mt * 128 + 128]
                    r_ = tab_sb.ap()[:, sl * 8192 + kt * 512:
                                     sl * 8192 + kt * 512 + 512]
                    pairs.append((l, r_))
                mm_group(bank, bank[0].ap(), pairs,
                         [wt_sb.buf, tab_sb.buf], nm=f"fo_{outname}")
                op("scalar", lambda s, _d=stg.ap()[:, nch0:nch0 + 512],
                   _s=bank[0].ap(): s.activation(_d, _s, AF.Copy),
                   reads=[bank[1]], writes=[stg.buf], nm="fo_e")
            nsl = ncols // 256
            d = dap(outname)[:, 0:nsl * 4096].rearrange(
                "p (s km) -> p s km", km=4096)[:, :, mt * 256:(mt + 1) * 256]
            s_ = stg.ap()[:, 0:ncols].rearrange("p (s m) -> p s m", m=256)
            P.dma("AQ", d, s_, reads=[stg.buf], writes=[db(outname)],
                  nm=f"st_{outname}")

    for w, wn in (("q", "Wq_T"), ("k", "Wk_T"), ("v", "Wv_T"), ("s", "Ws_T")):
        fold_one(wn, "Ch", f"fold_{w}C", NB, True)
        fold_one(wn, "Sm", f"fold_{w}S", NB, False)
    fold_one("Wt_T", "Ch", "fold_tC", NB, True)
    fold_one("Wt_T", "Sm", "fold_tS", NB, False)
    ar.free(wt_sb)
    ar.free(tab_sb)
    ar.free(stg)

    DEFER = []
    DEFER2 = []

    def drain_defer(n, lst=None):
        lst = DEFER if lst is None else lst
        for _ in range(n):
            if lst:
                lst.pop(0)()

    def emit_w1eff_fold():
        QW = 512
        pa, pb, po = w1pool
        for grp, (na, nb_, sgn) in enumerate((("W1e_a", "W1e_b", OP.add),
                                              ("W1e_c", "W1e_d",
                                               OP.subtract))):
            for kt in range(NBT):
                for hh in range(HID // QW):

                    def piece(na=na, nb_=nb_, sgn=sgn, kt=kt, hh=hh,
                              grp=grp):
                        P.dma("SQ", pa.ap(),
                              dap(na)[kt * 128:(kt + 1) * 128,
                                      hh * QW:(hh + 1) * QW],
                              reads=[db(na)], writes=[pa.buf], nm="ld_w1a")
                        P.dma("SQ", pb.ap(),
                              dap(nb_)[kt * 128:(kt + 1) * 128,
                                       hh * QW:(hh + 1) * QW],
                              reads=[db(nb_)], writes=[pb.buf], nm="ld_w1b")
                        op("vector", lambda v, _o=po.ap(), _a=pa.ap(),
                           _b=pb.ap(), _s=sgn:
                           v.tensor_tensor(_o, _a, _b, _s),
                           reads=[pa.buf, pb.buf], writes=[po.buf],
                           nm="w1fold")
                        ktg = grp * NBT + kt
                        d = dap("W1eff").rearrange(
                            "p (s km) -> p s km", km=4096)[
                            :, hh * (QW // 256):(hh + 1) * (QW // 256),
                            ktg * 256:(ktg + 1) * 256]
                        P.dma("AQ", d,
                              po.ap().rearrange("p (s m) -> p s m", m=256),
                              reads=[po.buf], writes=[db("W1eff")],
                              nm="st_w1eff")
                    DEFER.append(piece)

    def emit_converts():
        for s_, d_ in (("W2a", "W2a_d"), ("W1f_T", "W1f_d"),
                       ("W3f_T", "W3f_d"), ("W2f_T", "W2f_d"),
                       ("A1p", "A1d"), ("A3p", "A3d"), ("A2p", "A2d"),
                       ("B1p", "B1d"), ("B3p", "B3d"), ("B2p", "B2d")):
            C = dh(s_).shape[1]
            step = 32768
            for c0 in range(0, C, step):
                w = min(step, C - c0)

                def piece(s_=s_, d_=d_, c0=c0, w=w):
                    P.dma("GQ", dap(d_)[:, c0:c0 + w], dap(s_)[:, c0:c0 + w],
                          reads=[db(s_)], writes=[db(d_)], nm=f"cv_{s_}")
                (DEFER if s_ == "W2a" else DEFER2).append(piece)

    ln1 = ar.alloc(NT * TC, BF16, "ln1")
    slab0 = ar.alloc(NT * 256, BF16, "slab0")
    slab1 = ar.alloc(NT * 256, BF16, "slab1")
    onesTC = ar.alloc(TC, F32, "onesTC")
    tmp1 = ar.alloc(TC, F32, "tmp1")
    tmp2 = ar.alloc(TC, F32, "tmp2")
    tmp34 = [None, None]
    op("vector", lambda v: v.memset(onesTC.ap(), 1.0), writes=[onesTC.buf])

    def tile(a, i, w=TC):
        return a.ap()[:, i * w:(i + 1) * w]

    xbf_bufs = [ar.alloc(NT * TC, BF16, "x_bf0")]
    xseq = ([("xR_T", c) for c in range(NCH)] +
            [("xT", c) for c in range(NCH)] +
            [("xT", c) for c in range(NCH)])
    xseq_i = [0]

    def issue_xload(i):
        if i >= len(xseq):
            return
        xsrc, c = xseq[i]
        xb = xbf_bufs[0]
        P.dma("GQ", xb.ap(), dap(xsrc)[:, c * NT * TC:(c + 1) * NT * TC],
              reads=[db(xsrc)], writes=[xb.buf], nm="ld_xbf")

    issue_xload(0)

    def compute_ln1(c, want_stats, sq_scratch, xsrc="xT"):
        i = xseq_i[0]
        assert xseq[i] == (xsrc, c), (xseq[i], xsrc, c)
        xseq_i[0] += 1
        x_bf = xbf_bufs[0]
        m_ap = stats_m.ap()[0:1, c * TC:(c + 1) * TC]
        r_ap = stats_r.ap()[0:1, c * TC:(c + 1) * TC]
        if want_stats:
            ln_stats([tile(x_bf, i2) for i2 in range(NT)], [x_bf.buf],
                     sq_scratch, TC, m_ap, r_ap, stats_m.buf, stats_r.buf,
                     True)
        bm = bcast_row_sb(m_ap, stats_m.buf, TC)
        br = bcast_row_sb(r_ap, stats_r.buf, TC)
        lnv = ln1.ap().rearrange("p (t m) -> p t m", m=TC)
        xv = x_bf.ap().rearrange("p (t m) -> p t m", m=TC)
        op("vector", lambda v, _d=lnv, _a=xv,
           _b=bm[0].ap()[:, 0:TC].unsqueeze(1).broadcast_to([128, NT, TC]):
           v.tensor_sub(_d, _a, _b),
           reads=[x_bf.buf, bm[1]], writes=[ln1.buf], nm="ln1a")
        op("vector", lambda v, _d=lnv, _a=lnv,
           _b=br[0].ap()[:, 0:TC].unsqueeze(1).broadcast_to([128, NT, TC]):
           v.tensor_mul(_d, _a, _b),
           reads=[ln1.buf, br[1]], writes=[ln1.buf], nm="ln1b")
        issue_xload(i + 1)

    def proj_plane(foldname, outplane, nm):
        for q4 in range(4):
            slab = slab0 if q4 % 2 == 0 else slab1
            load_image_slab(foldname, slab, q4, 4096, nm=f"l{nm}")
            for mt in range(2):
                bank = next_bank()
                pairs = []
                for kt in range(NT):
                    l = slab.ap()[:, kt * 256 + mt * 128:
                                  kt * 256 + mt * 128 + 128]
                    pairs.append((l, tile(ln1, kt)))
                mm_group(bank, bank[0].ap(), pairs, [slab.buf, ln1.buf], nm=nm)
                gi = q4 * 2 + mt
                op("scalar", lambda s, _d=tile(outplane, gi),
                   _s=bank[0].ap(): s.activation(_d, _s, AF.Copy),
                   reads=[bank[1]], writes=[outplane.buf], nm=nm + "e")

    def pk_mul_planes(aR, aI, bR, bI, oR, oI, conj, nm):
        s1 = OP.add if conj else OP.subtract
        s2 = OP.subtract if conj else OP.add
        for i in range(NBT):
            op("vector", lambda v, _d=tmp1.ap()[:, 0:TC], _a=tile(aR, i),
               _b=tile(bR, i): v.tensor_mul(_d, _a, _b),
               reads=[aR.buf, bR.buf], writes=[tmp1.buf], nm=nm + "1")
            op("vector", lambda v, _d=tmp2.ap()[:, 0:TC], _a=tile(aI, i),
               _b=tile(bI, i): v.tensor_mul(_d, _a, _b),
               reads=[aI.buf, bI.buf], writes=[tmp2.buf], nm=nm + "2")
            op("vector", lambda v, _d=tile(oR, i), _a=tmp1.ap()[:, 0:TC],
               _b=tmp2.ap()[:, 0:TC], _s=s1: v.tensor_tensor(_d, _a, _b, _s),
               reads=[tmp1.buf, tmp2.buf], writes=[oR.buf], nm=nm + "3")
            if i == 0:
                op("vector", lambda v, _d=tile(oR, 0)[0:1, :],
                   _s=tmp1.ap()[0:1, 0:TC]: v.tensor_copy(_d, _s),
                   reads=[tmp1.buf], writes=[oR.buf], nm=nm + "f1")
                op("vector", lambda v, _d=tmp1.ap()[0:1, 0:TC],
                   _s=tmp2.ap()[0:1, 0:TC]: v.tensor_copy(_d, _s),
                   reads=[tmp2.buf], writes=[tmp1.buf], nm=nm + "f2")
            op("vector", lambda v, _d=tmp2.ap()[:, 0:TC], _a=tile(aR, i),
               _b=tile(bI, i): v.tensor_mul(_d, _a, _b),
               reads=[aR.buf, bI.buf], writes=[tmp2.buf], nm=nm + "4")
            op("vector", lambda v, _d=tile(oI, i), _a=tile(aI, i),
               _b=tile(bR, i): v.tensor_mul(_d, _a, _b),
               reads=[aI.buf, bR.buf], writes=[oI.buf], nm=nm + "5")
            op("vector", lambda v, _d=tile(oI, i), _a=tmp2.ap()[:, 0:TC],
               _b=tile(oI, i), _s=s2: v.tensor_tensor(_d, _a, _b, _s),
               reads=[tmp2.buf, oI.buf], writes=[oI.buf], nm=nm + "6")
            if i == 0:
                op("vector", lambda v, _d=tile(oI, 0)[0:1, :],
                   _s=tmp1.ap()[0:1, 0:TC]: v.tensor_copy(_d, _s),
                   reads=[tmp1.buf], writes=[oI.buf], nm=nm + "f3")

    sq_g = ar.alloc(8 * TC, BF16, "sq_g")
    plA = ar.alloc(NBT * TC, F32, "plA")
    plB = ar.alloc(NBT * TC, F32, "plB")
    plC = ar.alloc(NBT * TC, F32, "plC")
    plD = ar.alloc(NBT * TC, F32, "plD")
    plE = ar.alloc(NBT * TC, F32, "plE")
    plF = ar.alloc(NBT * TC, F32, "plF")

    def p1_chunk(c, xsrc, carrybuf, full, tap=False):
        compute_ln1(c, True, sq_g, xsrc)
        if tap:
            P.dma("GQ", dbg_ln1.ap().rearrange("(kt p) m -> p kt m", p=128),
                  ln1.ap().rearrange("p (kt m) -> p kt m", m=TC),
                  reads=[ln1.buf], writes=[dbg_ln1_b], nm="tap_ln1")
        proj_plane("fold_qC", plA, "pqC")
        proj_plane("fold_qS", plB, "pqS")
        proj_plane("fold_kC", plC, "pkC")
        proj_plane("fold_kS", plD, "pkS")
        if tap:
            for half, pl in ((0, plA), (1, plB)):
                P.dma("AQ", dbg_q.ap()[half * NB:(half + 1) * NB, :]
                      .rearrange("(kt p) m -> p kt m", p=128),
                      pl.ap().rearrange("p (kt m) -> p kt m", m=TC),
                      reads=[pl.buf], writes=[dbg_q_b], nm="tap_q")
        pk_mul_planes(plA, plB, plC, plD, plE, plF, True, "m")
        proj_plane("fold_vC", plA, "pvC")
        proj_plane("fold_vS", plB, "pvS")
        pk_mul_planes(plE, plF, plA, plB, plC, plD, False, "p")
        if tap:
            for half, pl in ((0, plC), (1, plD)):
                P.dma("AQ", dbg_p.ap()[half * NB:(half + 1) * NB, :]
                      .rearrange("(kt p) m -> p kt m", p=128),
                      pl.ap().rearrange("p (kt m) -> p kt m", m=TC),
                      reads=[pl.buf], writes=[dbg_p_b], nm="tap_p")
        for half, (src, dstp) in enumerate(((plC, plE), (plD, plF))):
            for i in range(NBT):
                ci = half * 8 + i
                P.plan("vector", lambda v, _d=tile(dstp, i), _o=onesTC.ap(),
                       _s=tile(src, i), _in=carrybuf.ap()[:, ci:ci + 1]:
                       v.tensor_tensor_scan(_d, _o, _s, _in, OP.mult, OP.add),
                       reads=[src.buf, carrybuf.buf, onesTC.buf],
                       writes=[dstp.buf], nm="scan", selfsync=True)
                P.plan("vector", lambda v, _d=carrybuf.ap()[:, ci:ci + 1],
                       _s=tile(dstp, i)[:, TC - 1:TC]: v.tensor_copy(_d, _s),
                       reads=[dstp.buf], writes=[carrybuf.buf], nm="carry",
                       selfsync=True)
        if not full:
            return
        for tabname, dstp in (("Ch", plE), ("Sm", plF)):
            for s8 in range(2):
                P.dma("SQ", slab0.ap()[:, 0:4096],
                      dap(tabname)[:, s8 * 8192:s8 * 8192 + 4096],
                      reads=[db(tabname)], writes=[slab0.buf], nm="lfft0")
                P.dma("SQ", slab1.ap()[:, 0:4096],
                      dap(tabname)[:, s8 * 8192 + 4096:(s8 + 1) * 8192],
                      reads=[db(tabname)], writes=[slab1.buf], nm="lfft1")
                for mt in range(4):
                    bank = next_bank()
                    pairs = []
                    for kt in range(NT):
                        sb = slab0 if kt < 8 else slab1
                        l = sb.ap()[:, (kt % 8) * 512 + mt * 128:
                                    (kt % 8) * 512 + mt * 128 + 128]
                        pairs.append((l, tile(ln1, kt)))
                    mm_group(bank, bank[0].ap(), pairs,
                             [slab0.buf, slab1.buf, ln1.buf], nm="fft")
                    gi = s8 * 4 + mt
                    op("vector", lambda v, _d=tile(dstp, gi),
                       _a=bank[0].ap(), _b=tile(dstp, gi):
                       v.tensor_add(_d, _a, _b),
                       reads=[bank[1], dstp.buf], writes=[dstp.buf],
                       nm="fftadd")
        for half, pl in ((0, plE), (1, plF)):
            d = dap("cum_d")[:, c * NT * TC + half * NBT * TC:
                             c * NT * TC + (half + 1) * NBT * TC]
            P.dma("AQ", d, pl.ap(), reads=[pl.buf], writes=[db("cum_d")],
                  nm="st_cum")

    emit_w1eff_fold()
    NW1P = len(DEFER)
    emit_converts()
    DEFER[:] = DEFER[NW1P:] + DEFER[:NW1P]
    NDEF0 = len(DEFER)
    NDEF2 = len(DEFER2)

    for c in range(NCH):
        p1_chunk(c, "xR_T", rcarry, False)
        drain_defer(NDEF0 // 9 + 1)
    for c in range(NCH):
        p1_chunk(c, "xT", carry, True, tap=(debug and c == 0))
        drain_defer(NDEF0 // 9 + 1)

    ar.free(sq_g)
    ar.free(plA)
    ar.free(plB)
    ar.free(plC)
    ar.free(plD)
    ar.free(plE)
    ar.free(plF)
    tmp3 = ar.alloc(TC, F32, "tmp3")
    tmp4 = ar.alloc(TC, F32, "tmp4")

    def xch_fn(g):
        xch_sem = globals()["_XCH_SEM"]
        cval = g.value_load(cid_sb.ap()[0:1, 0:1])
        sh = dap("sh_x").rearrange("(c p) m -> c p m", p=128)
        for i in g.Switch(cval, 2):
            g.dma_start(sh[i][:, 0:16], carry.ap()).then_inc(xch_sem, 16)
        g.wait_ge(xch_sem, 16)
        for i in g.Switch(cval, 2):
            g.dma_start(sh[i][:, 16:32], rcarry.ap()).then_inc(xch_sem, 16)
        g.wait_ge(xch_sem, 32)
        for i in g.Switch(cval, 2):
            g.dma_start(dap("sh_f")[i:i + 1, :],
                        mg_sb.ap()[0:1, :]).then_inc(xch_sem, 16)
        g.wait_ge(xch_sem, 48)
        with g.register("nd") as nd, g.register("cnt") as cnt, \
                g.register("acc") as acc:
            mgv = g.value_load(cid_sb.ap()[0:1, 1:2])
            g.reg_mov(nd, 1)
            g.reg_mov(cnt, 48)
            with g.While(nd):
                g.reg_add(cnt, cnt, 16)
                g.dma_start(fl_sb.ap()[0:1, :],
                            dap("sh_f").rearrange("a b -> (a b)")
                            .rearrange("(one f) -> one f", one=1)
                            ).then_inc(xch_sem, 16)
                g.wait_ge(xch_sem, cnt)
                g.reg_mov(acc, 0)
                with g.register("ecmp") as e:
                    for r_ in range(2):
                        v = g.value_load(
                            fl_sb.ap()[0:1, r_ * 16:r_ * 16 + 1])
                        g.reg_alu(e, v, mgv, mybir.AluOpType.is_equal)
                        g.reg_add(acc, acc, e)
                g.reg_alu(nd, acc, 2, mybir.AluOpType.not_equal)
            g.reg_add(cnt, cnt, 16)
            g.dma_start(ex_sb.ap().rearrange("p (c m) -> p c m", m=32),
                        dap("sh_x").rearrange("(c p) m -> p c m", p=128)
                        ).then_inc(xch_sem, 16)
            g.wait_ge(xch_sem, cnt)
            g.reg_save(pollct.ap()[0:1, 0:1], cnt)
        return g.nop()
    P.plan("gpsimd", xch_fn, reads=[carry.buf, rcarry.buf, cid_sb.buf,
                                    mg_sb.buf, db("sh_x"), db("sh_f")],
           writes=[ex_sb.buf, fl_sb.buf, pollct.buf], nm="exchange")
    op("vector", lambda v: v.memset(basep.ap(), 0.0), writes=[basep.buf])
    for s_ in range(2):
        op("vector", lambda v, _d=basep.ap(),
           _a=ex_sb.ap()[:, s_ * 32:s_ * 32 + 16],
           _m=exw_sb.ap()[:, s_:s_ + 1], _b=basep.ap():
           v.scalar_tensor_tensor(_d, _a, _m, _b, OP.mult, OP.add),
           reads=[ex_sb.buf, exw_sb.buf, basep.buf], writes=[basep.buf],
           nm="baseT")
        op("vector", lambda v, _d=basep.ap(),
           _a=ex_sb.ap()[:, s_ * 32 + 16:s_ * 32 + 32],
           _m=exw_sb.ap()[:, 2 + s_:3 + s_], _b=basep.ap():
           v.scalar_tensor_tensor(_d, _a, _m, _b, OP.mult, OP.add),
           reads=[ex_sb.buf, exw_sb.buf, basep.buf], writes=[basep.buf],
           nm="baseR")

    if debug:
        dsc = ar.alloc(64, F32, "dbgsm")
        op("vector", lambda v, _d=dsc.ap()[:, 0:16], _s=carry.ap():
           v.tensor_copy(_d, _s), reads=[carry.buf], writes=[dsc.buf])
        op("vector", lambda v, _d=dsc.ap()[:, 16:32], _s=rcarry.ap():
           v.tensor_copy(_d, _s), reads=[rcarry.buf], writes=[dsc.buf])
        op("vector", lambda v, _d=dsc.ap()[:, 32:48], _s=basep.ap():
           v.tensor_copy(_d, _s), reads=[basep.buf], writes=[dsc.buf])
        op("vector", lambda v, _d=dsc.ap()[:, 48:63],
           _s=ex_sb.ap()[:, 0:15]: v.tensor_copy(_d, _s),
           reads=[ex_sb.buf], writes=[dsc.buf])
        op("vector", lambda v, _d=dsc.ap()[0:1, 63:64],
           _s=pollct.ap()[0:1, 0:1].bitcast(F32) if False else
           pollct.ap()[0:1, 0:1]: v.tensor_copy(_d, _s),
           reads=[pollct.buf], writes=[dsc.buf])
        P.dma("AQ", dbg_sm.ap(), dsc.ap(), reads=[dsc.buf],
              writes=[dbg_sm_b], nm="st_dbgsm")
        ar.free(dsc)
        dbb = ar.alloc(4096, F32, "dbgb")
        for c in range(NCH * NT * TC // 4096):
            P.dma("SQ", dbb.ap(), dap("cum_d")[:, c * 4096:(c + 1) * 4096],
                  reads=[db("cum_d")], writes=[dbb.buf], nm="dbgld")
            P.dma("AQ", dbg_cum.ap()[:, c * 4096:(c + 1) * 4096], dbb.ap(),
                  reads=[dbb.buf], writes=[dbg_cum_b], nm="dbgst")
        ar.free(dbb)


    for c in range(NCH):
        compute_ln1(c, False, None)
        if c == 0:
            drain_defer(len(DEFER))
            if debug:
                for j in range(4):
                    P.dma("GQ", dbg_w1.ap()[:, j * 32768:(j + 1) * 32768],
                          dap("W1eff")[:, j * 32768:(j + 1) * 32768],
                          reads=[db("W1eff")], writes=[dbg_w1_b],
                          nm="tap_w1")
        else:
            drain_defer(NDEF2 // 3 + 1, DEFER2)
        pC = ar.alloc(NBT * TC, F32, "pC")
        pD = ar.alloc(NBT * TC, F32, "pD")
        trb = ar.alloc(NT * TC, BF16, "trb")
        pA = ar.alloc(NBT * TC, F32, "pA")
        pB = ar.alloc(NBT * TC, F32, "pB")
        tmp5 = ar.alloc(TC, F32, "tmp5")
        tmp6 = ar.alloc(TC, F32, "tmp6")
        for half, pl in ((0, pA), (1, pB)):
            s = dap("cum_d")[:, c * NT * TC + half * NBT * TC:
                             c * NT * TC + (half + 1) * NBT * TC]
            P.dma("SQ", pl.ap(), s, reads=[db("cum_d")], writes=[pl.buf],
                  nm="ld_cum")
            for i in range(NBT):
                ci = half * 8 + i
                op("vector", lambda v, _d=tile(pl, i), _s=tile(pl, i),
                   _b=basep.ap()[:, ci:ci + 1]:
                   v.tensor_scalar_add(_d, _s, _b),
                   reads=[pl.buf, basep.buf], writes=[pl.buf], nm="addbase")
        proj_plane("fold_sC", pC, "psC")
        proj_plane("fold_sS", pD, "psS")
        for i in range(NBT):
            op("vector", lambda v, _d=tmp1.ap()[:, 0:TC], _a=tile(pC, i),
               _b=tile(pA, i): v.tensor_mul(_d, _a, _b),
               reads=[pC.buf, pA.buf], writes=[tmp1.buf], nm="ta1")
            op("vector", lambda v, _d=tmp2.ap()[:, 0:TC], _a=tile(pD, i),
               _b=tile(pB, i): v.tensor_mul(_d, _a, _b),
               reads=[pD.buf, pB.buf], writes=[tmp2.buf], nm="ta2")
            op("vector", lambda v, _d=tmp5.ap()[:, 0:TC],
               _a=tmp1.ap()[:, 0:TC], _b=tmp2.ap()[:, 0:TC]:
               v.tensor_add(_d, _a, _b),
               reads=[tmp1.buf, tmp2.buf], writes=[tmp5.buf], nm="ta3")
            if i == 0:
                op("vector", lambda v, _d=tmp5.ap()[0:1, 0:TC],
                   _s=tmp1.ap()[0:1, 0:TC]: v.tensor_copy(_d, _s),
                   reads=[tmp1.buf], writes=[tmp5.buf], nm="taf1")
                op("vector", lambda v, _d=tmp1.ap()[0:1, 0:TC],
                   _s=tmp2.ap()[0:1, 0:TC]: v.tensor_copy(_d, _s),
                   reads=[tmp2.buf], writes=[tmp1.buf], nm="taf2")
            op("vector", lambda v, _d=tmp2.ap()[:, 0:TC], _a=tile(pC, i),
               _b=tile(pB, i): v.tensor_mul(_d, _a, _b),
               reads=[pC.buf, pB.buf], writes=[tmp2.buf], nm="ta4")
            op("vector", lambda v, _d=tmp6.ap()[:, 0:TC], _a=tile(pD, i),
               _b=tile(pA, i): v.tensor_mul(_d, _a, _b),
               reads=[pD.buf, pA.buf], writes=[tmp6.buf], nm="ta5")
            op("vector", lambda v, _d=tmp6.ap()[:, 0:TC],
               _a=tmp2.ap()[:, 0:TC], _b=tmp6.ap()[:, 0:TC]:
               v.tensor_sub(_d, _a, _b),
               reads=[tmp2.buf, tmp6.buf], writes=[tmp6.buf], nm="ta6")
            if i == 0:
                op("vector", lambda v, _d=tmp6.ap()[0:1, 0:TC],
                   _s=tmp1.ap()[0:1, 0:TC]: v.tensor_copy(_d, _s),
                   reads=[tmp1.buf], writes=[tmp6.buf], nm="taf3")
            op("scalar", lambda s, _d=tile(trb, i), _s=tmp5.ap()[:, 0:TC]:
               s.copy(_d, _s), reads=[tmp5.buf], writes=[trb.buf], nm="trbR")
            op("scalar", lambda s, _d=tile(trb, 8 + i),
               _s=tmp6.ap()[:, 0:TC]:
               s.copy(_d, _s), reads=[tmp6.buf], writes=[trb.buf], nm="trbI")
            op("vector", lambda v, _d=tmp1.ap()[:, 0:TC], _a=tile(pC, i),
               _b=tmp5.ap()[:, 0:TC]: v.tensor_mul(_d, _a, _b),
               reads=[pC.buf, tmp5.buf], writes=[tmp1.buf], nm="sf1")
            op("vector", lambda v, _d=tmp2.ap()[:, 0:TC], _a=tile(pD, i),
               _b=tmp6.ap()[:, 0:TC]: v.tensor_mul(_d, _a, _b),
               reads=[pD.buf, tmp6.buf], writes=[tmp2.buf], nm="sf2")
            op("vector", lambda v, _d=tmp3.ap()[:, 0:TC], _a=tile(pC, i),
               _b=tmp6.ap()[:, 0:TC]: v.tensor_mul(_d, _a, _b),
               reads=[pC.buf, tmp6.buf], writes=[tmp3.buf], nm="sf3")
            op("vector", lambda v, _d=tmp4.ap()[:, 0:TC], _a=tile(pD, i),
               _b=tmp5.ap()[:, 0:TC]: v.tensor_mul(_d, _a, _b),
               reads=[pD.buf, tmp5.buf], writes=[tmp4.buf], nm="sf4")
            op("vector", lambda v, _d=tmp3.ap()[:, 0:TC],
               _a=tmp3.ap()[:, 0:TC], _b=tmp4.ap()[:, 0:TC]:
               v.tensor_add(_d, _a, _b),
               reads=[tmp3.buf, tmp4.buf], writes=[tmp3.buf], nm="sf5")
            if i == 0:
                op("vector", lambda v, _d=tmp3.ap()[0:1, 0:TC],
                   _s=tmp2.ap()[0:1, 0:TC]: v.tensor_copy(_d, _s),
                   reads=[tmp2.buf], writes=[tmp3.buf], nm="sf6")
            op("vector", lambda v, _d=tmp4.ap()[:, 0:TC],
               _a=tmp1.ap()[:, 0:TC], _b=tmp2.ap()[:, 0:TC]:
               v.tensor_sub(_d, _a, _b),
               reads=[tmp1.buf, tmp2.buf], writes=[tmp4.buf], nm="sf7")
            if i == 0:
                op("vector", lambda v, _d=tmp4.ap()[0:1, 0:TC],
                   _s=tmp1.ap()[0:1, 0:TC]: v.tensor_copy(_d, _s),
                   reads=[tmp1.buf], writes=[tmp4.buf], nm="sf8")
            op("vector", lambda v, _d=tile(pC, i), _a=tile(pA, i),
               _b=tmp4.ap()[:, 0:TC]: v.tensor_sub(_d, _a, _b),
               reads=[pA.buf, tmp4.buf], writes=[pC.buf], nm="hfR")
            op("vector", lambda v, _d=tile(pD, i), _a=tile(pB, i),
               _b=tmp3.ap()[:, 0:TC]: v.tensor_sub(_d, _a, _b),
               reads=[pB.buf, tmp3.buf], writes=[pD.buf], nm="hfI")
        if debug and c == 0:
            P.dma("GQ", dbg_u.ap().rearrange("p (kt m) -> p kt m", m=TC),
                  trb.ap().rearrange("p (kt m) -> p kt m", m=TC),
                  reads=[trb.buf], writes=[dbg_u_b], nm="tap_u")
        ar.free(pA)
        ar.free(pB)
        ar.free(tmp5)
        ar.free(tmp6)
        ar.free(tmp1)
        ar.free(tmp2)
        ar.free(tmp3)
        ar.free(tmp4)
        hfb = ar.alloc(NT * TC, BF16, "hfb")
        for i in range(NBT):
            op("scalar", lambda s, _d=tile(hfb, i), _s=tile(pC, i):
               s.copy(_d, _s), reads=[pC.buf], writes=[hfb.buf], nm="hfbR")
            op("scalar", lambda s, _d=tile(hfb, 8 + i), _s=tile(pD, i):
               s.copy(_d, _s), reads=[pD.buf], writes=[hfb.buf], nm="hfbI")
        ar.free(pC)
        ar.free(pD)
        if debug and c == 0:
            P.dma("GQ", dbg_hf.ap().rearrange("p (kt m) -> p kt m", m=TC),
                  hfb.ap().rearrange("p (kt m) -> p kt m", m=TC),
                  reads=[hfb.buf], writes=[dbg_hf_b], nm="tap_hf")
        h1h = [ar.alloc(NHT * TC // 2, BF16, f"h1h{q}") for q in range(2)]

        def h1t(i):
            return h1h[i // 32].ap()[:, (i % 32) * TC:(i % 32 + 1) * TC]

        def h1b(i):
            return h1h[i // 32].buf

        for sl in range(32):
            slab = slab0 if sl % 2 == 0 else slab1
            load_image_slab("W1eff", slab, sl, 4096, nm="lw1e")
            for mt in range(2):
                gm = sl * 2 + mt
                bank = next_bank()
                pairs = []
                for kt in range(NT):
                    l = slab.ap()[:, kt * 256 + mt * 128:
                                  kt * 256 + mt * 128 + 128]
                    pairs.append((l, tile(trb, kt)))
                mm_group(bank, bank[0].ap(), pairs, [slab.buf, trb.buf],
                         nm="mlp1")
                op("scalar", lambda s, _d=h1t(gm), _s=bank[0].ap(),
                   _b=b1_sb.ap()[:, gm:gm + 1]:
                   s.activation(_d, _s, AF.Relu, bias=_b),
                   reads=[bank[1], b1_sb.buf], writes=[h1b(gm)], nm="relu")
        ar.free(trb)
        if debug and c == 0:
            for q in range(2):
                P.dma("GQ", dbg_h1.ap()[:, q * 16384:(q + 1) * 16384]
                      .rearrange("p (j m) -> p j m", m=TC),
                      h1h[q].ap().rearrange("p (j m) -> p j m", m=TC),
                      reads=[h1h[q].buf], writes=[dbg_h1_b], nm="tap_h1")
        trfq = [ar.alloc(4 * TC, BF16, f"trfq{q}") for q in range(8)]

        def trft(i):
            return trfq[i // 4].ap()[:, (i % 4) * TC:(i % 4 + 1) * TC]

        def trfb(i):
            return trfq[i // 4].buf

        for mt in range(2 * NT):
            bank = next_bank()
            for kh in range(2):
                slab = slab0 if kh == 0 else slab1
                load_image_slab("W2a_d", slab, mt * 2 + kh, 4096, nm="lw2a")
                pairs = []
                for kt in range(32):
                    pairs.append((slab.ap()[:, kt * 128:kt * 128 + 128],
                                  h1t(kh * 32 + kt)))
                mm_group(bank, bank[0].ap(), pairs,
                         [slab.buf, h1h[kh].buf],
                         nm="mlp2", start=(kh == 0), stop=(kh == 1))
            op("scalar", lambda s, _d=trft(mt), _s=bank[0].ap(),
               _b=b2_sb.ap()[:, mt:mt + 1]:
               s.activation(_d, _s, AF.Identity, bias=_b),
               reads=[bank[1], b2_sb.buf], writes=[trfb(mt)], nm="mlp2e")
        if debug and c == 0:
            for q in range(8):
                P.dma("GQ", dbg_trf.ap()[:, q * 4 * TC:(q + 1) * 4 * TC]
                      .rearrange("p (j m) -> p j m", m=TC),
                      trfq[q].ap().rearrange("p (j m) -> p j m", m=TC),
                      reads=[trfq[q].buf], writes=[dbg_trf_b], nm="tap_trf")
        ar.free(h1h[0])
        ar.free(h1h[1])
        tmp1 = ar.alloc(TC, F32, "tmp1")
        tmp2 = ar.alloc(TC, F32, "tmp2")
        tmp3 = ar.alloc(TC, F32, "tmp3")
        tmp4 = ar.alloc(TC, F32, "tmp4")
        tPR = ar.alloc(NBT * TC, F32, "tPR")
        tPI = ar.alloc(NBT * TC, F32, "tPI")
        proj_plane("fold_tC", tPR, "ptC")
        proj_plane("fold_tS", tPI, "ptS")
        if debug and c == 0:
            P.dma("AQ", dbg_t.ap()[:, 0:NBT * TC], tPR.ap(),
                  reads=[tPR.buf], writes=[dbg_t_b], nm="tap_t")
            P.dma("AQ", dbg_t.ap()[:, NBT * TC:NT * TC], tPI.ap(),
                  reads=[tPI.buf], writes=[dbg_t_b], nm="tap_t2")
        for half in range(2):
            xoff = (1 + half) * 128
            for j in range(NBT):
                dtile = half * 16 + j
                ftile = half * 16 + 15 - j
                bank = next_bank()
                mm_group(bank, bank[0].ap(),
                         [(flip_sb.ap()[:, 0:128], trft(dtile)),
                          (flip_sb.ap()[:, xoff:xoff + 128], trft(ftile))],
                         [flip_sb.buf, trfb(dtile), trfb(ftile)], nm="tfold")
                op("scalar", lambda s, _d=trft(dtile), _s=bank[0].ap():
                   s.activation(_d, _s, AF.Copy),
                   reads=[bank[1]], writes=[trfb(dtile)], nm="tfolde")
        op("scalar", lambda s, _d=trft(0)[0:1, :], _s=trft(0)[0:1, :]:
           s.activation(_d, _s, AF.Copy, scale=2.0),
           reads=[trfb(0)], writes=[trfb(0)], nm="tfx0")
        op("scalar", lambda s, _d=trft(16)[0:1, :], _s=trft(8)[0:1, :]:
           s.activation(_d, _s, AF.Copy),
           reads=[trfb(8)], writes=[trfb(16)], nm="tfx1")
        for j in range(1, NBT):
            op("vector", lambda v, _d=trft(j)[0:1, :],
               _a=trft(16 - j)[0:1, :], _b=trft(j)[0:1, :]:
               v.scalar_tensor_tensor(_d, _a, 0.5, _b, OP.mult, OP.add),
               reads=[trfb(16 - j), trfb(j)], writes=[trfb(j)], nm="tfxr")
            op("vector", lambda v, _d=trft(16 + j)[0:1, :],
               _a=trft(32 - j)[0:1, :], _b=trft(16 + j)[0:1, :]:
               v.scalar_tensor_tensor(_d, _a, -0.5, _b, OP.mult, OP.add),
               reads=[trfb(32 - j), trfb(16 + j)], writes=[trfb(16 + j)],
               nm="tfxi")
        for j in range(NBT):
            op("vector", lambda v, _d=tmp1.ap()[:, 0:TC],
               _a=tile(tPR, j), _b=trft(j): v.tensor_mul(_d, _a, _b),
               reads=[tPR.buf, trfb(j)], writes=[tmp1.buf], nm="Q1")
            op("vector", lambda v, _d=tmp2.ap()[:, 0:TC],
               _a=tile(tPI, j), _b=trft(16 + j): v.tensor_mul(_d, _a, _b),
               reads=[tPI.buf, trfb(16 + j)], writes=[tmp2.buf], nm="Q2")
            op("vector", lambda v, _d=tmp3.ap()[:, 0:TC],
               _a=tmp1.ap()[:, 0:TC], _b=tmp2.ap()[:, 0:TC]:
               v.tensor_sub(_d, _a, _b),
               reads=[tmp1.buf, tmp2.buf], writes=[tmp3.buf], nm="Q3")
            if j == 0:
                op("vector", lambda v, _d=tmp3.ap()[0:1, 0:TC],
                   _s=tmp1.ap()[0:1, 0:TC]: v.tensor_copy(_d, _s),
                   reads=[tmp1.buf], writes=[tmp3.buf], nm="Qf1")
            op("vector", lambda v, _d=tmp1.ap()[:, 0:TC],
               _a=tile(tPR, j), _b=trft(16 + j): v.tensor_mul(_d, _a, _b),
               reads=[tPR.buf, trfb(16 + j)], writes=[tmp1.buf], nm="Q4")
            op("vector", lambda v, _d=tmp4.ap()[:, 0:TC],
               _a=tile(tPI, j), _b=trft(j): v.tensor_mul(_d, _a, _b),
               reads=[tPI.buf, trfb(j)], writes=[tmp4.buf], nm="Q5")
            op("vector", lambda v, _d=tmp1.ap()[:, 0:TC],
               _a=tmp1.ap()[:, 0:TC], _b=tmp4.ap()[:, 0:TC]:
               v.tensor_add(_d, _a, _b),
               reads=[tmp1.buf, tmp4.buf], writes=[tmp1.buf], nm="Q6")
            if j == 0:
                op("vector", lambda v, _d=tmp1.ap()[0:1, 0:TC],
                   _s=tmp2.ap()[0:1, 0:TC]: v.tensor_copy(_d, _s),
                   reads=[tmp2.buf], writes=[tmp1.buf], nm="Qf2")
            op("vector", lambda v, _d=tmp3.ap()[:, 0:TC],
               _a=tmp3.ap()[:, 0:TC], _b=tile(hfb, j):
               v.tensor_add(_d, _a, _b),
               reads=[tmp3.buf, hfb.buf], writes=[tmp3.buf], nm="Qh1")
            op("vector", lambda v, _d=tmp1.ap()[:, 0:TC],
               _a=tmp1.ap()[:, 0:TC], _b=tile(hfb, 8 + j):
               v.tensor_add(_d, _a, _b),
               reads=[tmp1.buf, hfb.buf], writes=[tmp1.buf], nm="Qh2")
            op("scalar", lambda s, _d=trft(8 + j), _s=tmp3.ap()[:, 0:TC]:
               s.copy(_d, _s), reads=[tmp3.buf], writes=[trfb(8 + j)],
               nm="Qc1")
            op("scalar", lambda s, _d=trft(24 + j), _s=tmp1.ap()[:, 0:TC]:
               s.copy(_d, _s), reads=[tmp1.buf], writes=[trfb(24 + j)],
               nm="Qc2")
        ar.free(tPR)
        ar.free(tPI)
        ar.free(hfb)
        if debug and c == 0:
            for q in range(8):
                P.dma("GQ", dbg_P.ap()[:, q * 4 * TC:(q + 1) * 4 * TC]
                      .rearrange("p (j m) -> p j m", m=TC),
                      trfq[q].ap().rearrange("p (j m) -> p j m", m=TC),
                      reads=[trfq[q].buf], writes=[dbg_P_b], nm="tap_P")
        attn_o = ar.alloc(NT * TC, F32, "attn_o")
        ic2s = ar.alloc(NBT * 256, BF16, "ic2s")
        is2s = ar.alloc(NBT * 256, BF16, "is2s")
        for msl in range(8):
            load_image_slab("IC2", ic2s, msl, 2048, nm="lic2")
            load_image_slab("IS2", is2s, msl, 2048, nm="lis2")
            for mt in range(2):
                gm = msl * 2 + mt
                bank = next_bank()
                pairs = []
                for kt in range(NBT):
                    pairs.append((ic2s.ap()[:, kt * 256 + mt * 128:
                                            kt * 256 + mt * 128 + 128],
                                  trft(8 + kt)))
                for kt in range(NBT):
                    pairs.append((is2s.ap()[:, kt * 256 + mt * 128:
                                            kt * 256 + mt * 128 + 128],
                                  trft(24 + kt)))
                mm_group(bank, bank[0].ap(), pairs,
                         [ic2s.buf, is2s.buf] + [q.buf for q in trfq],
                         nm="idft")
                op("scalar", lambda s, _d=tile(attn_o, gm), _s=bank[0].ap():
                   s.activation(_d, _s, AF.Copy),
                   reads=[bank[1]], writes=[attn_o.buf], nm="idfte")
        ar.free(ic2s)
        ar.free(is2s)
        for q in trfq:
            ar.free(q)
        if debug and c == 0:
            P.dma("AQ", dbg_ao.ap(), attn_o.ap(), reads=[attn_o.buf],
                  writes=[dbg_ao_b], nm="tap_ao")
        sqb = ar.alloc(8 * TC, BF16, "sqb")
        lm = ar.alloc(TC, F32, "lm2")
        lr = ar.alloc(TC, F32, "lr2")
        ln_stats([tile(attn_o, i) for i in range(NT)], [attn_o.buf], sqb, TC,
                 lm.ap()[0:1, :], lr.ap()[0:1, :], lm.buf, lr.buf, False)
        ar.free(sqb)
        bm = bcast_row_sb(lm.ap()[0:1, :], lm.buf, TC)
        br = bcast_row_sb(lr.ap()[0:1, :], lr.buf, TC)
        xof = ar.alloc(NT * TC, F32, "xof")
        P.dma("SQ", xof.ap(), dap("xT")[:, c * NT * TC:(c + 1) * NT * TC],
              reads=[db("xT")], writes=[xof.buf], nm="ld_xf32")
        for h2_ in range(2):
            aov = attn_o.ap()[:, h2_ * 8 * TC:(h2_ + 1) * 8 * TC].rearrange(
                "p (t m) -> p t m", m=TC)
            xov = xof.ap()[:, h2_ * 8 * TC:(h2_ + 1) * 8 * TC].rearrange(
                "p (t m) -> p t m", m=TC)
            op("vector", lambda v, _d=aov, _a=aov,
               _b=bm[0].ap()[:, 0:TC].unsqueeze(1)
               .broadcast_to([128, NBT, TC]):
               v.tensor_sub(_d, _a, _b),
               reads=[attn_o.buf, bm[1]], writes=[attn_o.buf], nm="r1")
            op("vector", lambda v, _d=aov, _a=aov,
               _b=br[0].ap()[:, 0:TC].unsqueeze(1)
               .broadcast_to([128, NBT, TC]):
               v.tensor_mul(_d, _a, _b),
               reads=[attn_o.buf, br[1]], writes=[attn_o.buf], nm="r2")
            op("vector", lambda v, _d=aov, _a=aov, _b=xov:
               v.tensor_add(_d, _a, _b),
               reads=[attn_o.buf, xof.buf], writes=[attn_o.buf], nm="r3")
        for h2 in range(2):
            cf = 2 * c + h2
            d = dap("x1_d")[:, cf * NT * TCF:(cf + 1) * NT * TCF].rearrange(
                "p (kt m) -> p kt m", m=TCF)
            s2_ = attn_o.ap().rearrange("p (kt m) -> p kt m", m=TC)[
                :, :, h2 * TCF:(h2 + 1) * TCF]
            P.dma("AQ", d, s2_, reads=[attn_o.buf], writes=[db("x1_d")],
                  nm="st_x1")
        ar.free(lm)
        ar.free(lr)
        ar.free(xof)
        ar.free(attn_o)

    if debug:
        dbb2 = ar.alloc(8192, F32, "dbgb2")
        for c in range(NCHF * NT * TCF // 8192):
            P.dma("SQ", dbb2.ap(), dap("x1_d")[:, c * 8192:(c + 1) * 8192],
                  reads=[db("x1_d")], writes=[dbb2.buf], nm="dbgld2")
            P.dma("AQ", dbg_x1.ap()[:, c * 8192:(c + 1) * 8192], dbb2.ap(),
                  reads=[dbb2.buf], writes=[dbg_x1_b], nm="dbgst2")
        ar.free(dbb2)

    drain_defer(len(DEFER))
    drain_defer(len(DEFER2), DEFER2)
    ar.free(w1pool[0])
    ar.free(w1pool[1])
    ar.free(w1pool[2])

    ar.free(ln1)
    ar.free(onesTC)
    ar.free(slab0)
    ar.free(slab1)
    ar.free(tmp1)
    ar.free(tmp2)
    ar.free(tmp3)
    ar.free(tmp4)
    ar.free(stats_m)
    ar.free(stats_r)

    sqf = None
    t_x1 = ar.alloc(TCF, BF16, "t_x1")
    t_s1 = ar.alloc(TCF, BF16, "t_s1")
    t_x3 = ar.alloc(TCF, BF16, "t_x3")
    out_acc = ar.alloc(NT * TCF, F32, "out_acc")
    tmpf = ar.alloc(TCF, F32, "tmpf")
    stk = ar.alloc(96, F32, "stk")
    rw_row = ar.alloc(TCF, F32, "rw_row")
    mk_bf = ar.alloc(2 * 256, BF16, "mk_bf")
    mexp = ar.alloc(4 * TCF, F32, "mexp")
    rwb = ar.alloc(2 * TCF, F32, "rwb")
    a1a = ar.alloc(2 * TCF, F32, "a1a")
    a3a = ar.alloc(2 * TCF, F32, "a3a")
    amk = [ar.alloc(2 * TCF, BF16, f"am{k}") for k in range(2)]
    am3k = [ar.alloc(2 * TCF, BF16, f"am3{k}") for k in range(2)]
    am2k = [ar.alloc(2 * TCF, BF16, f"am2{k}") for k in range(2)]
    b1s = ar.alloc(2 * 512, BF16, "b1s")
    b3s = ar.alloc(2 * 512, BF16, "b3s")
    b2s = ar.alloc(2 * 512, BF16, "b2s")
    a2s0 = ar.alloc(8 * 256, BF16, "a2s0")

    def tilef(a, i):
        return a.ap()[:, i * TCF:(i + 1) * TCF]

    for c in range(NCHF):
        ln2 = ar.alloc(NT * TCF, BF16, "ln2")
        sqf = ar.alloc(8 * TCF, BF16, "sqf")
        x1b = ar.alloc(NT * TCF, BF16, "x1b")
        P.dma("GQ", x1b.ap(), dap("x1_d")[:, c * NT * TCF:(c + 1) * NT * TCF],
              reads=[db("x1_d")], writes=[x1b.buf], nm="ld_x1b")
        lm = ar.alloc(TCF, F32, "lmf")
        lr = ar.alloc(TCF, F32, "lrf")
        ln_stats([tilef(x1b, i) for i in range(NT)], [x1b.buf], sqf, TCF,
                 lm.ap()[0:1, :], lr.ap()[0:1, :], lm.buf, lr.buf, True)
        bm = bcast_row(lm.ap()[0:1, :], lm.buf, TCF)
        br = bcast_row(lr.ap()[0:1, :], lr.buf, TCF)
        for i in range(NT):
            op("vector", lambda v, _d=tilef(ln2, i), _a=tilef(x1b, i),
               _b=bm[0].ap()[:, 0:TCF]: v.tensor_sub(_d, _a, _b),
               reads=[x1b.buf, bm[1]], writes=[ln2.buf], nm="l2a")
            op("vector", lambda v, _d=tilef(ln2, i), _a=tilef(ln2, i),
               _b=br[0].ap()[:, 0:TCF]: v.tensor_mul(_d, _a, _b),
               reads=[ln2.buf, br[1]], writes=[ln2.buf], nm="l2b")
        ar.free(lm)
        ar.free(lr)
        ar.free(x1b)
        ar.free(sqf)
        for tt in range(2):
            bank = next_bank()
            pairs = []
            for kt in range(NT):
                l = ln2.ap()[:, kt * TCF + tt * 128:kt * TCF + tt * 128 + 128]
                pairs.append((l, wg_cast.ap()[:, kt * E:(kt + 1) * E]))
            mm_group(bank, bank[0].ap()[:, 0:E], pairs,
                     [ln2.buf, wg_cast.buf], nm="logits")
            lgt = stk.ap()[:, 80:88]
            m1 = stk.ap()[:, 88:89]
            m2 = stk.ap()[:, 72:73]
            msk1 = stk.ap()[:, 0:8]
            msk2 = stk.ap()[:, 32:40]
            l2_ = stk.ap()[:, 40:48]
            dcol = stk.ap()[:, 73:74]
            rw2c = stk.ap()[:, 74:75]
            rw1c = stk.ap()[:, 64:65]
            P.plan("vector", lambda v, _d=lgt, _s=bank[0].ap()[:, 0:E]:
                   v.tensor_copy(_d, _s), reads=[bank[1]], writes=[stk.buf],
                   nm="lgc", selfsync=True)
            P.plan("vector", lambda v, _d=m1, _s=lgt:
                   v.reduce_max(_d, _s, AX),
                   reads=[stk.buf], writes=[stk.buf], nm="m1",
                   selfsync=True)
            P.plan("vector", lambda v, _d=msk1, _s=lgt, _m=m1:
                   v.tensor_single_scalar(_d, _s, _m, OP.is_ge),
                   reads=[stk.buf], writes=[stk.buf], nm="msk1",
                   selfsync=True)
            P.plan("vector", lambda v, _d=l2_, _a=msk1, _b=lgt:
                   v.scalar_tensor_tensor(_d, _a, -1e30, _b, OP.mult,
                                          OP.add),
                   reads=[stk.buf], writes=[stk.buf], nm="l2m",
                   selfsync=True)
            P.plan("vector", lambda v, _d=m2, _s=l2_:
                   v.reduce_max(_d, _s, AX),
                   reads=[stk.buf], writes=[stk.buf], nm="m2",
                   selfsync=True)
            P.plan("vector", lambda v, _d=msk2, _s=l2_, _m=m2:
                   v.tensor_single_scalar(_d, _s, _m, OP.is_ge),
                   reads=[stk.buf], writes=[stk.buf], nm="msk2",
                   selfsync=True)
            P.plan("vector", lambda v, _d=dcol, _a=m2, _b=m1:
                   v.tensor_sub(_d, _a, _b), reads=[stk.buf],
                   writes=[stk.buf], nm="d12", selfsync=True)
            op("scalar", lambda s, _d=rw2c, _s=dcol:
               s.activation(_d, _s, AF.Sigmoid),
               reads=[stk.buf], writes=[stk.buf], nm="sig")
            P.plan("vector", lambda v, _d=rw1c, _s=rw2c:
                   v.tensor_scalar(_d, _s, -1.0, 1.0, OP.mult, OP.add),
                   reads=[stk.buf], writes=[stk.buf], nm="rw1",
                   selfsync=True)
            if debug and c == 0:
                P.dma("AQ", dbg_stk.ap()[:, tt * 96:(tt + 1) * 96],
                      stk.ap()[:, 0:96], reads=[stk.buf],
                      writes=[dbg_stk_b], nm="dstk")
            bank2 = next_bank()
            op("tensor", lambda t, _o=bank2[0].ap()[0:96, 0:128],
               _i=stk.ap()[:, 0:96], _id=idn_sb.ap():
               t.transpose(_o, _i, _id),
               reads=[stk.buf, idn_sb.buf], writes=[bank2[1]], nm="sttr")
            op("vector", lambda v,
               _d=mk_bf.ap()[0:8, tt * 128:(tt + 1) * 128],
               _s=bank2[0].ap()[0:8, 0:128]: v.tensor_copy(_d, _s),
               reads=[bank2[1]], writes=[mk_bf.buf], nm="mk1")
            op("vector", lambda v,
               _d=mk_bf.ap()[0:8, 256 + tt * 128:256 + (tt + 1) * 128],
               _s=bank2[0].ap()[32:40, 0:128]: v.tensor_copy(_d, _s),
               reads=[bank2[1]], writes=[mk_bf.buf], nm="mk2")
            op("vector", lambda v,
               _d=rw_row.ap()[0:1, tt * 128:(tt + 1) * 128],
               _s=bank2[0].ap()[64:65, 0:128]: v.tensor_copy(_d, _s),
               reads=[bank2[1]], writes=[rw_row.buf], nm="rwrow")
        for k in range(2):
            for mt in range(2):
                bank = next_bank()
                mm_group(bank, bank[0].ap()[:, 0:TCF],
                         [(expm_sb.ap()[0:8, mt * 128:(mt + 1) * 128],
                           mk_bf.ap()[0:8, k * 256:k * 256 + TCF])],
                         [expm_sb.buf, mk_bf.buf], nm="mexp")
                op("vector", lambda v,
                   _d=mexp.ap()[:, (k * 2 + mt) * TCF:(k * 2 + mt + 1) * TCF],
                   _s=bank[0].ap()[:, 0:TCF]: v.tensor_copy(_d, _s),
                   reads=[bank[1]], writes=[mexp.buf], nm="mexpe")
        bankr = bcast_row(rw_row.ap()[0:1, 0:TCF], rw_row.buf, TCF)
        op("vector", lambda v, _d=rwb.ap()[:, 0:TCF],
           _s=bankr[0].ap()[:, 0:TCF]: v.tensor_copy(_d, _s),
           reads=[bankr[1]], writes=[rwb.buf], nm="rwbe")
        op("vector", lambda v, _d=rwb.ap()[:, TCF:2 * TCF],
           _s=rwb.ap()[:, 0:TCF]:
           v.tensor_scalar(_d, _s, -1.0, 1.0, OP.mult, OP.add),
           reads=[rwb.buf], writes=[rwb.buf], nm="rwb2")
        if debug and c == 0:
            P.dma("AQ", dbg_ffn.ap()[:, 0:1024], mexp.ap(),
                  reads=[mexp.buf], writes=[dbg_ffn_b], nm="dffn1")
            P.dma("AQ", dbg_ffn.ap()[:, 1024:1536], rwb.ap(),
                  reads=[rwb.buf], writes=[dbg_ffn_b], nm="dffn2")
        aslab = ar.alloc(NT * 256, BF16, "aslab")
        for aname, dsta in (("A1d", a1a), ("A3d", a3a)):
            load_image_slab(aname, aslab, 0, NT * 256, nm="ld_a")
            for mt in range(2):
                bank = next_bank()
                pairs = []
                for kt in range(NT):
                    l = aslab.ap()[:, kt * 256 + mt * 128:
                                   kt * 256 + mt * 128 + 128]
                    pairs.append((l, tilef(ln2, kt)))
                mm_group(bank, bank[0].ap()[:, 0:TCF], pairs,
                         [aslab.buf, ln2.buf], nm="a_all")
                op("vector", lambda v, _d=tilef(dsta, mt),
                   _s=bank[0].ap()[:, 0:TCF]: v.tensor_copy(_d, _s),
                   reads=[bank[1]], writes=[dsta.buf], nm="a_e")
        if debug and c == 0:
            P.dma("AQ", dbg_ffn.ap()[:, 1536:2048],
                  a1a.ap(), reads=[a1a.buf], writes=[dbg_ffn_b], nm="dffn3")
        ar.free(aslab)
        wslab0 = ar.alloc(NT * 512, BF16, "wslab0")
        wslab1 = ar.alloc(NT * 512, BF16, "wslab1")
        base1 = ar.alloc(NFT * TCF, BF16, "base1")
        base3 = ar.alloc(NFT * TCF, BF16, "base3")
        for wname, dstb in (("W1f_d", base1), ("W3f_d", base3)):
            for sl in range(14):
                slab = wslab0 if sl % 2 == 0 else wslab1
                load_image_slab(wname, slab, sl, 8192, nm="ld_wf")
                for mt in range(4):
                    gm = sl * 4 + mt
                    bank = next_bank()
                    pairs = []
                    for kt in range(NT):
                        l = slab.ap()[:, kt * 512 + mt * 128:
                                      kt * 512 + mt * 128 + 128]
                        pairs.append((l, tilef(ln2, kt)))
                    mm_group(bank, bank[0].ap()[:, 0:TCF], pairs,
                             [slab.buf, ln2.buf], nm="baseff")
                    op("scalar", lambda s, _d=tilef(dstb, gm),
                       _s=bank[0].ap()[:, 0:TCF]:
                       s.activation(_d, _s, AF.Copy),
                       reads=[bank[1]], writes=[dstb.buf], nm="basee")
        ar.free(ln2)
        ar.free(wslab0)
        ar.free(wslab1)
        x2k = [ar.alloc(NFT * TCF, BF16, f"x2_{k}") for k in range(2)]
        for k in range(2):
            for mt in range(2):
                mexp_ap = mexp.ap()[:, (k * 2 + mt) * TCF:
                                    (k * 2 + mt + 1) * TCF]
                op("vector", lambda v, _d=tilef(amk[k], mt),
                   _a=tilef(a1a, mt), _b=mexp_ap: v.tensor_mul(_d, _a, _b),
                   reads=[a1a.buf, mexp.buf], writes=[amk[k].buf], nm="am1")
                op("vector", lambda v, _d=tilef(am3k[k], mt),
                   _a=tilef(a3a, mt), _b=mexp_ap: v.tensor_mul(_d, _a, _b),
                   reads=[a3a.buf, mexp.buf], writes=[am3k[k].buf], nm="am3")
        if debug and c == 0:
            P.dma("GQ", dbg_f2.ap()[:, 1536:2048], amk[0].ap()[:, 0:512],
                  reads=[amk[0].buf], writes=[dbg_f2_b], nm="df2d")
        for msl in range(14):
            load_image_slab("B1d", b1s, msl, 1024, nm="ld_b1")
            load_image_slab("B3d", b3s, msl, 1024, nm="ld_b3")
            for mt4 in range(4):
                mt = msl * 4 + mt4
                for k in range(2):
                    bank = next_bank()
                    pairs = []
                    for kt in range(2):
                        l = b1s.ap()[:, kt * 512 + mt4 * 128:
                                     kt * 512 + mt4 * 128 + 128]
                        pairs.append((l, tilef(amk[k], kt)))
                    mm_group(bank, bank[0].ap()[:, 0:TCF], pairs,
                             [b1s.buf, amk[k].buf], nm="lora1")
                    op("vector", lambda v, _d=t_x1.ap()[:, 0:TCF],
                       _a=bank[0].ap()[:, 0:TCF], _b=tilef(base1, mt):
                       v.tensor_add(_d, _a, _b),
                       reads=[bank[1], base1.buf], writes=[t_x1.buf],
                       nm="x1k")
                    op("scalar", lambda s, _d=t_s1.ap()[:, 0:TCF],
                       _s=t_x1.ap()[:, 0:TCF]: s.activation(_d, _s, AF.Silu),
                       reads=[t_x1.buf], writes=[t_s1.buf], nm="silu")
                    bank3 = next_bank()
                    pairs = []
                    for kt in range(2):
                        l = b3s.ap()[:, kt * 512 + mt4 * 128:
                                     kt * 512 + mt4 * 128 + 128]
                        pairs.append((l, tilef(am3k[k], kt)))
                    mm_group(bank3, bank3[0].ap()[:, 0:TCF], pairs,
                             [b3s.buf, am3k[k].buf], nm="lora3")
                    op("vector", lambda v, _d=t_x3.ap()[:, 0:TCF],
                       _a=bank3[0].ap()[:, 0:TCF], _b=tilef(base3, mt):
                       v.tensor_add(_d, _a, _b),
                       reads=[bank3[1], base3.buf], writes=[t_x3.buf],
                       nm="x3k")
                    op("vector", lambda v, _d=tilef(x2k[k], mt),
                       _a=t_s1.ap()[:, 0:TCF], _b=t_x3.ap()[:, 0:TCF]:
                       v.tensor_mul(_d, _a, _b),
                       reads=[t_s1.buf, t_x3.buf], writes=[x2k[k].buf],
                       nm="x2k")
        if debug and c == 0:
            P.dma("GQ", dbg_f2.ap()[:, 0:256], t_x1.ap()[:, 0:TCF],
                  reads=[t_x1.buf], writes=[dbg_f2_b], nm="df3a")
            P.dma("GQ", dbg_f2.ap()[:, 256:512], t_s1.ap()[:, 0:TCF],
                  reads=[t_s1.buf], writes=[dbg_f2_b], nm="df3b")
        ar.free(base1)
        ar.free(base3)
        banks_a2 = [[next_bank() for _ in range(2)] for _ in range(2)]
        for sl in range(7):
            load_image_slab("A2d", a2s0, sl, 2048, nm="ld_a2")
            for k in range(2):
                for mt in range(2):
                    bank = banks_a2[k][mt]
                    pairs = []
                    for kt in range(8):
                        l = a2s0.ap()[:, kt * 256 + mt * 128:
                                      kt * 256 + mt * 128 + 128]
                        pairs.append((l, tilef(x2k[k], sl * 8 + kt)))
                    mm_group(bank, bank[0].ap()[:, 0:TCF], pairs,
                             [a2s0.buf, x2k[k].buf], nm="a2all",
                             start=(sl == 0), stop=(sl == 6))
        for k in range(2):
            for mt in range(2):
                bank = banks_a2[k][mt]
                mexp_ap = mexp.ap()[:, (k * 2 + mt) * TCF:
                                    (k * 2 + mt + 1) * TCF]
                op("vector", lambda v, _d=tilef(am2k[k], mt),
                   _a=bank[0].ap()[:, 0:TCF], _b=mexp_ap:
                   v.tensor_mul(_d, _a, _b),
                   reads=[bank[1], mexp.buf], writes=[am2k[k].buf], nm="am2")
        wslab0 = ar.alloc(NT * 512, BF16, "wslab0")
        wslab1 = ar.alloc(NT * 512, BF16, "wslab1")
        for gm in range(NT):
            slab = wslab0 if gm % 2 == 0 else wslab1
            load_image_slab("W2f_d", slab, gm, 7168, nm="ld_w2f")
            load_image_slab("B2d", b2s, gm, 256, nm="ld_b2")
            for k in range(2):
                bank = next_bank()
                pairs = []
                for kt in range(NFT):
                    l = slab.ap()[:, kt * 128:kt * 128 + 128]
                    pairs.append((l, tilef(x2k[k], kt)))
                for kt in range(2):
                    l = b2s.ap()[:, kt * 128:kt * 128 + 128]
                    pairs.append((l, tilef(am2k[k], kt)))
                mm_group(bank, bank[0].ap()[:, 0:TCF], pairs,
                         [slab.buf, b2s.buf, x2k[k].buf, am2k[k].buf],
                         nm="down")
                rw_ap = rwb.ap()[:, k * TCF:(k + 1) * TCF]
                if k == 0:
                    op("vector", lambda v, _d=tilef(out_acc, gm),
                       _a=bank[0].ap()[:, 0:TCF], _b=rw_ap:
                       v.tensor_mul(_d, _a, _b),
                       reads=[bank[1], rwb.buf], writes=[out_acc.buf],
                       nm="ok0")
                else:
                    op("vector", lambda v, _d=tmpf.ap()[:, 0:TCF],
                       _a=bank[0].ap()[:, 0:TCF], _b=rw_ap:
                       v.tensor_mul(_d, _a, _b),
                       reads=[bank[1], rwb.buf], writes=[tmpf.buf],
                       nm="ok1a")
                    op("vector", lambda v, _d=tilef(out_acc, gm),
                       _a=tilef(out_acc, gm), _b=tmpf.ap()[:, 0:TCF]:
                       v.tensor_add(_d, _a, _b),
                       reads=[out_acc.buf, tmpf.buf],
                       writes=[out_acc.buf], nm="ok1b")
        ar.free(x2k[0])
        ar.free(x2k[1])
        ar.free(wslab0)
        ar.free(wslab1)
        lm = ar.alloc(TCF, F32, "lmf2")
        lr = ar.alloc(TCF, F32, "lrf2")
        sqf = ar.alloc(8 * TCF, BF16, "sqf2")
        ln_stats([tilef(out_acc, i) for i in range(NT)], [out_acc.buf], sqf,
                 TCF, lm.ap()[0:1, :], lr.ap()[0:1, :], lm.buf, lr.buf, False)
        bm = bcast_row(lm.ap()[0:1, :], lm.buf, TCF)
        br = bcast_row(lr.ap()[0:1, :], lr.buf, TCF)
        xrf = ar.alloc(NT * TCF, F32, "xrf")
        P.dma("SQ", xrf.ap(), dap("x1_d")[:, c * NT * TCF:(c + 1) * NT * TCF],
              reads=[db("x1_d")], writes=[xrf.buf], nm="ld_x1f")
        for i in range(NT):
            op("vector", lambda v, _d=tilef(out_acc, i), _a=tilef(out_acc, i),
               _b=bm[0].ap()[:, 0:TCF]: v.tensor_sub(_d, _a, _b),
               reads=[out_acc.buf, bm[1]], writes=[out_acc.buf], nm="f1")
            op("vector", lambda v, _d=tilef(out_acc, i), _a=tilef(out_acc, i),
               _b=br[0].ap()[:, 0:TCF]: v.tensor_mul(_d, _a, _b),
               reads=[out_acc.buf, br[1]], writes=[out_acc.buf], nm="f2")
            op("vector", lambda v, _d=tilef(out_acc, i), _a=tilef(out_acc, i),
               _b=tilef(xrf, i): v.tensor_add(_d, _a, _b),
               reads=[out_acc.buf, xrf.buf], writes=[out_acc.buf], nm="f3")
        P.dma("AQ", out_h.ap()[:, c * NT * TCF:(c + 1) * NT * TCF],
              out_acc.ap(), reads=[out_acc.buf], writes=[out_b],
              nm="st_out")
        ar.free(lm)
        ar.free(lr)
        ar.free(xrf)
        ar.free(sqf)

    with contextlib.ExitStack() as stack:
        sems = {}
        for name in list(ENGINES) + list(QUEUES):
            sems[name] = stack.enter_context(nc.semaphore(f"sem_{name}"))
        for name in ENGINES:
            sems["vs_" + name] = stack.enter_context(
                nc.semaphore(f"sem_vs_{name}"))
        nonlocal_xch = stack.enter_context(nc.semaphore("sem_xch"))
        globals()["_XCH_SEM"] = nonlocal_xch
        P.finalize_and_emit(sems)
    return nc


def prep_inputs(inputs):
    g = inputs
    x = np.asarray(g["L0_attn_x"], np.float32).reshape(T, D)
    tabs = make_tables()
    W1 = np.asarray(g["L0_attn_W1"], np.float32)
    zero = np.zeros((1, HID), np.float32)
    w2a_slabs = [(kh * 32, 32, mt * 128, 128)
                 for mt in range(32) for kh in range(2)]
    xslabs = [(0, NT, c * TC, TC) for c in range(NCH)]
    com = {
        "Wq_T": np.ascontiguousarray(np.asarray(g["L0_attn_Wq"], np.float32).T),
        "Wk_T": np.ascontiguousarray(np.asarray(g["L0_attn_Wk"], np.float32).T),
        "Wv_T": np.ascontiguousarray(np.asarray(g["L0_attn_Wv"], np.float32).T),
        "Ws_T": np.ascontiguousarray(np.asarray(g["L0_attn_Ws"], np.float32).T),
        "Wt_T": np.ascontiguousarray(np.asarray(g["L0_attn_Wt"], np.float32).T),
        "W1e_a": np.ascontiguousarray(W1[0:NB]),
        "W1e_b": np.ascontiguousarray(np.vstack([zero, W1[2047:1024:-1]])),
        "W1e_c": np.ascontiguousarray(np.vstack([W1[1024:1025],
                                                 W1[2049:3072]])),
        "W1e_d": np.ascontiguousarray(np.vstack([zero, W1[4095:3072:-1]])),
        "W2a": image_tab(np.asarray(g["L0_attn_W2"], np.float32), w2a_slabs),
        "b1t": np.ascontiguousarray(
            np.asarray(g["L0_attn_b1"], np.float32).reshape(NHT, 128).T),
        "b2t": np.ascontiguousarray(
            np.asarray(g["L0_attn_b2"], np.float32).reshape(2 * NT, 128).T),
        "W1f_T": image_tab(np.asarray(g["L0_ffn_W1"], np.float32).T,
                           geom_mslabs(D, FF, 512)),
        "W3f_T": image_tab(np.asarray(g["L0_ffn_W3"], np.float32).T,
                           geom_mslabs(D, FF, 512)),
        "W2f_T": image_tab(np.asarray(g["L0_ffn_W2"], np.float32).T,
                           geom_mslabs(FF, D, 128)),
        "Wg_tiled": np.ascontiguousarray(
            np.asarray(g["L0_ffn_Wg"], np.float32).T.reshape(NT, 128, E)
            .transpose(1, 0, 2).reshape(128, NT * E)),
        "A1p": image_tab(np.asarray(g["L0_ffn_A1"], np.float32)
                         .reshape(ER, D).T, geom_mslabs(D, 256, 256)),
        "A3p": image_tab(np.asarray(g["L0_ffn_A3"], np.float32)
                         .reshape(ER, D).T, geom_mslabs(D, 256, 256)),
        "A2p": image_tab(np.asarray(g["L0_ffn_A2"], np.float32)
                         .reshape(ER, FF).T, geom_kslabs(FF, 256, 8)),
        "B1p": image_tab(np.asarray(g["L0_ffn_B1"], np.float32)
                         .transpose(0, 2, 1).reshape(ER, FF),
                         geom_mslabs(256, FF, 512)),
        "B3p": image_tab(np.asarray(g["L0_ffn_B3"], np.float32)
                         .transpose(0, 2, 1).reshape(ER, FF),
                         geom_mslabs(256, FF, 512)),
        "B2p": image_tab(np.asarray(g["L0_ffn_B2"], np.float32)
                         .transpose(0, 2, 1).reshape(ER, D),
                         geom_mslabs(256, D, 128)),
        "EXPM": expansion_mat(),
        "idn": np.eye(128, dtype=np.float32),
        "flipm": make_flipm(),
    }
    tabs2 = {
        "Ch": image_tab(tabs["Ch"], geom_mslabs(D, NB, 512)),
        "Sm": image_tab(tabs["Sm"], geom_mslabs(D, NB, 512)),

        "IC2": image_tab(tabs["IC2"], geom_mslabs(NB, D, 256)),
        "IS2": image_tab(tabs["IS2"], geom_mslabs(NB, D, 256)),

    }
    com.update(tabs2)
    magic = int(np.random.randint(1, 2**30))
    in_maps = []
    for c in range(NCORES):
        j = c % 4
        m = dict(com)
        m["xT"] = image_tab(
            np.ascontiguousarray(x[c * TLOC:(c + 1) * TLOC].T), xslabs)
        rsrc = c - 2 if j >= 2 else c
        m["xR_T"] = image_tab(
            np.ascontiguousarray(x[rsrc * TLOC:(rsrc + 1) * TLOC].T), xslabs)
        m["cid2"] = np.array([[c % 2, magic]], np.int32)
        w = np.zeros(4, np.float32)
        if j == 1:
            w[0] = 1.0
        elif j == 2:
            w[2] = 1.0
            w[3] = 1.0
        elif j == 3:
            w[0] = 1.0
            w[2] = 1.0
            w[3] = 1.0
        m["exw"] = np.ascontiguousarray(
            np.broadcast_to(w[None, :], (128, 4)).copy())
        in_maps.append(m)
    return in_maps


_prog_cache = {}


def kernel(**inputs):
    in_maps = prep_inputs(inputs)
    if "nc" not in _prog_cache:
        _prog_cache["nc"] = build_program()
    nc = _prog_cache["nc"]
    res = run_bass_kernel_spmd(nc, in_maps, core_ids=list(range(NCORES)))
    shards = []
    for r in res.results:
        img = np.asarray(r["outT"]).reshape(128, NCHF, NT, TCF)
        shards.append(img.transpose(2, 0, 1, 3).reshape(D, TLOC).T)
    out = np.concatenate(shards, axis=0).reshape(B, S, D)
    return np.ascontiguousarray(out.astype(np.float32))


if __name__ == "__main__":
    import time
    t0 = time.time()
    nc = build_program()
    print(f"build {time.time()-t0:.1f}s; "
          f"n_instructions={nc.n_instructions()}")


# version 26
# speedup vs baseline: 1.1283x; 1.0006x over previous
import sys
import contextlib
for _p in ("/opt/trn_rl_repo",):
    if _p not in sys.path:
        sys.path.append(_p)
import numpy as np
import ml_dtypes
import concourse.bass as bass
import concourse.mybir as mybir
from concourse.mybir import AluOpType as OP
from concourse.bass_utils import run_bass_kernel_spmd

F32, BF16 = mybir.dt.float32, mybir.dt.bfloat16
AF = mybir.ActivationFunctionType
BF = ml_dtypes.bfloat16
AX = mybir.AxisListType.X

B, S, D = 2, 8192, 2048
T = B * S
NCORES = 8
TLOC = T // NCORES
NB = D // 2
NT = D // 128
NBT = NB // 128
TC = 512
NCH = TLOC // TC
TCF = 256
NCHF = TLOC // TCF
FF = 7168
NFT = FF // 128
E, R = 8, 32
ER = E * R
HID = 8192
NHT = HID // 128
EPS = 1e-5


def make_tables():
    f = np.arange(D, dtype=np.float64)[:, None]
    bh = np.arange(NB, dtype=np.float64)[None, :]
    bf_ = np.arange(D, dtype=np.float64)[None, :]
    ah = 2 * np.pi * f * bh / D
    af = 2 * np.pi * f * bf_ / D
    Ch = np.cos(ah)
    Sm = -np.sin(ah)
    Sm[:, 0] = np.cos(np.pi * np.arange(D))
    Cf = np.cos(af)
    Sf = -np.sin(af)
    jh = np.arange(NB, dtype=np.float64)[:, None]
    n = np.arange(D, dtype=np.float64)[None, :]
    IC2 = 2 * np.cos(2 * np.pi * jh * n / D) / D
    IC2[0, :] = 1.0 / D
    IS2 = -2 * np.sin(2 * np.pi * jh * n / D) / D
    IS2[0, :] = np.cos(np.pi * np.arange(D)) / D
    jf = np.arange(D, dtype=np.float64)[:, None]
    ICf = np.cos(2 * np.pi * jf * n / D) / D
    ISf = -np.sin(2 * np.pi * jf * n / D) / D
    t = {"Ch": Ch, "Sm": Sm, "Cf": Cf, "Sf": Sf,
         "IC2": IC2, "IS2": IS2, "ICf": ICf, "ISf": ISf}
    return {k: np.ascontiguousarray(v.astype(BF)) for k, v in t.items()}


def make_flipm():
    idh = 0.5 * np.eye(128)
    xf = np.zeros((128, 128))
    for m in range(1, 128):
        xf[128 - m, m] = 0.5
    return np.ascontiguousarray(
        np.concatenate([idh, xf, -xf], axis=1).astype(BF))


def image_tab(tab, slabs):
    cols = sum(nkt * mc for (_, nkt, _, mc) in slabs)
    img = np.empty((128, cols), tab.dtype)
    off = 0
    for (kt0, nkt, m0, mc) in slabs:
        blk = tab[kt0 * 128:(kt0 + nkt) * 128, m0:m0 + mc]
        img[:, off:off + nkt * mc] = (
            blk.reshape(nkt, 128, mc).transpose(1, 0, 2).reshape(128, -1))
        off += nkt * mc
    return np.ascontiguousarray(img)


def geom_mslabs(K, M, mcols):
    return [(0, K // 128, m0, mcols) for m0 in range(0, M, mcols)]


def geom_kslabs(K, M, nkt):
    return [(k0, nkt, 0, M) for k0 in range(0, K // 128, nkt)]


def expansion_mat():
    m = np.zeros((8, 256), np.float32)
    for e in range(8):
        for r_ in range(32):
            m[e, (e % 4) * 32 + r_ + (e // 4) * 128] = 1.0
    return np.ascontiguousarray(m.astype(BF))


ENGINES = ("tensor", "vector", "scalar", "gpsimd", "sync")
QUEUES = {"SQ": "sync", "GQ": "gpsimd", "AQ": "scalar"}


class Buf:
    __slots__ = ("name", "wtok", "rtoks")

    def __init__(self, name):
        self.name = name
        self.wtok = None
        self.rtoks = []


class PlanOp:
    __slots__ = ("engine", "fn", "deps", "queue", "idx", "needs_inc", "nm",
                 "selfsync")

    def __init__(self, engine, fn, deps, queue, nm="", selfsync=False):
        self.engine = engine
        self.fn = fn
        self.deps = deps
        self.queue = queue
        self.idx = None
        self.needs_inc = False
        self.nm = nm
        self.selfsync = selfsync


class Prog:
    def __init__(self, nc):
        self.nc = nc
        self.ops = {e: [] for e in ENGINES}
        self.qcount = {q: 0 for q in QUEUES}

    def plan(self, engine, fn, reads=(), writes=(), queue=None, nm="",
             selfsync=False):
        deps = []
        for b in reads:
            if b.wtok is not None:
                deps.append(b.wtok)
        for b in writes:
            if b.wtok is not None:
                deps.append(b.wtok)
            deps.extend(b.rtoks)
        op_ = PlanOp(engine, fn, deps, queue, nm, selfsync)
        if queue is not None:
            self.qcount[queue] += 1
            op_.idx = self.qcount[queue]
        self.ops[engine].append(op_)
        for b in reads:
            b.rtoks.append(op_)
        for b in writes:
            b.wtok = op_
            b.rtoks = []
        return op_

    def dma(self, queue, dst_ap, src_ap, reads=(), writes=(), nm="", **kw):
        def fn(e, _dst=dst_ap, _src=src_ap, _kw=kw):
            return e.dma_start(_dst, _src, **_kw)
        return self.plan(QUEUES[queue], fn, reads=reads, writes=writes,
                         queue=queue, nm=nm or f"dma_{queue}")

    def finalize_and_emit(self, sems):
        for e in ENGINES:
            for op_ in self.ops[e]:
                for d in op_.deps:
                    if d.queue is None and d.engine != op_.engine:
                        d.needs_inc = True
        ecount = {e: 0 for e in ENGINES}
        for e in ENGINES:
            for op_ in self.ops[e]:
                if op_.queue is None and op_.needs_inc:
                    ecount[e] += 1
                    op_.idx = ecount[e]
        self.final_counts = dict(ecount)
        nc = self.nc

        def emit_stream(eng_obj, ename):
            high = {}
            vs_n = 0
            vs_sem = sems.get("vs_" + ename)
            for op_ in self.ops[ename]:
                for d in op_.deps:
                    if d.queue is not None:
                        sem, val = sems[d.queue], 16 * d.idx
                    elif d.engine != ename:
                        sem, val = sems[d.engine], d.idx
                    else:
                        continue
                    if high.get(sem.name, 0) < val:
                        eng_obj.wait_ge(sem, val)
                        high[sem.name] = val
                inst = op_.fn(eng_obj)
                if op_.queue is not None:
                    inst.then_inc(sems[op_.queue], 16)
                elif op_.selfsync:
                    vs_n += 1
                    inst.then_inc(vs_sem, 1)
                    eng_obj.wait_ge(vs_sem, vs_n)
                    if op_.needs_inc:
                        eng_obj.sem_inc(sems[ename], 1)
                elif op_.needs_inc:
                    inst.then_inc(sems[ename], 1)

        with nc.Block() as block:
            block.tensor(lambda t: emit_stream(t, "tensor"))
            block.vector(lambda v: emit_stream(v, "vector"))
            block.scalar(lambda s: emit_stream(s, "scalar"))
            block.gpsimd(lambda g: emit_stream(g, "gpsimd"))

            def sync_fn(s):
                emit_stream(s, "sync")
                for q in QUEUES:
                    if self.qcount[q]:
                        s.wait_ge(sems[q], 16 * self.qcount[q])
            block.sync(sync_fn)


class ArAlloc:
    __slots__ = ("t", "buf", "off", "size")

    def __init__(self, t, buf, off, size):
        self.t = t
        self.buf = buf
        self.off = off
        self.size = size

    def ap(self):
        return self.t.ap()


class Arena:
    def __init__(self, nc, base, top):
        self.nc = nc
        self.base = base
        self.top = top
        self.free_list = [[base, top, []]]
        self.n = 0
        self.peak = base
        self.active = {}

    def alloc(self, cols, dtype, name):
        esz = 4 if dtype == F32 else 2
        nbytes = (cols * esz + 511) & ~511
        for seg in self.free_list:
            if seg[1] - seg[0] >= nbytes:
                off = seg[0]
                preds = list(seg[2])
                seg[0] += nbytes
                if seg[0] >= seg[1]:
                    self.free_list.remove(seg)
                self.n += 1
                self.peak = max(self.peak, off + nbytes)
                for (ao, asz), anm in self.active.items():
                    assert off + nbytes <= ao or off >= ao + asz, (
                        f"arena overlap: {name}@{off}+{nbytes} vs "
                        f"{anm}@{ao}+{asz}")
                t = self.nc.alloc_sbuf_tensor_at(
                    f"ar_{name}_{self.n}", [128, cols], dtype, offset=off)
                buf = Buf(f"{name}@{off}")
                buf.rtoks = preds
                a = ArAlloc(t, buf, off, nbytes)
                self.active[(off, nbytes)] = name
                return a
        raise MemoryError(
            f"arena OOM for {name} ({nbytes}B); free="
            f"{[(s[1]-s[0]) for s in self.free_list]}")

    def free(self, a):
        assert (a.off, a.size) in self.active, \
            f"double free {a.buf.name}@{a.off}+{a.size}"
        for seg in self.free_list:
            assert a.off + a.size <= seg[0] or a.off >= seg[1], \
                f"free overlap {a.buf.name}@{a.off}+{a.size} vs seg {seg[:2]}"
        self.active.pop((a.off, a.size), None)
        toks = list(a.buf.rtoks)
        if a.buf.wtok is not None:
            toks.append(a.buf.wtok)
        self.free_list.append([a.off, a.off + a.size, toks])
        self.free_list.sort(key=lambda s: s[0])
        merged = []
        for seg in self.free_list:
            if merged and merged[-1][1] == seg[0]:
                merged[-1][1] = seg[1]
                merged[-1][2] = merged[-1][2] + seg[2]
            else:
                merged.append(seg)
        self.free_list = merged


def build_program(debug=False):
    nc = bass.Bass("TRN2", target_bir_lowering=False)
    P = Prog(nc)
    din = {}

    def ein(name, shape, dtype):
        din[name] = (nc.dram_tensor(name, shape, dtype, kind="ExternalInput"),
                     Buf("d_" + name))

    def internal(name, shape, dtype):
        din[name] = (nc.dram_tensor(name, shape, dtype, kind="Internal"),
                     Buf("i_" + name))

    ein("xT", (128, NCH * NT * TC), F32)
    for w in ("Wq_T", "Wk_T", "Wv_T", "Ws_T", "Wt_T"):
        ein(w, (D, D), F32)
    for w in ("W1e_a", "W1e_b", "W1e_c", "W1e_d"):
        ein(w, (NB, HID), F32)
    ein("W2a", (128, 64 * 4096), F32)
    ein("b1t", (128, NHT), F32)
    ein("b2t", (128, 2 * NT), F32)
    ein("W1f_T", (128, 14 * 8192), F32)
    ein("W3f_T", (128, 14 * 8192), F32)
    ein("W2f_T", (128, 16 * 7168), F32)
    ein("Wg_tiled", (128, NT * E), F32)
    ein("A1p", (128, NT * 256), F32)
    ein("A3p", (128, NT * 256), F32)
    ein("A2p", (128, 7 * 2048), F32)
    ein("B1p", (128, 14 * 1024), F32)
    ein("B3p", (128, 14 * 1024), F32)
    ein("B2p", (128, 16 * 256), F32)
    for tname, shp in (("Ch", (128, 2 * 8192)), ("Sm", (128, 2 * 8192)),
                       ("IC2", (128, 8 * 2048)), ("IS2", (128, 8 * 2048))):
        ein(tname, shp, BF16)
    ein("EXPM", (8, 256), BF16)
    ein("idn", (128, 128), F32)
    ein("flipm", (128, 384), BF16)
    ein("xR_T", (128, NCH * NT * TC), F32)
    ein("cid2", (1, 2), mybir.dt.int32)
    ein("exw", (128, 4), F32)

    out_h = nc.dram_tensor("outT", (128, NCHF * NT * TCF), F32,
                           kind="ExternalOutput")
    out_b = Buf("outT")
    if debug:
        dbg_x1 = nc.dram_tensor("dbg_x1", (128, NCHF * NT * TCF), F32,
                                kind="ExternalOutput")
        dbg_x1_b = Buf("dbg_x1")
        dbg_cum = nc.dram_tensor("dbg_cum", (128, NCH * NT * TC), F32,
                                 kind="ExternalOutput")
        dbg_cum_b = Buf("dbg_cum")
        dbg_sm = nc.dram_tensor("dbg_sm", (128, 64), F32,
                                kind="ExternalOutput")
        dbg_sm_b = Buf("dbg_sm")
        dbg_ln1 = nc.dram_tensor("dbg_ln1", (D, TC), F32,
                                 kind="ExternalOutput")
        dbg_ln1_b = Buf("dbg_ln1")
        dbg_q = nc.dram_tensor("dbg_q", (D, TC), F32, kind="ExternalOutput")
        dbg_q_b = Buf("dbg_q")
        dbg_p = nc.dram_tensor("dbg_p", (D, TC), F32, kind="ExternalOutput")
        dbg_p_b = Buf("dbg_p")
        dbg_ffn = nc.dram_tensor("dbg_ffn", (128, 2048), F32,
                                 kind="ExternalOutput")
        dbg_ffn_b = Buf("dbg_ffn")
        dbg_f2 = nc.dram_tensor("dbg_f2", (128, 2048), F32,
                                kind="ExternalOutput")
        dbg_f2_b = Buf("dbg_f2")
        dbg_stk = nc.dram_tensor("dbg_stk", (128, 192), F32,
                                 kind="ExternalOutput")
        dbg_stk_b = Buf("dbg_stk")
        dbg_u = nc.dram_tensor("dbg_u", (128, NT * TC), F32,
                               kind="ExternalOutput")
        dbg_u_b = Buf("dbg_u")
        dbg_hf = nc.dram_tensor("dbg_hf", (128, NT * TC), F32,
                                kind="ExternalOutput")
        dbg_hf_b = Buf("dbg_hf")
        dbg_trf = nc.dram_tensor("dbg_trf", (128, 2 * NT * TC), F32,
                                 kind="ExternalOutput")
        dbg_trf_b = Buf("dbg_trf")
        dbg_t = nc.dram_tensor("dbg_t", (128, 2 * NT * TC), F32,
                               kind="ExternalOutput")
        dbg_t_b = Buf("dbg_t")
        dbg_P = nc.dram_tensor("dbg_P", (128, 2 * NT * TC), F32,
                               kind="ExternalOutput")
        dbg_P_b = Buf("dbg_P")
        dbg_ao = nc.dram_tensor("dbg_ao", (128, NT * TC), F32,
                                kind="ExternalOutput")
        dbg_ao_b = Buf("dbg_ao")
        dbg_w1 = nc.dram_tensor("dbg_w1", (128, 32 * 4096), F32,
                                kind="ExternalOutput")
        dbg_w1_b = Buf("dbg_w1")
        dbg_h1 = nc.dram_tensor("dbg_h1", (128, NHT * TC), F32,
                                kind="ExternalOutput")
        dbg_h1_b = Buf("dbg_h1")

    for w in ("q", "k", "v", "s"):
        internal(f"fold_{w}C", (128, 4 * 4096), BF16)
        internal(f"fold_{w}S", (128, 4 * 4096), BF16)
    internal("fold_tC", (128, 4 * 4096), BF16)
    internal("fold_tS", (128, 4 * 4096), BF16)
    internal("W1eff", (128, 32 * 4096), BF16)
    internal("W2a_d", (128, 64 * 4096), BF16)
    internal("W1f_d", (128, 14 * 8192), BF16)
    internal("W3f_d", (128, 14 * 8192), BF16)
    internal("W2f_d", (128, 16 * 7168), BF16)
    internal("A1d", (128, NT * 256), BF16)
    internal("A3d", (128, NT * 256), BF16)
    internal("A2d", (128, 7 * 2048), BF16)
    internal("B1d", (128, 14 * 1024), BF16)
    internal("B3d", (128, 14 * 1024), BF16)
    internal("B2d", (128, 16 * 256), BF16)
    internal("cum_d", (128, NCH * NT * TC), F32)
    internal("x1_d", (128, NCHF * NT * TCF), F32)
    din["sh_x"] = (nc.dram_tensor("sh_x", (256, 32), F32, kind="Internal",
                                  addr_space="Shared"), Buf("i_sh_x"))
    din["sh_f"] = (nc.dram_tensor("sh_f", (2, 16), mybir.dt.int32,
                                  kind="Internal", addr_space="Shared"),
                   Buf("i_sh_f"))

    def dh(name):
        return din[name][0]

    def db(name):
        return din[name][1]

    def dap(name):
        return din[name][0].ap()

    ar = Arena(nc, (nc.sbuf_base + 511) & ~511, nc.sbuf_top & ~511)
    psum = [(nc.alloc_psum_tensor(f"ps{i}", [128, 512], F32), Buf(f"ps{i}"))
            for i in range(8)]
    ps_rr = [0]

    def next_bank():
        i = ps_rr[0] % 8
        ps_rr[0] += 1
        return psum[i]

    def op(engine, fn, reads=(), writes=(), nm=""):
        return P.plan(engine, fn, reads=reads, writes=writes, nm=nm)

    ones_bf = ar.alloc(16, BF16, "ones_bf")
    ones_f1 = ar.alloc(128, F32, "ones_f1")
    wg_cast = ar.alloc(NT * E, BF16, "wgc")
    b1_sb = ar.alloc(NHT, F32, "b1")
    b2_sb = ar.alloc(2 * NT, F32, "b2")
    cid_sb = ar.alloc(2, mybir.dt.int32, "cid_sb")
    mg_sb = ar.alloc(16, mybir.dt.int32, "mg_sb")
    fl_sb = ar.alloc(32, mybir.dt.int32, "fl_sb")
    ex_sb = ar.alloc(64, F32, "ex_sb")
    pollct = ar.alloc(2, mybir.dt.int32, "pollct")
    exw_sb = ar.alloc(4, F32, "exw_sb")
    idn_sb = ar.alloc(128, F32, "idn")
    flip_sb = ar.alloc(384, BF16, "flipm")
    expm_sb = ar.alloc(256, BF16, "expm")
    stats_m = ar.alloc(TLOC, F32, "stats_m")
    stats_r = ar.alloc(TLOC, F32, "stats_r")
    carry = ar.alloc(16, F32, "carry")
    rcarry = ar.alloc(16, F32, "rcarry")
    basep = ar.alloc(16, F32, "base")
    w1pool = [ar.alloc(512, F32, "w1pa"), ar.alloc(512, F32, "w1pb"),
              ar.alloc(512, BF16, "w1po")]
    sr_sm = ar.alloc(512, F32, "sr_sm")
    sr_s2 = ar.alloc(512, F32, "sr_s2")
    sr_tmp = ar.alloc(512, F32, "sr_tmp")

    P.dma("SQ", b1_sb.ap(), dap("b1t"), reads=[db("b1t")], writes=[b1_sb.buf])
    P.dma("SQ", b2_sb.ap(), dap("b2t"), reads=[db("b2t")], writes=[b2_sb.buf])
    P.dma("SQ", cid_sb.ap()[0:1, :], dap("cid2"), reads=[db("cid2")],
          writes=[cid_sb.buf])
    P.dma("SQ", exw_sb.ap(), dap("exw"), reads=[db("exw")],
          writes=[exw_sb.buf])
    P.dma("SQ", idn_sb.ap(), dap("idn"), reads=[db("idn")],
          writes=[idn_sb.buf])
    P.dma("SQ", flip_sb.ap(), dap("flipm"), reads=[db("flipm")],
          writes=[flip_sb.buf])
    P.dma("SQ", expm_sb.ap()[0:8, :], dap("EXPM"), reads=[db("EXPM")],
          writes=[expm_sb.buf])
    P.dma("GQ", wg_cast.ap(), dap("Wg_tiled"), reads=[db("Wg_tiled")],
          writes=[wg_cast.buf])
    eps_sb = ar.alloc(2, F32, "eps")
    op("vector", lambda v: v.memset(eps_sb.ap()[0:1, 0:1], EPS),
       writes=[eps_sb.buf])
    op("vector", lambda v: v.memset(ones_bf.ap(), 1.0), writes=[ones_bf.buf])
    op("vector", lambda v: v.memset(ones_f1.ap()[0:1, :], 1.0),
       writes=[ones_f1.buf])
    op("vector", lambda v: v.memset(carry.ap(), 0.0), writes=[carry.buf])
    op("vector", lambda v: v.memset(rcarry.ap(), 0.0), writes=[rcarry.buf])
    op("vector", lambda v, _d=None: v.tensor_copy(
        mg_sb.ap()[0:1, :],
        cid_sb.ap()[0:1, 1:2].broadcast_to([1, 16])),
        reads=[cid_sb.buf], writes=[mg_sb.buf], nm="mgrep")

    def mm_group(bank, bank_ap, pairs, reads, nm="mm", start=True, stop=True):
        def fn(t, _pairs=tuple(pairs), _ap=bank_ap, _sa=start, _so=stop):
            n = len(_pairs)
            inst = None
            for i, (l, r_) in enumerate(_pairs):
                inst = t.matmul(_ap, l, r_, start=(_sa and i == 0),
                                stop=(_so and i == n - 1))
            return inst
        return op("tensor", fn, reads=list(reads), writes=[bank[1]], nm=nm)

    def load_image_slab(dramname, dst, s, W, nm="ldw"):
        return P.dma("SQ", dst.ap()[:, 0:W],
                     dap(dramname)[:, s * W:(s + 1) * W],
                     reads=[db(dramname)], writes=[dst.buf], nm=nm)

    def ln_stats(src_tiles, src_bufs, scratch, w, m_ap, r_ap, mbuf, rbuf,
                 src_is_bf16):
        nt = len(src_tiles)
        if src_is_bf16:
            bank = next_bank()
            mm_group(bank, bank[0].ap()[0:1, 0:w],
                     [(ones_bf.ap()[:, 0:1], a) for a in src_tiles],
                     list(src_bufs) + [ones_bf.buf], nm="ln_sum")
        else:
            bank = next_bank()
            for hf in range(2):
                for i in range(nt // 2):
                    g = hf * (nt // 2) + i
                    op("scalar", lambda s,
                       _d=scratch.ap()[:, i * w:(i + 1) * w],
                       _s=src_tiles[g]: s.activation(_d, _s, AF.Copy),
                       reads=list(src_bufs), writes=[scratch.buf],
                       nm="lncast")
                mm_group(bank, bank[0].ap()[0:1, 0:w],
                         [(ones_bf.ap()[:, 0:1],
                           scratch.ap()[:, i * w:(i + 1) * w])
                          for i in range(nt // 2)],
                         [scratch.buf, ones_bf.buf], nm="ln_sum",
                         start=(hf == 0), stop=(hf == 1))
        op("scalar", lambda s, _d=sr_sm.ap()[0:1, 0:w],
           _s=bank[0].ap()[0:1, 0:w]:
           s.activation(_d, _s, AF.Copy, scale=1.0 / D),
           reads=[bank[1]], writes=[sr_sm.buf], nm="ln_m")
        bank2 = next_bank()
        for hf in range(2):
            for i in range(nt // 2):
                g = hf * (nt // 2) + i
                op("scalar", lambda s, _d=scratch.ap()[:, i * w:(i + 1) * w],
                   _s=src_tiles[g]: s.activation(_d, _s, AF.Square),
                   reads=list(src_bufs), writes=[scratch.buf], nm="lnsq")
            mm_group(bank2, bank2[0].ap()[0:1, 0:w],
                     [(ones_bf.ap()[:, 0:1],
                       scratch.ap()[:, i * w:(i + 1) * w])
                      for i in range(nt // 2)],
                     [scratch.buf, ones_bf.buf], nm="ln_sum2",
                     start=(hf == 0), stop=(hf == 1))
        op("scalar", lambda s, _d=sr_s2.ap()[0:1, 0:w],
           _s=bank2[0].ap()[0:1, 0:w]:
           s.activation(_d, _s, AF.Copy, scale=1.0 / D),
           reads=[bank2[1]], writes=[sr_s2.buf], nm="ln_e2")
        op("vector", lambda v, _t=sr_tmp.ap()[0:1, 0:w],
           _m=sr_sm.ap()[0:1, 0:w]: v.tensor_mul(_t, _m, _m),
           reads=[sr_sm.buf], writes=[sr_tmp.buf], nm="ln_msq")
        op("vector", lambda v, _t=sr_tmp.ap()[0:1, 0:w],
           _a=sr_s2.ap()[0:1, 0:w], _b=sr_tmp.ap()[0:1, 0:w]:
           v.tensor_sub(_t, _a, _b),
           reads=[sr_s2.buf, sr_tmp.buf], writes=[sr_tmp.buf], nm="ln_var")
        op("scalar", lambda s, _t=sr_tmp.ap()[0:1, 0:w],
           _e=eps_sb.ap()[0:1, 0:1]:
           s.activation(_t, _t, AF.Sqrt, bias=_e),
           reads=[sr_tmp.buf, eps_sb.buf], writes=[sr_tmp.buf], nm="ln_sqrt")
        P.plan("vector", lambda v, _d=r_ap, _s=sr_tmp.ap()[0:1, 0:w]:
               v.reciprocal(_d, _s), reads=[sr_tmp.buf], writes=[rbuf],
               nm="ln_rs", selfsync=True)
        op("vector", lambda v, _d=m_ap, _s=sr_sm.ap()[0:1, 0:w]:
           v.tensor_copy(_d, _s), reads=[sr_sm.buf], writes=[mbuf], nm="ln_mc")

    def bcast_row(row_ap, rowbuf, w, bf=False):
        bank = next_bank()
        lo = ones_f1
        mm_group(bank, bank[0].ap()[:, 0:w],
                 [(lo.ap()[0:1, 0:128], row_ap)],
                 [rowbuf, lo.buf], nm="bcast")
        return bank

    bcast_rr = [0]

    def bcast_row_sb(row_ap, rowbuf, w):
        bank = bcast_row(row_ap, rowbuf, w)
        dst = sr_tmp if bcast_rr[0] % 2 == 0 else sr_sm
        bcast_rr[0] += 1
        op("scalar", lambda s, _d=dst.ap()[:, 0:w],
           _s=bank[0].ap()[:, 0:w]: s.activation(_d, _s, AF.Copy),
           reads=[bank[1]], writes=[dst.buf], nm="bcastc")
        return (dst, dst.buf)

    wt_sb = ar.alloc(NT * D, BF16, "wt_sb")
    tab_sb = ar.alloc(NT * D, BF16, "tab_sb")
    stg = ar.alloc(2048, BF16, "stage")

    def fold_one(wname, tname, outname, ncols, reload_w):
        if reload_w:
            src = dap(wname).rearrange("(kt p) m -> p kt m", p=128)
            P.dma("GQ", wt_sb.ap().rearrange("p (kt m) -> p kt m", m=D), src,
                  reads=[db(wname)], writes=[wt_sb.buf], nm=f"cWT_{wname}")
        P.dma("SQ", tab_sb.ap()[:, 0:NT * ncols], dap(tname),
              reads=[db(tname)], writes=[tab_sb.buf], nm=f"ldtab_{tname}")
        for mt in range(NT):
            for nch0 in range(0, ncols, 512):
                sl = nch0 // 512
                bank = next_bank()
                pairs = []
                for kt in range(NT):
                    l = wt_sb.ap()[:, kt * D + mt * 128:kt * D + mt * 128 + 128]
                    r_ = tab_sb.ap()[:, sl * 8192 + kt * 512:
                                     sl * 8192 + kt * 512 + 512]
                    pairs.append((l, r_))
                mm_group(bank, bank[0].ap(), pairs,
                         [wt_sb.buf, tab_sb.buf], nm=f"fo_{outname}")
                op("scalar", lambda s, _d=stg.ap()[:, nch0:nch0 + 512],
                   _s=bank[0].ap(): s.activation(_d, _s, AF.Copy),
                   reads=[bank[1]], writes=[stg.buf], nm="fo_e")
            nsl = ncols // 256
            d = dap(outname)[:, 0:nsl * 4096].rearrange(
                "p (s km) -> p s km", km=4096)[:, :, mt * 256:(mt + 1) * 256]
            s_ = stg.ap()[:, 0:ncols].rearrange("p (s m) -> p s m", m=256)
            P.dma("AQ", d, s_, reads=[stg.buf], writes=[db(outname)],
                  nm=f"st_{outname}")

    for w, wn in (("q", "Wq_T"), ("k", "Wk_T"), ("v", "Wv_T"), ("s", "Ws_T")):
        fold_one(wn, "Ch", f"fold_{w}C", NB, True)
        fold_one(wn, "Sm", f"fold_{w}S", NB, False)
    fold_one("Wt_T", "Ch", "fold_tC", NB, True)
    fold_one("Wt_T", "Sm", "fold_tS", NB, False)
    ar.free(wt_sb)
    ar.free(tab_sb)
    ar.free(stg)

    DEFER = []
    DEFER2 = []

    def drain_defer(n, lst=None):
        lst = DEFER if lst is None else lst
        for _ in range(n):
            if lst:
                lst.pop(0)()

    def emit_w1eff_fold():
        QW = 512
        pa, pb, po = w1pool
        for grp, (na, nb_, sgn) in enumerate((("W1e_a", "W1e_b", OP.add),
                                              ("W1e_c", "W1e_d",
                                               OP.subtract))):
            for kt in range(NBT):
                for hh in range(HID // QW):

                    def piece(na=na, nb_=nb_, sgn=sgn, kt=kt, hh=hh,
                              grp=grp):
                        P.dma("SQ", pa.ap(),
                              dap(na)[kt * 128:(kt + 1) * 128,
                                      hh * QW:(hh + 1) * QW],
                              reads=[db(na)], writes=[pa.buf], nm="ld_w1a")
                        P.dma("SQ", pb.ap(),
                              dap(nb_)[kt * 128:(kt + 1) * 128,
                                       hh * QW:(hh + 1) * QW],
                              reads=[db(nb_)], writes=[pb.buf], nm="ld_w1b")
                        op("vector", lambda v, _o=po.ap(), _a=pa.ap(),
                           _b=pb.ap(), _s=sgn:
                           v.tensor_tensor(_o, _a, _b, _s),
                           reads=[pa.buf, pb.buf], writes=[po.buf],
                           nm="w1fold")
                        ktg = grp * NBT + kt
                        d = dap("W1eff").rearrange(
                            "p (s km) -> p s km", km=4096)[
                            :, hh * (QW // 256):(hh + 1) * (QW // 256),
                            ktg * 256:(ktg + 1) * 256]
                        P.dma("AQ", d,
                              po.ap().rearrange("p (s m) -> p s m", m=256),
                              reads=[po.buf], writes=[db("W1eff")],
                              nm="st_w1eff")
                    DEFER.append(piece)

    def emit_converts():
        for s_, d_ in (("W2a", "W2a_d"), ("W1f_T", "W1f_d"),
                       ("W3f_T", "W3f_d"), ("W2f_T", "W2f_d"),
                       ("A1p", "A1d"), ("A3p", "A3d"), ("A2p", "A2d"),
                       ("B1p", "B1d"), ("B3p", "B3d"), ("B2p", "B2d")):
            C = dh(s_).shape[1]
            step = 32768
            for c0 in range(0, C, step):
                w = min(step, C - c0)

                def piece(s_=s_, d_=d_, c0=c0, w=w):
                    P.dma("GQ", dap(d_)[:, c0:c0 + w], dap(s_)[:, c0:c0 + w],
                          reads=[db(s_)], writes=[db(d_)], nm=f"cv_{s_}")
                (DEFER if s_ == "W2a" else DEFER2).append(piece)

    ln1 = ar.alloc(NT * TC, BF16, "ln1")
    slab0 = ar.alloc(NT * 256, BF16, "slab0")
    slab1 = ar.alloc(NT * 256, BF16, "slab1")
    onesTC = ar.alloc(TC, F32, "onesTC")
    tmp1 = ar.alloc(TC, F32, "tmp1")
    tmp2 = ar.alloc(TC, F32, "tmp2")
    tmp34 = [None, None]
    op("vector", lambda v: v.memset(onesTC.ap(), 1.0), writes=[onesTC.buf])

    def tile(a, i, w=TC):
        return a.ap()[:, i * w:(i + 1) * w]

    xbf_bufs = [ar.alloc(NT * TC, BF16, "x_bf0")]
    xseq = ([("xR_T", c) for c in range(NCH)] +
            [("xT", c) for c in range(NCH)] +
            [("xT", c) for c in range(NCH)])
    xseq_i = [0]

    def issue_xload(i):
        if i >= len(xseq):
            return
        xsrc, c = xseq[i]
        xb = xbf_bufs[0]
        P.dma("GQ", xb.ap(), dap(xsrc)[:, c * NT * TC:(c + 1) * NT * TC],
              reads=[db(xsrc)], writes=[xb.buf], nm="ld_xbf")

    issue_xload(0)

    def compute_ln1(c, want_stats, sq_scratch, xsrc="xT"):
        i = xseq_i[0]
        assert xseq[i] == (xsrc, c), (xseq[i], xsrc, c)
        xseq_i[0] += 1
        x_bf = xbf_bufs[0]
        m_ap = stats_m.ap()[0:1, c * TC:(c + 1) * TC]
        r_ap = stats_r.ap()[0:1, c * TC:(c + 1) * TC]
        if want_stats:
            ln_stats([tile(x_bf, i2) for i2 in range(NT)], [x_bf.buf],
                     sq_scratch, TC, m_ap, r_ap, stats_m.buf, stats_r.buf,
                     True)
        bm = bcast_row_sb(m_ap, stats_m.buf, TC)
        br = bcast_row_sb(r_ap, stats_r.buf, TC)
        lnv = ln1.ap().rearrange("p (t m) -> p t m", m=TC)
        xv = x_bf.ap().rearrange("p (t m) -> p t m", m=TC)
        op("vector", lambda v, _d=lnv, _a=xv,
           _b=bm[0].ap()[:, 0:TC].unsqueeze(1).broadcast_to([128, NT, TC]):
           v.tensor_sub(_d, _a, _b),
           reads=[x_bf.buf, bm[1]], writes=[ln1.buf], nm="ln1a")
        op("vector", lambda v, _d=lnv, _a=lnv,
           _b=br[0].ap()[:, 0:TC].unsqueeze(1).broadcast_to([128, NT, TC]):
           v.tensor_mul(_d, _a, _b),
           reads=[ln1.buf, br[1]], writes=[ln1.buf], nm="ln1b")
        issue_xload(i + 1)

    def proj_plane(foldname, outplane, nm):
        for q4 in range(4):
            slab = slab0 if q4 % 2 == 0 else slab1
            load_image_slab(foldname, slab, q4, 4096, nm=f"l{nm}")
            for mt in range(2):
                bank = next_bank()
                pairs = []
                for kt in range(NT):
                    l = slab.ap()[:, kt * 256 + mt * 128:
                                  kt * 256 + mt * 128 + 128]
                    pairs.append((l, tile(ln1, kt)))
                mm_group(bank, bank[0].ap(), pairs, [slab.buf, ln1.buf], nm=nm)
                gi = q4 * 2 + mt
                op("scalar", lambda s, _d=tile(outplane, gi),
                   _s=bank[0].ap(): s.activation(_d, _s, AF.Copy),
                   reads=[bank[1]], writes=[outplane.buf], nm=nm + "e")

    def pk_mul_planes(aR, aI, bR, bI, oR, oI, conj, nm):
        s1 = OP.add if conj else OP.subtract
        s2 = OP.subtract if conj else OP.add
        for i in range(NBT):
            op("vector", lambda v, _d=tmp1.ap()[:, 0:TC], _a=tile(aR, i),
               _b=tile(bR, i): v.tensor_mul(_d, _a, _b),
               reads=[aR.buf, bR.buf], writes=[tmp1.buf], nm=nm + "1")
            op("vector", lambda v, _d=tmp2.ap()[:, 0:TC], _a=tile(aI, i),
               _b=tile(bI, i): v.tensor_mul(_d, _a, _b),
               reads=[aI.buf, bI.buf], writes=[tmp2.buf], nm=nm + "2")
            op("vector", lambda v, _d=tile(oR, i), _a=tmp1.ap()[:, 0:TC],
               _b=tmp2.ap()[:, 0:TC], _s=s1: v.tensor_tensor(_d, _a, _b, _s),
               reads=[tmp1.buf, tmp2.buf], writes=[oR.buf], nm=nm + "3")
            if i == 0:
                op("vector", lambda v, _d=tile(oR, 0)[0:1, :],
                   _s=tmp1.ap()[0:1, 0:TC]: v.tensor_copy(_d, _s),
                   reads=[tmp1.buf], writes=[oR.buf], nm=nm + "f1")
                op("vector", lambda v, _d=tmp1.ap()[0:1, 0:TC],
                   _s=tmp2.ap()[0:1, 0:TC]: v.tensor_copy(_d, _s),
                   reads=[tmp2.buf], writes=[tmp1.buf], nm=nm + "f2")
            op("vector", lambda v, _d=tmp2.ap()[:, 0:TC], _a=tile(aR, i),
               _b=tile(bI, i): v.tensor_mul(_d, _a, _b),
               reads=[aR.buf, bI.buf], writes=[tmp2.buf], nm=nm + "4")
            op("vector", lambda v, _d=tile(oI, i), _a=tile(aI, i),
               _b=tile(bR, i): v.tensor_mul(_d, _a, _b),
               reads=[aI.buf, bR.buf], writes=[oI.buf], nm=nm + "5")
            op("vector", lambda v, _d=tile(oI, i), _a=tmp2.ap()[:, 0:TC],
               _b=tile(oI, i), _s=s2: v.tensor_tensor(_d, _a, _b, _s),
               reads=[tmp2.buf, oI.buf], writes=[oI.buf], nm=nm + "6")
            if i == 0:
                op("vector", lambda v, _d=tile(oI, 0)[0:1, :],
                   _s=tmp1.ap()[0:1, 0:TC]: v.tensor_copy(_d, _s),
                   reads=[tmp1.buf], writes=[oI.buf], nm=nm + "f3")

    sq_g = ar.alloc(8 * TC, BF16, "sq_g")
    plA = ar.alloc(NBT * TC, F32, "plA")
    plB = ar.alloc(NBT * TC, F32, "plB")
    plC = ar.alloc(NBT * TC, F32, "plC")
    plD = ar.alloc(NBT * TC, F32, "plD")
    plE = ar.alloc(NBT * TC, F32, "plE")
    plF = ar.alloc(NBT * TC, F32, "plF")

    def p1_chunk(c, xsrc, carrybuf, full, tap=False):
        compute_ln1(c, True, sq_g, xsrc)
        if tap:
            P.dma("GQ", dbg_ln1.ap().rearrange("(kt p) m -> p kt m", p=128),
                  ln1.ap().rearrange("p (kt m) -> p kt m", m=TC),
                  reads=[ln1.buf], writes=[dbg_ln1_b], nm="tap_ln1")
        proj_plane("fold_qC", plA, "pqC")
        proj_plane("fold_qS", plB, "pqS")
        proj_plane("fold_kC", plC, "pkC")
        proj_plane("fold_kS", plD, "pkS")
        if tap:
            for half, pl in ((0, plA), (1, plB)):
                P.dma("AQ", dbg_q.ap()[half * NB:(half + 1) * NB, :]
                      .rearrange("(kt p) m -> p kt m", p=128),
                      pl.ap().rearrange("p (kt m) -> p kt m", m=TC),
                      reads=[pl.buf], writes=[dbg_q_b], nm="tap_q")
        pk_mul_planes(plA, plB, plC, plD, plE, plF, True, "m")
        proj_plane("fold_vC", plA, "pvC")
        proj_plane("fold_vS", plB, "pvS")
        pk_mul_planes(plE, plF, plA, plB, plC, plD, False, "p")
        if tap:
            for half, pl in ((0, plC), (1, plD)):
                P.dma("AQ", dbg_p.ap()[half * NB:(half + 1) * NB, :]
                      .rearrange("(kt p) m -> p kt m", p=128),
                      pl.ap().rearrange("p (kt m) -> p kt m", m=TC),
                      reads=[pl.buf], writes=[dbg_p_b], nm="tap_p")
        for half, (src, dstp) in enumerate(((plC, plE), (plD, plF))):
            for i in range(NBT):
                ci = half * 8 + i
                last = (half == 1 and i == NBT - 1)
                P.plan("vector", lambda v, _d=tile(dstp, i), _o=onesTC.ap(),
                       _s=tile(src, i), _in=carrybuf.ap()[:, ci:ci + 1]:
                       v.tensor_tensor_scan(_d, _o, _s, _in, OP.mult, OP.add),
                       reads=[src.buf, carrybuf.buf, onesTC.buf],
                       writes=[dstp.buf], nm="scan", selfsync=last)
        for half, dstp in ((0, plE), (1, plF)):
            P.plan("vector", lambda v,
                   _d=carrybuf.ap()[:, half * 8:(half + 1) * 8]
                   .rearrange("p (t m) -> p t m", m=1),
                   _s=dstp.ap().rearrange("p (t m) -> p t m", m=TC)
                   [:, :, TC - 1:TC]: v.tensor_copy(_d, _s),
                   reads=[dstp.buf], writes=[carrybuf.buf], nm="carry",
                   selfsync=True)
        if not full:
            return
        for tabname, dstp in (("Ch", plE), ("Sm", plF)):
            for s8 in range(2):
                P.dma("SQ", slab0.ap()[:, 0:4096],
                      dap(tabname)[:, s8 * 8192:s8 * 8192 + 4096],
                      reads=[db(tabname)], writes=[slab0.buf], nm="lfft0")
                P.dma("SQ", slab1.ap()[:, 0:4096],
                      dap(tabname)[:, s8 * 8192 + 4096:(s8 + 1) * 8192],
                      reads=[db(tabname)], writes=[slab1.buf], nm="lfft1")
                for mt in range(4):
                    bank = next_bank()
                    pairs = []
                    for kt in range(NT):
                        sb = slab0 if kt < 8 else slab1
                        l = sb.ap()[:, (kt % 8) * 512 + mt * 128:
                                    (kt % 8) * 512 + mt * 128 + 128]
                        pairs.append((l, tile(ln1, kt)))
                    mm_group(bank, bank[0].ap(), pairs,
                             [slab0.buf, slab1.buf, ln1.buf], nm="fft")
                    gi = s8 * 4 + mt
                    op("vector", lambda v, _d=tile(dstp, gi),
                       _a=bank[0].ap(), _b=tile(dstp, gi):
                       v.tensor_add(_d, _a, _b),
                       reads=[bank[1], dstp.buf], writes=[dstp.buf],
                       nm="fftadd")
        for half, pl in ((0, plE), (1, plF)):
            d = dap("cum_d")[:, c * NT * TC + half * NBT * TC:
                             c * NT * TC + (half + 1) * NBT * TC]
            P.dma("AQ", d, pl.ap(), reads=[pl.buf], writes=[db("cum_d")],
                  nm="st_cum")

    emit_w1eff_fold()
    emit_converts()
    NDEF0 = len(DEFER)
    NDEF2 = len(DEFER2)

    for c in range(NCH):
        p1_chunk(c, "xR_T", rcarry, False)
        drain_defer(NDEF0 // 9 + 1)
    for c in range(NCH):
        p1_chunk(c, "xT", carry, True, tap=(debug and c == 0))
        drain_defer(NDEF0 // 9 + 1)

    ar.free(sq_g)
    ar.free(plA)
    ar.free(plB)
    ar.free(plC)
    ar.free(plD)
    ar.free(plE)
    ar.free(plF)
    tmp3 = ar.alloc(TC, F32, "tmp3")
    tmp4 = ar.alloc(TC, F32, "tmp4")

    def xch_fn(g):
        xch_sem = globals()["_XCH_SEM"]
        cval = g.value_load(cid_sb.ap()[0:1, 0:1])
        sh = dap("sh_x").rearrange("(c p) m -> c p m", p=128)
        for i in g.Switch(cval, 2):
            g.dma_start(sh[i][:, 0:16], carry.ap()).then_inc(xch_sem, 16)
        g.wait_ge(xch_sem, 16)
        for i in g.Switch(cval, 2):
            g.dma_start(sh[i][:, 16:32], rcarry.ap()).then_inc(xch_sem, 16)
        g.wait_ge(xch_sem, 32)
        for i in g.Switch(cval, 2):
            g.dma_start(dap("sh_f")[i:i + 1, :],
                        mg_sb.ap()[0:1, :]).then_inc(xch_sem, 16)
        g.wait_ge(xch_sem, 48)
        with g.register("nd") as nd, g.register("cnt") as cnt, \
                g.register("acc") as acc:
            mgv = g.value_load(cid_sb.ap()[0:1, 1:2])
            g.reg_mov(nd, 1)
            g.reg_mov(cnt, 48)
            with g.While(nd):
                g.reg_add(cnt, cnt, 16)
                g.dma_start(fl_sb.ap()[0:1, :],
                            dap("sh_f").rearrange("a b -> (a b)")
                            .rearrange("(one f) -> one f", one=1)
                            ).then_inc(xch_sem, 16)
                g.wait_ge(xch_sem, cnt)
                g.reg_mov(acc, 0)
                with g.register("ecmp") as e:
                    for r_ in range(2):
                        v = g.value_load(
                            fl_sb.ap()[0:1, r_ * 16:r_ * 16 + 1])
                        g.reg_alu(e, v, mgv, mybir.AluOpType.is_equal)
                        g.reg_add(acc, acc, e)
                g.reg_alu(nd, acc, 2, mybir.AluOpType.not_equal)
            g.reg_add(cnt, cnt, 16)
            g.dma_start(ex_sb.ap().rearrange("p (c m) -> p c m", m=32),
                        dap("sh_x").rearrange("(c p) m -> p c m", p=128)
                        ).then_inc(xch_sem, 16)
            g.wait_ge(xch_sem, cnt)
            g.reg_save(pollct.ap()[0:1, 0:1], cnt)
        return g.nop()
    P.plan("gpsimd", xch_fn, reads=[carry.buf, rcarry.buf, cid_sb.buf,
                                    mg_sb.buf, db("sh_x"), db("sh_f")],
           writes=[ex_sb.buf, fl_sb.buf, pollct.buf], nm="exchange")
    op("vector", lambda v: v.memset(basep.ap(), 0.0), writes=[basep.buf])
    for s_ in range(2):
        op("vector", lambda v, _d=basep.ap(),
           _a=ex_sb.ap()[:, s_ * 32:s_ * 32 + 16],
           _m=exw_sb.ap()[:, s_:s_ + 1], _b=basep.ap():
           v.scalar_tensor_tensor(_d, _a, _m, _b, OP.mult, OP.add),
           reads=[ex_sb.buf, exw_sb.buf, basep.buf], writes=[basep.buf],
           nm="baseT")
        op("vector", lambda v, _d=basep.ap(),
           _a=ex_sb.ap()[:, s_ * 32 + 16:s_ * 32 + 32],
           _m=exw_sb.ap()[:, 2 + s_:3 + s_], _b=basep.ap():
           v.scalar_tensor_tensor(_d, _a, _m, _b, OP.mult, OP.add),
           reads=[ex_sb.buf, exw_sb.buf, basep.buf], writes=[basep.buf],
           nm="baseR")

    if debug:
        dsc = ar.alloc(64, F32, "dbgsm")
        op("vector", lambda v, _d=dsc.ap()[:, 0:16], _s=carry.ap():
           v.tensor_copy(_d, _s), reads=[carry.buf], writes=[dsc.buf])
        op("vector", lambda v, _d=dsc.ap()[:, 16:32], _s=rcarry.ap():
           v.tensor_copy(_d, _s), reads=[rcarry.buf], writes=[dsc.buf])
        op("vector", lambda v, _d=dsc.ap()[:, 32:48], _s=basep.ap():
           v.tensor_copy(_d, _s), reads=[basep.buf], writes=[dsc.buf])
        op("vector", lambda v, _d=dsc.ap()[:, 48:63],
           _s=ex_sb.ap()[:, 0:15]: v.tensor_copy(_d, _s),
           reads=[ex_sb.buf], writes=[dsc.buf])
        op("vector", lambda v, _d=dsc.ap()[0:1, 63:64],
           _s=pollct.ap()[0:1, 0:1].bitcast(F32) if False else
           pollct.ap()[0:1, 0:1]: v.tensor_copy(_d, _s),
           reads=[pollct.buf], writes=[dsc.buf])
        P.dma("AQ", dbg_sm.ap(), dsc.ap(), reads=[dsc.buf],
              writes=[dbg_sm_b], nm="st_dbgsm")
        ar.free(dsc)
        dbb = ar.alloc(4096, F32, "dbgb")
        for c in range(NCH * NT * TC // 4096):
            P.dma("SQ", dbb.ap(), dap("cum_d")[:, c * 4096:(c + 1) * 4096],
                  reads=[db("cum_d")], writes=[dbb.buf], nm="dbgld")
            P.dma("AQ", dbg_cum.ap()[:, c * 4096:(c + 1) * 4096], dbb.ap(),
                  reads=[dbb.buf], writes=[dbg_cum_b], nm="dbgst")
        ar.free(dbb)


    for c in range(NCH):
        compute_ln1(c, False, None)
        if c == 0:
            drain_defer(len(DEFER))
            if debug:
                for j in range(4):
                    P.dma("GQ", dbg_w1.ap()[:, j * 32768:(j + 1) * 32768],
                          dap("W1eff")[:, j * 32768:(j + 1) * 32768],
                          reads=[db("W1eff")], writes=[dbg_w1_b],
                          nm="tap_w1")
        else:
            drain_defer(NDEF2 // 3 + 1, DEFER2)
        pC = ar.alloc(NBT * TC, F32, "pC")
        pD = ar.alloc(NBT * TC, F32, "pD")
        trb = ar.alloc(NT * TC, BF16, "trb")
        pA = ar.alloc(NBT * TC, F32, "pA")
        pB = ar.alloc(NBT * TC, F32, "pB")
        tmp5 = ar.alloc(TC, F32, "tmp5")
        tmp6 = ar.alloc(TC, F32, "tmp6")
        for half, pl in ((0, pA), (1, pB)):
            s = dap("cum_d")[:, c * NT * TC + half * NBT * TC:
                             c * NT * TC + (half + 1) * NBT * TC]
            P.dma("SQ", pl.ap(), s, reads=[db("cum_d")], writes=[pl.buf],
                  nm="ld_cum")
            for i in range(NBT):
                ci = half * 8 + i
                op("vector", lambda v, _d=tile(pl, i), _s=tile(pl, i),
                   _b=basep.ap()[:, ci:ci + 1]:
                   v.tensor_scalar_add(_d, _s, _b),
                   reads=[pl.buf, basep.buf], writes=[pl.buf], nm="addbase")
        proj_plane("fold_sC", pC, "psC")
        proj_plane("fold_sS", pD, "psS")
        for i in range(NBT):
            op("vector", lambda v, _d=tmp1.ap()[:, 0:TC], _a=tile(pC, i),
               _b=tile(pA, i): v.tensor_mul(_d, _a, _b),
               reads=[pC.buf, pA.buf], writes=[tmp1.buf], nm="ta1")
            op("vector", lambda v, _d=tmp2.ap()[:, 0:TC], _a=tile(pD, i),
               _b=tile(pB, i): v.tensor_mul(_d, _a, _b),
               reads=[pD.buf, pB.buf], writes=[tmp2.buf], nm="ta2")
            op("vector", lambda v, _d=tmp5.ap()[:, 0:TC],
               _a=tmp1.ap()[:, 0:TC], _b=tmp2.ap()[:, 0:TC]:
               v.tensor_add(_d, _a, _b),
               reads=[tmp1.buf, tmp2.buf], writes=[tmp5.buf], nm="ta3")
            if i == 0:
                op("vector", lambda v, _d=tmp5.ap()[0:1, 0:TC],
                   _s=tmp1.ap()[0:1, 0:TC]: v.tensor_copy(_d, _s),
                   reads=[tmp1.buf], writes=[tmp5.buf], nm="taf1")
                op("vector", lambda v, _d=tmp1.ap()[0:1, 0:TC],
                   _s=tmp2.ap()[0:1, 0:TC]: v.tensor_copy(_d, _s),
                   reads=[tmp2.buf], writes=[tmp1.buf], nm="taf2")
            op("vector", lambda v, _d=tmp2.ap()[:, 0:TC], _a=tile(pC, i),
               _b=tile(pB, i): v.tensor_mul(_d, _a, _b),
               reads=[pC.buf, pB.buf], writes=[tmp2.buf], nm="ta4")
            op("vector", lambda v, _d=tmp6.ap()[:, 0:TC], _a=tile(pD, i),
               _b=tile(pA, i): v.tensor_mul(_d, _a, _b),
               reads=[pD.buf, pA.buf], writes=[tmp6.buf], nm="ta5")
            op("vector", lambda v, _d=tmp6.ap()[:, 0:TC],
               _a=tmp2.ap()[:, 0:TC], _b=tmp6.ap()[:, 0:TC]:
               v.tensor_sub(_d, _a, _b),
               reads=[tmp2.buf, tmp6.buf], writes=[tmp6.buf], nm="ta6")
            if i == 0:
                op("vector", lambda v, _d=tmp6.ap()[0:1, 0:TC],
                   _s=tmp1.ap()[0:1, 0:TC]: v.tensor_copy(_d, _s),
                   reads=[tmp1.buf], writes=[tmp6.buf], nm="taf3")
            op("scalar", lambda s, _d=tile(trb, i), _s=tmp5.ap()[:, 0:TC]:
               s.copy(_d, _s), reads=[tmp5.buf], writes=[trb.buf], nm="trbR")
            op("scalar", lambda s, _d=tile(trb, 8 + i),
               _s=tmp6.ap()[:, 0:TC]:
               s.copy(_d, _s), reads=[tmp6.buf], writes=[trb.buf], nm="trbI")
            op("vector", lambda v, _d=tmp1.ap()[:, 0:TC], _a=tile(pC, i),
               _b=tmp5.ap()[:, 0:TC]: v.tensor_mul(_d, _a, _b),
               reads=[pC.buf, tmp5.buf], writes=[tmp1.buf], nm="sf1")
            op("vector", lambda v, _d=tmp2.ap()[:, 0:TC], _a=tile(pD, i),
               _b=tmp6.ap()[:, 0:TC]: v.tensor_mul(_d, _a, _b),
               reads=[pD.buf, tmp6.buf], writes=[tmp2.buf], nm="sf2")
            op("vector", lambda v, _d=tmp3.ap()[:, 0:TC], _a=tile(pC, i),
               _b=tmp6.ap()[:, 0:TC]: v.tensor_mul(_d, _a, _b),
               reads=[pC.buf, tmp6.buf], writes=[tmp3.buf], nm="sf3")
            op("vector", lambda v, _d=tmp4.ap()[:, 0:TC], _a=tile(pD, i),
               _b=tmp5.ap()[:, 0:TC]: v.tensor_mul(_d, _a, _b),
               reads=[pD.buf, tmp5.buf], writes=[tmp4.buf], nm="sf4")
            op("vector", lambda v, _d=tmp3.ap()[:, 0:TC],
               _a=tmp3.ap()[:, 0:TC], _b=tmp4.ap()[:, 0:TC]:
               v.tensor_add(_d, _a, _b),
               reads=[tmp3.buf, tmp4.buf], writes=[tmp3.buf], nm="sf5")
            if i == 0:
                op("vector", lambda v, _d=tmp3.ap()[0:1, 0:TC],
                   _s=tmp2.ap()[0:1, 0:TC]: v.tensor_copy(_d, _s),
                   reads=[tmp2.buf], writes=[tmp3.buf], nm="sf6")
            op("vector", lambda v, _d=tmp4.ap()[:, 0:TC],
               _a=tmp1.ap()[:, 0:TC], _b=tmp2.ap()[:, 0:TC]:
               v.tensor_sub(_d, _a, _b),
               reads=[tmp1.buf, tmp2.buf], writes=[tmp4.buf], nm="sf7")
            if i == 0:
                op("vector", lambda v, _d=tmp4.ap()[0:1, 0:TC],
                   _s=tmp1.ap()[0:1, 0:TC]: v.tensor_copy(_d, _s),
                   reads=[tmp1.buf], writes=[tmp4.buf], nm="sf8")
            op("vector", lambda v, _d=tile(pC, i), _a=tile(pA, i),
               _b=tmp4.ap()[:, 0:TC]: v.tensor_sub(_d, _a, _b),
               reads=[pA.buf, tmp4.buf], writes=[pC.buf], nm="hfR")
            op("vector", lambda v, _d=tile(pD, i), _a=tile(pB, i),
               _b=tmp3.ap()[:, 0:TC]: v.tensor_sub(_d, _a, _b),
               reads=[pB.buf, tmp3.buf], writes=[pD.buf], nm="hfI")
        if debug and c == 0:
            P.dma("GQ", dbg_u.ap().rearrange("p (kt m) -> p kt m", m=TC),
                  trb.ap().rearrange("p (kt m) -> p kt m", m=TC),
                  reads=[trb.buf], writes=[dbg_u_b], nm="tap_u")
        ar.free(pA)
        ar.free(pB)
        ar.free(tmp5)
        ar.free(tmp6)
        ar.free(tmp1)
        ar.free(tmp2)
        ar.free(tmp3)
        ar.free(tmp4)
        hfb = ar.alloc(NT * TC, BF16, "hfb")
        for i in range(NBT):
            op("scalar", lambda s, _d=tile(hfb, i), _s=tile(pC, i):
               s.copy(_d, _s), reads=[pC.buf], writes=[hfb.buf], nm="hfbR")
            op("scalar", lambda s, _d=tile(hfb, 8 + i), _s=tile(pD, i):
               s.copy(_d, _s), reads=[pD.buf], writes=[hfb.buf], nm="hfbI")
        ar.free(pC)
        ar.free(pD)
        if debug and c == 0:
            P.dma("GQ", dbg_hf.ap().rearrange("p (kt m) -> p kt m", m=TC),
                  hfb.ap().rearrange("p (kt m) -> p kt m", m=TC),
                  reads=[hfb.buf], writes=[dbg_hf_b], nm="tap_hf")
        h1h = [ar.alloc(NHT * TC // 2, BF16, f"h1h{q}") for q in range(2)]

        def h1t(i):
            return h1h[i // 32].ap()[:, (i % 32) * TC:(i % 32 + 1) * TC]

        def h1b(i):
            return h1h[i // 32].buf

        for sl in range(32):
            slab = slab0 if sl % 2 == 0 else slab1
            load_image_slab("W1eff", slab, sl, 4096, nm="lw1e")
            for mt in range(2):
                gm = sl * 2 + mt
                bank = next_bank()
                pairs = []
                for kt in range(NT):
                    l = slab.ap()[:, kt * 256 + mt * 128:
                                  kt * 256 + mt * 128 + 128]
                    pairs.append((l, tile(trb, kt)))
                mm_group(bank, bank[0].ap(), pairs, [slab.buf, trb.buf],
                         nm="mlp1")
                op("scalar", lambda s, _d=h1t(gm), _s=bank[0].ap(),
                   _b=b1_sb.ap()[:, gm:gm + 1]:
                   s.activation(_d, _s, AF.Relu, bias=_b),
                   reads=[bank[1], b1_sb.buf], writes=[h1b(gm)], nm="relu")
        ar.free(trb)
        if debug and c == 0:
            for q in range(2):
                P.dma("GQ", dbg_h1.ap()[:, q * 16384:(q + 1) * 16384]
                      .rearrange("p (j m) -> p j m", m=TC),
                      h1h[q].ap().rearrange("p (j m) -> p j m", m=TC),
                      reads=[h1h[q].buf], writes=[dbg_h1_b], nm="tap_h1")
        trfq = [ar.alloc(4 * TC, BF16, f"trfq{q}") for q in range(8)]

        def trft(i):
            return trfq[i // 4].ap()[:, (i % 4) * TC:(i % 4 + 1) * TC]

        def trfb(i):
            return trfq[i // 4].buf

        for mt in range(2 * NT):
            bank = next_bank()
            for kh in range(2):
                slab = slab0 if kh == 0 else slab1
                load_image_slab("W2a_d", slab, mt * 2 + kh, 4096, nm="lw2a")
                pairs = []
                for kt in range(32):
                    pairs.append((slab.ap()[:, kt * 128:kt * 128 + 128],
                                  h1t(kh * 32 + kt)))
                mm_group(bank, bank[0].ap(), pairs,
                         [slab.buf, h1h[kh].buf],
                         nm="mlp2", start=(kh == 0), stop=(kh == 1))
            op("scalar", lambda s, _d=trft(mt), _s=bank[0].ap(),
               _b=b2_sb.ap()[:, mt:mt + 1]:
               s.activation(_d, _s, AF.Identity, bias=_b),
               reads=[bank[1], b2_sb.buf], writes=[trfb(mt)], nm="mlp2e")
        if debug and c == 0:
            for q in range(8):
                P.dma("GQ", dbg_trf.ap()[:, q * 4 * TC:(q + 1) * 4 * TC]
                      .rearrange("p (j m) -> p j m", m=TC),
                      trfq[q].ap().rearrange("p (j m) -> p j m", m=TC),
                      reads=[trfq[q].buf], writes=[dbg_trf_b], nm="tap_trf")
        ar.free(h1h[0])
        ar.free(h1h[1])
        tmp1 = ar.alloc(TC, F32, "tmp1")
        tmp2 = ar.alloc(TC, F32, "tmp2")
        tmp3 = ar.alloc(TC, F32, "tmp3")
        tmp4 = ar.alloc(TC, F32, "tmp4")
        tPR = ar.alloc(NBT * TC, F32, "tPR")
        tPI = ar.alloc(NBT * TC, F32, "tPI")
        proj_plane("fold_tC", tPR, "ptC")
        proj_plane("fold_tS", tPI, "ptS")
        if debug and c == 0:
            P.dma("AQ", dbg_t.ap()[:, 0:NBT * TC], tPR.ap(),
                  reads=[tPR.buf], writes=[dbg_t_b], nm="tap_t")
            P.dma("AQ", dbg_t.ap()[:, NBT * TC:NT * TC], tPI.ap(),
                  reads=[tPI.buf], writes=[dbg_t_b], nm="tap_t2")
        for half in range(2):
            xoff = (1 + half) * 128
            for j in range(NBT):
                dtile = half * 16 + j
                ftile = half * 16 + 15 - j
                bank = next_bank()
                mm_group(bank, bank[0].ap(),
                         [(flip_sb.ap()[:, 0:128], trft(dtile)),
                          (flip_sb.ap()[:, xoff:xoff + 128], trft(ftile))],
                         [flip_sb.buf, trfb(dtile), trfb(ftile)], nm="tfold")
                op("scalar", lambda s, _d=trft(dtile), _s=bank[0].ap():
                   s.activation(_d, _s, AF.Copy),
                   reads=[bank[1]], writes=[trfb(dtile)], nm="tfolde")
        op("scalar", lambda s, _d=trft(0)[0:1, :], _s=trft(0)[0:1, :]:
           s.activation(_d, _s, AF.Copy, scale=2.0),
           reads=[trfb(0)], writes=[trfb(0)], nm="tfx0")
        op("scalar", lambda s, _d=trft(16)[0:1, :], _s=trft(8)[0:1, :]:
           s.activation(_d, _s, AF.Copy),
           reads=[trfb(8)], writes=[trfb(16)], nm="tfx1")
        for j in range(1, NBT):
            op("vector", lambda v, _d=trft(j)[0:1, :],
               _a=trft(16 - j)[0:1, :], _b=trft(j)[0:1, :]:
               v.scalar_tensor_tensor(_d, _a, 0.5, _b, OP.mult, OP.add),
               reads=[trfb(16 - j), trfb(j)], writes=[trfb(j)], nm="tfxr")
            op("vector", lambda v, _d=trft(16 + j)[0:1, :],
               _a=trft(32 - j)[0:1, :], _b=trft(16 + j)[0:1, :]:
               v.scalar_tensor_tensor(_d, _a, -0.5, _b, OP.mult, OP.add),
               reads=[trfb(32 - j), trfb(16 + j)], writes=[trfb(16 + j)],
               nm="tfxi")
        for j in range(NBT):
            op("vector", lambda v, _d=tmp1.ap()[:, 0:TC],
               _a=tile(tPR, j), _b=trft(j): v.tensor_mul(_d, _a, _b),
               reads=[tPR.buf, trfb(j)], writes=[tmp1.buf], nm="Q1")
            op("vector", lambda v, _d=tmp2.ap()[:, 0:TC],
               _a=tile(tPI, j), _b=trft(16 + j): v.tensor_mul(_d, _a, _b),
               reads=[tPI.buf, trfb(16 + j)], writes=[tmp2.buf], nm="Q2")
            op("vector", lambda v, _d=tmp3.ap()[:, 0:TC],
               _a=tmp1.ap()[:, 0:TC], _b=tmp2.ap()[:, 0:TC]:
               v.tensor_sub(_d, _a, _b),
               reads=[tmp1.buf, tmp2.buf], writes=[tmp3.buf], nm="Q3")
            if j == 0:
                op("vector", lambda v, _d=tmp3.ap()[0:1, 0:TC],
                   _s=tmp1.ap()[0:1, 0:TC]: v.tensor_copy(_d, _s),
                   reads=[tmp1.buf], writes=[tmp3.buf], nm="Qf1")
            op("vector", lambda v, _d=tmp1.ap()[:, 0:TC],
               _a=tile(tPR, j), _b=trft(16 + j): v.tensor_mul(_d, _a, _b),
               reads=[tPR.buf, trfb(16 + j)], writes=[tmp1.buf], nm="Q4")
            op("vector", lambda v, _d=tmp4.ap()[:, 0:TC],
               _a=tile(tPI, j), _b=trft(j): v.tensor_mul(_d, _a, _b),
               reads=[tPI.buf, trfb(j)], writes=[tmp4.buf], nm="Q5")
            op("vector", lambda v, _d=tmp1.ap()[:, 0:TC],
               _a=tmp1.ap()[:, 0:TC], _b=tmp4.ap()[:, 0:TC]:
               v.tensor_add(_d, _a, _b),
               reads=[tmp1.buf, tmp4.buf], writes=[tmp1.buf], nm="Q6")
            if j == 0:
                op("vector", lambda v, _d=tmp1.ap()[0:1, 0:TC],
                   _s=tmp2.ap()[0:1, 0:TC]: v.tensor_copy(_d, _s),
                   reads=[tmp2.buf], writes=[tmp1.buf], nm="Qf2")
            op("vector", lambda v, _d=tmp3.ap()[:, 0:TC],
               _a=tmp3.ap()[:, 0:TC], _b=tile(hfb, j):
               v.tensor_add(_d, _a, _b),
               reads=[tmp3.buf, hfb.buf], writes=[tmp3.buf], nm="Qh1")
            op("vector", lambda v, _d=tmp1.ap()[:, 0:TC],
               _a=tmp1.ap()[:, 0:TC], _b=tile(hfb, 8 + j):
               v.tensor_add(_d, _a, _b),
               reads=[tmp1.buf, hfb.buf], writes=[tmp1.buf], nm="Qh2")
            op("scalar", lambda s, _d=trft(8 + j), _s=tmp3.ap()[:, 0:TC]:
               s.copy(_d, _s), reads=[tmp3.buf], writes=[trfb(8 + j)],
               nm="Qc1")
            op("scalar", lambda s, _d=trft(24 + j), _s=tmp1.ap()[:, 0:TC]:
               s.copy(_d, _s), reads=[tmp1.buf], writes=[trfb(24 + j)],
               nm="Qc2")
        ar.free(tPR)
        ar.free(tPI)
        ar.free(hfb)
        if debug and c == 0:
            for q in range(8):
                P.dma("GQ", dbg_P.ap()[:, q * 4 * TC:(q + 1) * 4 * TC]
                      .rearrange("p (j m) -> p j m", m=TC),
                      trfq[q].ap().rearrange("p (j m) -> p j m", m=TC),
                      reads=[trfq[q].buf], writes=[dbg_P_b], nm="tap_P")
        attn_o = ar.alloc(NT * TC, F32, "attn_o")
        ic2s = ar.alloc(NBT * 256, BF16, "ic2s")
        is2s = ar.alloc(NBT * 256, BF16, "is2s")
        for msl in range(8):
            load_image_slab("IC2", ic2s, msl, 2048, nm="lic2")
            load_image_slab("IS2", is2s, msl, 2048, nm="lis2")
            for mt in range(2):
                gm = msl * 2 + mt
                bank = next_bank()
                pairs = []
                for kt in range(NBT):
                    pairs.append((ic2s.ap()[:, kt * 256 + mt * 128:
                                            kt * 256 + mt * 128 + 128],
                                  trft(8 + kt)))
                for kt in range(NBT):
                    pairs.append((is2s.ap()[:, kt * 256 + mt * 128:
                                            kt * 256 + mt * 128 + 128],
                                  trft(24 + kt)))
                mm_group(bank, bank[0].ap(), pairs,
                         [ic2s.buf, is2s.buf] + [q.buf for q in trfq],
                         nm="idft")
                op("scalar", lambda s, _d=tile(attn_o, gm), _s=bank[0].ap():
                   s.activation(_d, _s, AF.Copy),
                   reads=[bank[1]], writes=[attn_o.buf], nm="idfte")
        ar.free(ic2s)
        ar.free(is2s)
        for q in trfq:
            ar.free(q)
        if debug and c == 0:
            P.dma("AQ", dbg_ao.ap(), attn_o.ap(), reads=[attn_o.buf],
                  writes=[dbg_ao_b], nm="tap_ao")
        sqb = ar.alloc(8 * TC, BF16, "sqb")
        lm = ar.alloc(TC, F32, "lm2")
        lr = ar.alloc(TC, F32, "lr2")
        ln_stats([tile(attn_o, i) for i in range(NT)], [attn_o.buf], sqb, TC,
                 lm.ap()[0:1, :], lr.ap()[0:1, :], lm.buf, lr.buf, False)
        ar.free(sqb)
        bm = bcast_row_sb(lm.ap()[0:1, :], lm.buf, TC)
        br = bcast_row_sb(lr.ap()[0:1, :], lr.buf, TC)
        xof = ar.alloc(NT * TC, F32, "xof")
        P.dma("SQ", xof.ap(), dap("xT")[:, c * NT * TC:(c + 1) * NT * TC],
              reads=[db("xT")], writes=[xof.buf], nm="ld_xf32")
        for h2_ in range(2):
            aov = attn_o.ap()[:, h2_ * 8 * TC:(h2_ + 1) * 8 * TC].rearrange(
                "p (t m) -> p t m", m=TC)
            xov = xof.ap()[:, h2_ * 8 * TC:(h2_ + 1) * 8 * TC].rearrange(
                "p (t m) -> p t m", m=TC)
            op("vector", lambda v, _d=aov, _a=aov,
               _b=bm[0].ap()[:, 0:TC].unsqueeze(1)
               .broadcast_to([128, NBT, TC]):
               v.tensor_sub(_d, _a, _b),
               reads=[attn_o.buf, bm[1]], writes=[attn_o.buf], nm="r1")
            op("vector", lambda v, _d=aov, _a=aov,
               _b=br[0].ap()[:, 0:TC].unsqueeze(1)
               .broadcast_to([128, NBT, TC]):
               v.tensor_mul(_d, _a, _b),
               reads=[attn_o.buf, br[1]], writes=[attn_o.buf], nm="r2")
            op("vector", lambda v, _d=aov, _a=aov, _b=xov:
               v.tensor_add(_d, _a, _b),
               reads=[attn_o.buf, xof.buf], writes=[attn_o.buf], nm="r3")
        for h2 in range(2):
            cf = 2 * c + h2
            d = dap("x1_d")[:, cf * NT * TCF:(cf + 1) * NT * TCF].rearrange(
                "p (kt m) -> p kt m", m=TCF)
            s2_ = attn_o.ap().rearrange("p (kt m) -> p kt m", m=TC)[
                :, :, h2 * TCF:(h2 + 1) * TCF]
            P.dma("AQ", d, s2_, reads=[attn_o.buf], writes=[db("x1_d")],
                  nm="st_x1")
        ar.free(lm)
        ar.free(lr)
        ar.free(xof)
        ar.free(attn_o)

    if debug:
        dbb2 = ar.alloc(8192, F32, "dbgb2")
        for c in range(NCHF * NT * TCF // 8192):
            P.dma("SQ", dbb2.ap(), dap("x1_d")[:, c * 8192:(c + 1) * 8192],
                  reads=[db("x1_d")], writes=[dbb2.buf], nm="dbgld2")
            P.dma("AQ", dbg_x1.ap()[:, c * 8192:(c + 1) * 8192], dbb2.ap(),
                  reads=[dbb2.buf], writes=[dbg_x1_b], nm="dbgst2")
        ar.free(dbb2)

    drain_defer(len(DEFER))
    drain_defer(len(DEFER2), DEFER2)
    ar.free(w1pool[0])
    ar.free(w1pool[1])
    ar.free(w1pool[2])

    ar.free(ln1)
    ar.free(onesTC)
    ar.free(slab0)
    ar.free(slab1)
    ar.free(tmp1)
    ar.free(tmp2)
    ar.free(tmp3)
    ar.free(tmp4)
    ar.free(stats_m)
    ar.free(stats_r)

    sqf = None
    t_x1 = ar.alloc(TCF, BF16, "t_x1")
    t_s1 = ar.alloc(TCF, BF16, "t_s1")
    t_x3 = ar.alloc(TCF, BF16, "t_x3")
    out_acc = ar.alloc(NT * TCF, F32, "out_acc")
    tmpf = ar.alloc(TCF, F32, "tmpf")
    stk = ar.alloc(96, F32, "stk")
    rw_row = ar.alloc(TCF, F32, "rw_row")
    mk_bf = ar.alloc(2 * 256, BF16, "mk_bf")
    mexp = ar.alloc(4 * TCF, F32, "mexp")
    rwb = ar.alloc(2 * TCF, F32, "rwb")
    a1a = ar.alloc(2 * TCF, F32, "a1a")
    a3a = ar.alloc(2 * TCF, F32, "a3a")
    amk = [ar.alloc(2 * TCF, BF16, f"am{k}") for k in range(2)]
    am3k = [ar.alloc(2 * TCF, BF16, f"am3{k}") for k in range(2)]
    am2k = [ar.alloc(2 * TCF, BF16, f"am2{k}") for k in range(2)]
    b1s = ar.alloc(2 * 512, BF16, "b1s")
    b3s = ar.alloc(2 * 512, BF16, "b3s")
    b2s = ar.alloc(2 * 512, BF16, "b2s")
    a2s0 = ar.alloc(8 * 256, BF16, "a2s0")

    def tilef(a, i):
        return a.ap()[:, i * TCF:(i + 1) * TCF]

    for c in range(NCHF):
        ln2 = ar.alloc(NT * TCF, BF16, "ln2")
        sqf = ar.alloc(8 * TCF, BF16, "sqf")
        x1b = ar.alloc(NT * TCF, BF16, "x1b")
        P.dma("GQ", x1b.ap(), dap("x1_d")[:, c * NT * TCF:(c + 1) * NT * TCF],
              reads=[db("x1_d")], writes=[x1b.buf], nm="ld_x1b")
        lm = ar.alloc(TCF, F32, "lmf")
        lr = ar.alloc(TCF, F32, "lrf")
        ln_stats([tilef(x1b, i) for i in range(NT)], [x1b.buf], sqf, TCF,
                 lm.ap()[0:1, :], lr.ap()[0:1, :], lm.buf, lr.buf, True)
        bm = bcast_row(lm.ap()[0:1, :], lm.buf, TCF)
        br = bcast_row(lr.ap()[0:1, :], lr.buf, TCF)
        for i in range(NT):
            op("vector", lambda v, _d=tilef(ln2, i), _a=tilef(x1b, i),
               _b=bm[0].ap()[:, 0:TCF]: v.tensor_sub(_d, _a, _b),
               reads=[x1b.buf, bm[1]], writes=[ln2.buf], nm="l2a")
            op("vector", lambda v, _d=tilef(ln2, i), _a=tilef(ln2, i),
               _b=br[0].ap()[:, 0:TCF]: v.tensor_mul(_d, _a, _b),
               reads=[ln2.buf, br[1]], writes=[ln2.buf], nm="l2b")
        ar.free(lm)
        ar.free(lr)
        ar.free(x1b)
        ar.free(sqf)
        for tt in range(2):
            bank = next_bank()
            pairs = []
            for kt in range(NT):
                l = ln2.ap()[:, kt * TCF + tt * 128:kt * TCF + tt * 128 + 128]
                pairs.append((l, wg_cast.ap()[:, kt * E:(kt + 1) * E]))
            mm_group(bank, bank[0].ap()[:, 0:E], pairs,
                     [ln2.buf, wg_cast.buf], nm="logits")
            lgt = stk.ap()[:, 80:88]
            m1 = stk.ap()[:, 88:89]
            m2 = stk.ap()[:, 72:73]
            msk1 = stk.ap()[:, 0:8]
            msk2 = stk.ap()[:, 32:40]
            l2_ = stk.ap()[:, 40:48]
            dcol = stk.ap()[:, 73:74]
            rw2c = stk.ap()[:, 74:75]
            rw1c = stk.ap()[:, 64:65]
            P.plan("vector", lambda v, _d=lgt, _s=bank[0].ap()[:, 0:E]:
                   v.tensor_copy(_d, _s), reads=[bank[1]], writes=[stk.buf],
                   nm="lgc", selfsync=True)
            P.plan("vector", lambda v, _d=m1, _s=lgt:
                   v.reduce_max(_d, _s, AX),
                   reads=[stk.buf], writes=[stk.buf], nm="m1",
                   selfsync=True)
            P.plan("vector", lambda v, _d=msk1, _s=lgt, _m=m1:
                   v.tensor_single_scalar(_d, _s, _m, OP.is_ge),
                   reads=[stk.buf], writes=[stk.buf], nm="msk1",
                   selfsync=True)
            P.plan("vector", lambda v, _d=l2_, _a=msk1, _b=lgt:
                   v.scalar_tensor_tensor(_d, _a, -1e30, _b, OP.mult,
                                          OP.add),
                   reads=[stk.buf], writes=[stk.buf], nm="l2m",
                   selfsync=True)
            P.plan("vector", lambda v, _d=m2, _s=l2_:
                   v.reduce_max(_d, _s, AX),
                   reads=[stk.buf], writes=[stk.buf], nm="m2",
                   selfsync=True)
            P.plan("vector", lambda v, _d=msk2, _s=l2_, _m=m2:
                   v.tensor_single_scalar(_d, _s, _m, OP.is_ge),
                   reads=[stk.buf], writes=[stk.buf], nm="msk2",
                   selfsync=True)
            P.plan("vector", lambda v, _d=dcol, _a=m2, _b=m1:
                   v.tensor_sub(_d, _a, _b), reads=[stk.buf],
                   writes=[stk.buf], nm="d12", selfsync=True)
            op("scalar", lambda s, _d=rw2c, _s=dcol:
               s.activation(_d, _s, AF.Sigmoid),
               reads=[stk.buf], writes=[stk.buf], nm="sig")
            P.plan("vector", lambda v, _d=rw1c, _s=rw2c:
                   v.tensor_scalar(_d, _s, -1.0, 1.0, OP.mult, OP.add),
                   reads=[stk.buf], writes=[stk.buf], nm="rw1",
                   selfsync=True)
            if debug and c == 0:
                P.dma("AQ", dbg_stk.ap()[:, tt * 96:(tt + 1) * 96],
                      stk.ap()[:, 0:96], reads=[stk.buf],
                      writes=[dbg_stk_b], nm="dstk")
            bank2 = next_bank()
            op("tensor", lambda t, _o=bank2[0].ap()[0:96, 0:128],
               _i=stk.ap()[:, 0:96], _id=idn_sb.ap():
               t.transpose(_o, _i, _id),
               reads=[stk.buf, idn_sb.buf], writes=[bank2[1]], nm="sttr")
            op("vector", lambda v,
               _d=mk_bf.ap()[0:8, tt * 128:(tt + 1) * 128],
               _s=bank2[0].ap()[0:8, 0:128]: v.tensor_copy(_d, _s),
               reads=[bank2[1]], writes=[mk_bf.buf], nm="mk1")
            op("vector", lambda v,
               _d=mk_bf.ap()[0:8, 256 + tt * 128:256 + (tt + 1) * 128],
               _s=bank2[0].ap()[32:40, 0:128]: v.tensor_copy(_d, _s),
               reads=[bank2[1]], writes=[mk_bf.buf], nm="mk2")
            op("vector", lambda v,
               _d=rw_row.ap()[0:1, tt * 128:(tt + 1) * 128],
               _s=bank2[0].ap()[64:65, 0:128]: v.tensor_copy(_d, _s),
               reads=[bank2[1]], writes=[rw_row.buf], nm="rwrow")
        for k in range(2):
            for mt in range(2):
                bank = next_bank()
                mm_group(bank, bank[0].ap()[:, 0:TCF],
                         [(expm_sb.ap()[0:8, mt * 128:(mt + 1) * 128],
                           mk_bf.ap()[0:8, k * 256:k * 256 + TCF])],
                         [expm_sb.buf, mk_bf.buf], nm="mexp")
                op("vector", lambda v,
                   _d=mexp.ap()[:, (k * 2 + mt) * TCF:(k * 2 + mt + 1) * TCF],
                   _s=bank[0].ap()[:, 0:TCF]: v.tensor_copy(_d, _s),
                   reads=[bank[1]], writes=[mexp.buf], nm="mexpe")
        bankr = bcast_row(rw_row.ap()[0:1, 0:TCF], rw_row.buf, TCF)
        op("vector", lambda v, _d=rwb.ap()[:, 0:TCF],
           _s=bankr[0].ap()[:, 0:TCF]: v.tensor_copy(_d, _s),
           reads=[bankr[1]], writes=[rwb.buf], nm="rwbe")
        op("vector", lambda v, _d=rwb.ap()[:, TCF:2 * TCF],
           _s=rwb.ap()[:, 0:TCF]:
           v.tensor_scalar(_d, _s, -1.0, 1.0, OP.mult, OP.add),
           reads=[rwb.buf], writes=[rwb.buf], nm="rwb2")
        if debug and c == 0:
            P.dma("AQ", dbg_ffn.ap()[:, 0:1024], mexp.ap(),
                  reads=[mexp.buf], writes=[dbg_ffn_b], nm="dffn1")
            P.dma("AQ", dbg_ffn.ap()[:, 1024:1536], rwb.ap(),
                  reads=[rwb.buf], writes=[dbg_ffn_b], nm="dffn2")
        aslab = ar.alloc(NT * 256, BF16, "aslab")
        for aname, dsta in (("A1d", a1a), ("A3d", a3a)):
            load_image_slab(aname, aslab, 0, NT * 256, nm="ld_a")
            for mt in range(2):
                bank = next_bank()
                pairs = []
                for kt in range(NT):
                    l = aslab.ap()[:, kt * 256 + mt * 128:
                                   kt * 256 + mt * 128 + 128]
                    pairs.append((l, tilef(ln2, kt)))
                mm_group(bank, bank[0].ap()[:, 0:TCF], pairs,
                         [aslab.buf, ln2.buf], nm="a_all")
                op("vector", lambda v, _d=tilef(dsta, mt),
                   _s=bank[0].ap()[:, 0:TCF]: v.tensor_copy(_d, _s),
                   reads=[bank[1]], writes=[dsta.buf], nm="a_e")
        if debug and c == 0:
            P.dma("AQ", dbg_ffn.ap()[:, 1536:2048],
                  a1a.ap(), reads=[a1a.buf], writes=[dbg_ffn_b], nm="dffn3")
        ar.free(aslab)
        wslab0 = ar.alloc(NT * 512, BF16, "wslab0")
        wslab1 = ar.alloc(NT * 512, BF16, "wslab1")
        base1 = ar.alloc(NFT * TCF, BF16, "base1")
        base3 = ar.alloc(NFT * TCF, BF16, "base3")
        for wname, dstb in (("W1f_d", base1), ("W3f_d", base3)):
            for sl in range(14):
                slab = wslab0 if sl % 2 == 0 else wslab1
                load_image_slab(wname, slab, sl, 8192, nm="ld_wf")
                for mt in range(4):
                    gm = sl * 4 + mt
                    bank = next_bank()
                    pairs = []
                    for kt in range(NT):
                        l = slab.ap()[:, kt * 512 + mt * 128:
                                      kt * 512 + mt * 128 + 128]
                        pairs.append((l, tilef(ln2, kt)))
                    mm_group(bank, bank[0].ap()[:, 0:TCF], pairs,
                             [slab.buf, ln2.buf], nm="baseff")
                    op("scalar", lambda s, _d=tilef(dstb, gm),
                       _s=bank[0].ap()[:, 0:TCF]:
                       s.activation(_d, _s, AF.Copy),
                       reads=[bank[1]], writes=[dstb.buf], nm="basee")
        ar.free(ln2)
        ar.free(wslab0)
        ar.free(wslab1)
        x2k = [ar.alloc(NFT * TCF, BF16, f"x2_{k}") for k in range(2)]
        for k in range(2):
            for mt in range(2):
                mexp_ap = mexp.ap()[:, (k * 2 + mt) * TCF:
                                    (k * 2 + mt + 1) * TCF]
                op("vector", lambda v, _d=tilef(amk[k], mt),
                   _a=tilef(a1a, mt), _b=mexp_ap: v.tensor_mul(_d, _a, _b),
                   reads=[a1a.buf, mexp.buf], writes=[amk[k].buf], nm="am1")
                op("vector", lambda v, _d=tilef(am3k[k], mt),
                   _a=tilef(a3a, mt), _b=mexp_ap: v.tensor_mul(_d, _a, _b),
                   reads=[a3a.buf, mexp.buf], writes=[am3k[k].buf], nm="am3")
        if debug and c == 0:
            P.dma("GQ", dbg_f2.ap()[:, 1536:2048], amk[0].ap()[:, 0:512],
                  reads=[amk[0].buf], writes=[dbg_f2_b], nm="df2d")
        for msl in range(14):
            load_image_slab("B1d", b1s, msl, 1024, nm="ld_b1")
            load_image_slab("B3d", b3s, msl, 1024, nm="ld_b3")
            for mt4 in range(4):
                mt = msl * 4 + mt4
                for k in range(2):
                    bank = next_bank()
                    pairs = []
                    for kt in range(2):
                        l = b1s.ap()[:, kt * 512 + mt4 * 128:
                                     kt * 512 + mt4 * 128 + 128]
                        pairs.append((l, tilef(amk[k], kt)))
                    mm_group(bank, bank[0].ap()[:, 0:TCF], pairs,
                             [b1s.buf, amk[k].buf], nm="lora1")
                    op("vector", lambda v, _d=t_x1.ap()[:, 0:TCF],
                       _a=bank[0].ap()[:, 0:TCF], _b=tilef(base1, mt):
                       v.tensor_add(_d, _a, _b),
                       reads=[bank[1], base1.buf], writes=[t_x1.buf],
                       nm="x1k")
                    op("scalar", lambda s, _d=t_s1.ap()[:, 0:TCF],
                       _s=t_x1.ap()[:, 0:TCF]: s.activation(_d, _s, AF.Silu),
                       reads=[t_x1.buf], writes=[t_s1.buf], nm="silu")
                    bank3 = next_bank()
                    pairs = []
                    for kt in range(2):
                        l = b3s.ap()[:, kt * 512 + mt4 * 128:
                                     kt * 512 + mt4 * 128 + 128]
                        pairs.append((l, tilef(am3k[k], kt)))
                    mm_group(bank3, bank3[0].ap()[:, 0:TCF], pairs,
                             [b3s.buf, am3k[k].buf], nm="lora3")
                    op("vector", lambda v, _d=t_x3.ap()[:, 0:TCF],
                       _a=bank3[0].ap()[:, 0:TCF], _b=tilef(base3, mt):
                       v.tensor_add(_d, _a, _b),
                       reads=[bank3[1], base3.buf], writes=[t_x3.buf],
                       nm="x3k")
                    op("vector", lambda v, _d=tilef(x2k[k], mt),
                       _a=t_s1.ap()[:, 0:TCF], _b=t_x3.ap()[:, 0:TCF]:
                       v.tensor_mul(_d, _a, _b),
                       reads=[t_s1.buf, t_x3.buf], writes=[x2k[k].buf],
                       nm="x2k")
        if debug and c == 0:
            P.dma("GQ", dbg_f2.ap()[:, 0:256], t_x1.ap()[:, 0:TCF],
                  reads=[t_x1.buf], writes=[dbg_f2_b], nm="df3a")
            P.dma("GQ", dbg_f2.ap()[:, 256:512], t_s1.ap()[:, 0:TCF],
                  reads=[t_s1.buf], writes=[dbg_f2_b], nm="df3b")
        ar.free(base1)
        ar.free(base3)
        banks_a2 = [[next_bank() for _ in range(2)] for _ in range(2)]
        for sl in range(7):
            load_image_slab("A2d", a2s0, sl, 2048, nm="ld_a2")
            for k in range(2):
                for mt in range(2):
                    bank = banks_a2[k][mt]
                    pairs = []
                    for kt in range(8):
                        l = a2s0.ap()[:, kt * 256 + mt * 128:
                                      kt * 256 + mt * 128 + 128]
                        pairs.append((l, tilef(x2k[k], sl * 8 + kt)))
                    mm_group(bank, bank[0].ap()[:, 0:TCF], pairs,
                             [a2s0.buf, x2k[k].buf], nm="a2all",
                             start=(sl == 0), stop=(sl == 6))
        for k in range(2):
            for mt in range(2):
                bank = banks_a2[k][mt]
                mexp_ap = mexp.ap()[:, (k * 2 + mt) * TCF:
                                    (k * 2 + mt + 1) * TCF]
                op("vector", lambda v, _d=tilef(am2k[k], mt),
                   _a=bank[0].ap()[:, 0:TCF], _b=mexp_ap:
                   v.tensor_mul(_d, _a, _b),
                   reads=[bank[1], mexp.buf], writes=[am2k[k].buf], nm="am2")
        wslab0 = ar.alloc(NT * 512, BF16, "wslab0")
        wslab1 = ar.alloc(NT * 512, BF16, "wslab1")
        for gm in range(NT):
            slab = wslab0 if gm % 2 == 0 else wslab1
            load_image_slab("W2f_d", slab, gm, 7168, nm="ld_w2f")
            load_image_slab("B2d", b2s, gm, 256, nm="ld_b2")
            for k in range(2):
                bank = next_bank()
                pairs = []
                for kt in range(NFT):
                    l = slab.ap()[:, kt * 128:kt * 128 + 128]
                    pairs.append((l, tilef(x2k[k], kt)))
                for kt in range(2):
                    l = b2s.ap()[:, kt * 128:kt * 128 + 128]
                    pairs.append((l, tilef(am2k[k], kt)))
                mm_group(bank, bank[0].ap()[:, 0:TCF], pairs,
                         [slab.buf, b2s.buf, x2k[k].buf, am2k[k].buf],
                         nm="down")
                rw_ap = rwb.ap()[:, k * TCF:(k + 1) * TCF]
                if k == 0:
                    op("vector", lambda v, _d=tilef(out_acc, gm),
                       _a=bank[0].ap()[:, 0:TCF], _b=rw_ap:
                       v.tensor_mul(_d, _a, _b),
                       reads=[bank[1], rwb.buf], writes=[out_acc.buf],
                       nm="ok0")
                else:
                    op("vector", lambda v, _d=tmpf.ap()[:, 0:TCF],
                       _a=bank[0].ap()[:, 0:TCF], _b=rw_ap:
                       v.tensor_mul(_d, _a, _b),
                       reads=[bank[1], rwb.buf], writes=[tmpf.buf],
                       nm="ok1a")
                    op("vector", lambda v, _d=tilef(out_acc, gm),
                       _a=tilef(out_acc, gm), _b=tmpf.ap()[:, 0:TCF]:
                       v.tensor_add(_d, _a, _b),
                       reads=[out_acc.buf, tmpf.buf],
                       writes=[out_acc.buf], nm="ok1b")
        ar.free(x2k[0])
        ar.free(x2k[1])
        ar.free(wslab0)
        ar.free(wslab1)
        lm = ar.alloc(TCF, F32, "lmf2")
        lr = ar.alloc(TCF, F32, "lrf2")
        sqf = ar.alloc(8 * TCF, BF16, "sqf2")
        ln_stats([tilef(out_acc, i) for i in range(NT)], [out_acc.buf], sqf,
                 TCF, lm.ap()[0:1, :], lr.ap()[0:1, :], lm.buf, lr.buf, False)
        bm = bcast_row(lm.ap()[0:1, :], lm.buf, TCF)
        br = bcast_row(lr.ap()[0:1, :], lr.buf, TCF)
        xrf = ar.alloc(NT * TCF, F32, "xrf")
        P.dma("SQ", xrf.ap(), dap("x1_d")[:, c * NT * TCF:(c + 1) * NT * TCF],
              reads=[db("x1_d")], writes=[xrf.buf], nm="ld_x1f")
        for i in range(NT):
            op("vector", lambda v, _d=tilef(out_acc, i), _a=tilef(out_acc, i),
               _b=bm[0].ap()[:, 0:TCF]: v.tensor_sub(_d, _a, _b),
               reads=[out_acc.buf, bm[1]], writes=[out_acc.buf], nm="f1")
            op("vector", lambda v, _d=tilef(out_acc, i), _a=tilef(out_acc, i),
               _b=br[0].ap()[:, 0:TCF]: v.tensor_mul(_d, _a, _b),
               reads=[out_acc.buf, br[1]], writes=[out_acc.buf], nm="f2")
            op("vector", lambda v, _d=tilef(out_acc, i), _a=tilef(out_acc, i),
               _b=tilef(xrf, i): v.tensor_add(_d, _a, _b),
               reads=[out_acc.buf, xrf.buf], writes=[out_acc.buf], nm="f3")
        P.dma("AQ", out_h.ap()[:, c * NT * TCF:(c + 1) * NT * TCF],
              out_acc.ap(), reads=[out_acc.buf], writes=[out_b],
              nm="st_out")
        ar.free(lm)
        ar.free(lr)
        ar.free(xrf)
        ar.free(sqf)

    with contextlib.ExitStack() as stack:
        sems = {}
        for name in list(ENGINES) + list(QUEUES):
            sems[name] = stack.enter_context(nc.semaphore(f"sem_{name}"))
        for name in ENGINES:
            sems["vs_" + name] = stack.enter_context(
                nc.semaphore(f"sem_vs_{name}"))
        nonlocal_xch = stack.enter_context(nc.semaphore("sem_xch"))
        globals()["_XCH_SEM"] = nonlocal_xch
        P.finalize_and_emit(sems)
    return nc


def prep_inputs(inputs):
    g = inputs
    x = np.asarray(g["L0_attn_x"], np.float32).reshape(T, D)
    tabs = make_tables()
    W1 = np.asarray(g["L0_attn_W1"], np.float32)
    zero = np.zeros((1, HID), np.float32)
    w2a_slabs = [(kh * 32, 32, mt * 128, 128)
                 for mt in range(32) for kh in range(2)]
    xslabs = [(0, NT, c * TC, TC) for c in range(NCH)]
    com = {
        "Wq_T": np.ascontiguousarray(np.asarray(g["L0_attn_Wq"], np.float32).T),
        "Wk_T": np.ascontiguousarray(np.asarray(g["L0_attn_Wk"], np.float32).T),
        "Wv_T": np.ascontiguousarray(np.asarray(g["L0_attn_Wv"], np.float32).T),
        "Ws_T": np.ascontiguousarray(np.asarray(g["L0_attn_Ws"], np.float32).T),
        "Wt_T": np.ascontiguousarray(np.asarray(g["L0_attn_Wt"], np.float32).T),
        "W1e_a": np.ascontiguousarray(W1[0:NB]),
        "W1e_b": np.ascontiguousarray(np.vstack([zero, W1[2047:1024:-1]])),
        "W1e_c": np.ascontiguousarray(np.vstack([W1[1024:1025],
                                                 W1[2049:3072]])),
        "W1e_d": np.ascontiguousarray(np.vstack([zero, W1[4095:3072:-1]])),
        "W2a": image_tab(np.asarray(g["L0_attn_W2"], np.float32), w2a_slabs),
        "b1t": np.ascontiguousarray(
            np.asarray(g["L0_attn_b1"], np.float32).reshape(NHT, 128).T),
        "b2t": np.ascontiguousarray(
            np.asarray(g["L0_attn_b2"], np.float32).reshape(2 * NT, 128).T),
        "W1f_T": image_tab(np.asarray(g["L0_ffn_W1"], np.float32).T,
                           geom_mslabs(D, FF, 512)),
        "W3f_T": image_tab(np.asarray(g["L0_ffn_W3"], np.float32).T,
                           geom_mslabs(D, FF, 512)),
        "W2f_T": image_tab(np.asarray(g["L0_ffn_W2"], np.float32).T,
                           geom_mslabs(FF, D, 128)),
        "Wg_tiled": np.ascontiguousarray(
            np.asarray(g["L0_ffn_Wg"], np.float32).T.reshape(NT, 128, E)
            .transpose(1, 0, 2).reshape(128, NT * E)),
        "A1p": image_tab(np.asarray(g["L0_ffn_A1"], np.float32)
                         .reshape(ER, D).T, geom_mslabs(D, 256, 256)),
        "A3p": image_tab(np.asarray(g["L0_ffn_A3"], np.float32)
                         .reshape(ER, D).T, geom_mslabs(D, 256, 256)),
        "A2p": image_tab(np.asarray(g["L0_ffn_A2"], np.float32)
                         .reshape(ER, FF).T, geom_kslabs(FF, 256, 8)),
        "B1p": image_tab(np.asarray(g["L0_ffn_B1"], np.float32)
                         .transpose(0, 2, 1).reshape(ER, FF),
                         geom_mslabs(256, FF, 512)),
        "B3p": image_tab(np.asarray(g["L0_ffn_B3"], np.float32)
                         .transpose(0, 2, 1).reshape(ER, FF),
                         geom_mslabs(256, FF, 512)),
        "B2p": image_tab(np.asarray(g["L0_ffn_B2"], np.float32)
                         .transpose(0, 2, 1).reshape(ER, D),
                         geom_mslabs(256, D, 128)),
        "EXPM": expansion_mat(),
        "idn": np.eye(128, dtype=np.float32),
        "flipm": make_flipm(),
    }
    tabs2 = {
        "Ch": image_tab(tabs["Ch"], geom_mslabs(D, NB, 512)),
        "Sm": image_tab(tabs["Sm"], geom_mslabs(D, NB, 512)),

        "IC2": image_tab(tabs["IC2"], geom_mslabs(NB, D, 256)),
        "IS2": image_tab(tabs["IS2"], geom_mslabs(NB, D, 256)),

    }
    com.update(tabs2)
    magic = int(np.random.randint(1, 2**30))
    in_maps = []
    for c in range(NCORES):
        j = c % 4
        m = dict(com)
        m["xT"] = image_tab(
            np.ascontiguousarray(x[c * TLOC:(c + 1) * TLOC].T), xslabs)
        rsrc = c - 2 if j >= 2 else c
        m["xR_T"] = image_tab(
            np.ascontiguousarray(x[rsrc * TLOC:(rsrc + 1) * TLOC].T), xslabs)
        m["cid2"] = np.array([[c % 2, magic]], np.int32)
        w = np.zeros(4, np.float32)
        if j == 1:
            w[0] = 1.0
        elif j == 2:
            w[2] = 1.0
            w[3] = 1.0
        elif j == 3:
            w[0] = 1.0
            w[2] = 1.0
            w[3] = 1.0
        m["exw"] = np.ascontiguousarray(
            np.broadcast_to(w[None, :], (128, 4)).copy())
        in_maps.append(m)
    return in_maps


_prog_cache = {}


def kernel(**inputs):
    in_maps = prep_inputs(inputs)
    if "nc" not in _prog_cache:
        _prog_cache["nc"] = build_program()
    nc = _prog_cache["nc"]
    res = run_bass_kernel_spmd(nc, in_maps, core_ids=list(range(NCORES)))
    shards = []
    for r in res.results:
        img = np.asarray(r["outT"]).reshape(128, NCHF, NT, TCF)
        shards.append(img.transpose(2, 0, 1, 3).reshape(D, TLOC).T)
    out = np.concatenate(shards, axis=0).reshape(B, S, D)
    return np.ascontiguousarray(out.astype(np.float32))


if __name__ == "__main__":
    import time
    t0 = time.time()
    nc = build_program()
    print(f"build {time.time()-t0:.1f}s; "
          f"n_instructions={nc.n_instructions()}")
